# version 12
# baseline (speedup 1.0000x reference)
import contextlib
import numpy as np

import concourse.bass as bass
import concourse.mybir as mybir
from concourse.bass_utils import run_bass_kernel_spmd

F32 = mybir.dt.float32
BF16 = mybir.dt.bfloat16
AF = mybir.ActivationFunctionType
ALU = mybir.AluOpType
AX = mybir.AxisListType

P = 128
C = 1024
NCH = 8
T = 4096
NT = 32
NG = 16
H = 16
HD = 64
FF = 4096
NF = 32
MR = 12582912.0
NEGBIG = -1.0e30
EPS_LN = 1.0e-5
EPS_RMS = 1.0e-6
N_CORES = 8


class Emitter:

    ENGINES = ("sync", "tensor", "vector", "scalar", "gpsimd")

    def __init__(self, nc, stack):
        self.nc = nc
        self.q = {e: [] for e in self.ENGINES}
        self.sems = {}
        self.cnt = {}
        self.stack = stack
        self.dmaq = {}
        self.dmaidx = {}

    def sem(self, name):
        self.sems[name] = self.stack.enter_context(self.nc.semaphore(name))
        self.cnt[name] = 0

    def inc(self, name, n=1):
        self.cnt[name] += n
        return (self.sems[name], n)

    def tok(self, name):
        return self.cnt[name]

    def wait(self, eng, name, value=None):
        v = self.cnt[name] if value is None else value
        assert v <= self.cnt[name], (name, v, self.cnt[name])
        if v <= 0:
            return
        h = self.sems[name]
        self.q[eng].append(lambda e, h=h, v=v: e.wait_ge(h, v))

    DRAIN_ENGINES = ("vector", "scalar", "gpsimd")

    def do(self, eng, build, inc=None, n=1, drain=None):
        if drain is None:
            drain = eng in self.DRAIN_ENGINES
        if drain:
            self.q[eng].append(lambda e: e.drain())
        tgt = self.inc(inc, n) if inc else None

        def fn(e, build=build, tgt=tgt):
            ins = build(e)
            if tgt is not None:
                ins.then_inc(*tgt)

        self.q[eng].append(fn)
        return self.cnt[inc] if inc else None

    def dma_queue(self, qname, nlanes=4):
        lanes = []
        for i in range(nlanes):
            nm = f"{qname}{i}"
            self.sem(nm)
            lanes.append(nm)
        self.dmaq[qname] = lanes
        self.dmaidx[qname] = 0

    def dma(self, eng, qname, build):
        lanes = self.dmaq[qname]
        lane = lanes[self.dmaidx[qname] % len(lanes)]
        self.dmaidx[qname] += 1
        self.wait(eng, lane)
        v = self.do(eng, build, inc=lane, n=16)
        return [(lane, v)]

    def queue_tok(self, qname):
        return [(lane, self.cnt[lane]) for lane in self.dmaq[qname]]

    def wait_tok(self, eng, tok):
        for nm, v in tok:
            self.wait(eng, nm, v)

    def flush_block(self):
        qs = {e: fns for e, fns in self.q.items() if fns}
        with self.nc.Block() as blk:
            for e, fns in qs.items():
                def runner(engine, fns=fns):
                    for f in fns:
                        f(engine)
                getattr(blk, e)(runner)
        self.q = {e: [] for e in self.ENGINES}


def build_nc(phases=4):
    nc = bass.Bass("TRN2", target_bir_lowering=False)

    x_h = nc.dram_tensor("x", [T, C], F32, kind="ExternalInput")
    wqkv_h = nc.dram_tensor("wqkv", [C, 3 * C], F32, kind="ExternalInput")
    lnw_h = nc.dram_tensor("lnw", [1, C], F32, kind="ExternalInput")
    wproj_h = nc.dram_tensor("wproj", [C, C], F32, kind="ExternalInput")
    win_h = nc.dram_tensor("win", [C, FF], F32, kind="ExternalInput")
    bin_h = nc.dram_tensor("bin", [1, FF], F32, kind="ExternalInput")
    wout_h = nc.dram_tensor("wout", [FF, C], F32, kind="ExternalInput")
    bout_h = nc.dram_tensor("bout", [1, C], F32, kind="ExternalInput")
    out_h = nc.dram_tensor("out", [T, C], F32, kind="ExternalOutput")

    xmid_h = nc.dram_tensor("xmid_scr", [T, C], F32)
    hscr_h = nc.dram_tensor("h_scr", [16, P, NF * 256], BF16)
    cc1_in = nc.dram_tensor("cc1_in", [1, 8], F32)
    cc1_out = nc.dram_tensor("cc1_out", [1, 8], F32)
    cc2_in = nc.dram_tensor("cc2_in", [1, 8], F32)
    cc2_out = nc.dram_tensor("cc2_out", [1, 8], F32)

    with contextlib.ExitStack() as gstack:
        E = Emitter(nc, gstack)
        for s in ("smm", "sdv", "sac", "sgp", "scc"):
            E.sem(s)
        E.dma_queue("sx", 4)
        E.dma_queue("sa", 3)
        E.dma_queue("sgd", 4)

        sbg = lambda name, shape, dt: gstack.enter_context(nc.sbuf_tensor(name, shape, dt))

        lnwq_t = sbg("lnwq_t", [P, C], F32)
        lnwk_t = sbg("lnwk_t", [P, C], F32)
        bout_t = sbg("bout_t", [P, C], F32)
        binp = sbg("binp", [P, NF], F32)
        idb = sbg("idb", [P, P], BF16)
        idf = sbg("idf", [P, P], F32)
        m01 = sbg("m01", [P, P], F32)
        m01q1 = sbg("m01q1", [P, 256], F32)
        c127 = sbg("c127", [P, 1], F32)
        c32k = sbg("c32k", [1, 1], F32)
        ceps1 = sbg("ceps1", [P, 1], F32)
        ceps2 = sbg("ceps2", [P, 1], F32)
        cm05 = sbg("cm05", [P, 1], F32)
        nc.const_aps.register(F32, EPS_LN, ceps1[:, :])
        nc.const_aps.register(F32, EPS_RMS, ceps2[:, :])
        nc.const_aps.register(F32, -0.5, cm05[:, :])
        s1_pp = sbg("s1_pp", [P, 1], F32)
        s1i_pp = sbg("s1i_pp", [P, 1], F32)
        s2_pp = sbg("s2_pp", [P, 1], F32)
        s2i_pp = sbg("s2i_pp", [P, 1], F32)
        am_acc = sbg("am_acc", [P, 1], F32)
        am_row = sbg("am_row", [1, P], F32)
        ccsb = sbg("ccsb", [1, 8], F32)
        sc_a = sbg("sc_a", [1, 1], F32)
        sc_b = sbg("sc_b", [1, 1], F32)
        sc_c = sbg("sc_c", [1, 1], F32)
        rln = sbg("rln", [P, NT], F32)
        mrln = sbg("mrln", [P, NT], F32)
        mvst = sbg("mvst", [P, 2 * NT], F32)
        gmax = sbg("gmax", [P, NT], F32)

        with contextlib.ExitStack() as pstack:
            psb = lambda n, s, d: pstack.enter_context(nc.sbuf_tensor(n, s, d))
            onesf = psb("onesf", [P, P], F32)
            zerof = psb("zerof", [P, P], F32)
            bstage = psb("bstage", [NF, P], F32)
            bps = pstack.enter_context(nc.psum_tensor("bps", [P, NF], F32))

            tok_ld = []
            tok_ld += E.dma("sync", "sx", lambda s: s.dma_start(
                lnwq_t[:, :], bass.AP(lnw_h, 0, [[0, P], [1, C]])))
            tok_ld += E.dma("sync", "sx", lambda s: s.dma_start(
                lnwk_t[:, :], bass.AP(lnw_h, 0, [[0, P], [1, C]])))
            tok_ld += E.dma("sync", "sx", lambda s: s.dma_start(
                bout_t[:, :], bass.AP(bout_h, 0, [[0, P], [1, C]])))
            tok_ld += E.dma("sync", "sx", lambda s: s.dma_start(
                bstage[:, :], bass.AP(bin_h, 0, [[P, NF], [1, P]])))

            E.do("gpsimd", lambda g: g.memset(onesf[:, :], 1.0), inc="sgp")
            ms_tok = E.do("gpsimd", lambda g: g.memset(zerof[:, :], 0.0), inc="sgp")
            E.wait("gpsimd", "sgp", ms_tok)
            E.do("gpsimd", lambda g: g.affine_select(
                idf[:, :], onesf[:, :], [[1, P]], ALU.is_equal, 0.0,
                channel_multiplier=-1), inc="sgp")
            E.do("gpsimd", lambda g: g.affine_select(
                m01[:, :], zerof[:, :], [[-1, P]], ALU.is_ge, NEGBIG,
                channel_multiplier=1), inc="sgp")
            E.do("gpsimd", lambda g: g.memset(m01q1[:, 0:P], 0.0), inc="sgp")
            E.do("gpsimd", lambda g: g.affine_select(
                m01q1[:, P:256], zerof[:, :], [[-1, P]], ALU.is_ge, NEGBIG,
                channel_multiplier=1), inc="sgp")
            E.do("gpsimd", lambda g: g.memset(c127[:, :], 127.0), inc="sgp")
            E.do("gpsimd", lambda g: g.memset(ceps1[:, :], EPS_LN), inc="sgp")
            E.do("gpsimd", lambda g: g.memset(ceps2[:, :], EPS_RMS), inc="sgp")
            E.do("gpsimd", lambda g: g.memset(cm05[:, :], -0.5), inc="sgp")
            E.do("gpsimd", lambda g: g.memset(c32k[:, :], 32767.0), inc="sgp")
            tok_gp = E.tok("sgp")

            E.wait("vector", "sgp", tok_gp)
            E.wait_tok("vector", tok_ld)
            E.do("vector", lambda v: v.tensor_copy(idb[:, :], idf[:, :]))
            E.do("vector", lambda v: v.tensor_scalar_mul(
                lnwq_t[:, :], lnwq_t[:, :], 0.125), inc="sdv")

            E.wait("tensor", "sgp", tok_gp)
            E.wait_tok("tensor", tok_ld)
            E.do("tensor", lambda t: t.transpose(
                bps[:, :], bstage[:, :], idf[0:NF, 0:NF]), inc="smm")
            E.wait("vector", "smm")
            E.do("vector", lambda v: v.tensor_copy(binp[:, :], bps[:, :]), inc="sdv")
            E.flush_block()

        with contextlib.ExitStack() as s1k:
            psb = lambda n, s, d: s1k.enter_context(nc.sbuf_tensor(n, s, d))
            pps = lambda n, s, d: s1k.enter_context(nc.psum_tensor(n, s, d))

            wqkv_sb = psb("wqkv_sb", [P, NCH, 3 * C], BF16)
            wproj_sb = psb("wproj_sb", [P, NCH, C], BF16)
            wstage = [psb(f"wstage{i}", [P, 3 * C], F32) for i in range(2)]

            cast_tok = [0, 0]
            for cc in range(NCH):
                b = cc % 2
                E.wait("sync", "sdv", cast_tok[b])
                wt = E.dma("sync", "sx", lambda s, cc=cc, b=b: s.dma_start(
                    wstage[b][:, :], wqkv_h[cc * P:(cc + 1) * P, :]))
                E.wait_tok("vector", wt)
                cast_tok[b] = E.do("vector", lambda v, cc=cc, b=b: v.tensor_copy(
                    wqkv_sb[:, cc, :], wstage[b][:, :]), inc="sdv")
            for cc in range(NCH):
                b = cc % 2
                E.wait("sync", "sdv", cast_tok[b])
                wt = E.dma("sync", "sx", lambda s, cc=cc, b=b: s.dma_start(
                    wstage[b][:, 0:C], wproj_h[cc * P:(cc + 1) * P, :]))
                E.wait_tok("vector", wt)
                cast_tok[b] = E.do("vector", lambda v, cc=cc, b=b: v.tensor_copy(
                    wproj_sb[:, cc, :], wstage[b][:, 0:C]), inc="sdv")
            E.flush_block()

            xbuf = [[psb(f"xb{pb}_{t}", [P, C], F32) for t in range(2)] for pb in range(2)]
            x1c = [psb(f"x1c{t}", [P, C], BF16) for t in range(2)]
            qnb = [psb(f"qnb{t}", [P, C], BF16) for t in range(2)]
            knb = [psb(f"knb{t}", [P, C], BF16) for t in range(2)]
            x1T = psb("x1T", [P, 2, NCH, P], BF16)
            v_sb = psb("v_sb", [P, 2, C], BF16)
            qT = psb("qT", [P, NCH, 256], BF16)
            kT = psb("kT", [P, NCH, 256], BF16)
            pP = [psb(f"pP{i}", [P, 2, 384], F32) for i in range(2)]
            pmP = [psb(f"pmP{i}", [P, 2, 384], F32) for i in range(2)]
            pn = [psb(f"pn{i}", [P, 2, 384], BF16) for i in range(2)]
            pTb = [psb(f"pTb{i}", [P, 768], BF16) for i in range(2)]
            y_sb = psb("y_sb", [P, NCH, 256], BF16)
            xm_sb = [[psb(f"xm{pb}_{t}", [P, C], F32) for t in range(2)] for pb in range(2)]
            xp_t = [psb(f"xp{t}", [P, C], F32) for t in range(2)]
            st6 = psb("st6", [P, 12], F32)
            mv2 = psb("mv2", [P, 2], F32)
            t128a = psb("t128a", [P, 1], F32)
            t128b = psb("t128b", [P, 1], F32)
            tmx = psb("tmx", [P, 1], F32)
            tmn = psb("tmn", [P, 1], F32)
            rqk = [psb(f"rqk{i}", [P, 1], F32) for i in range(2)]
            nmq = [psb(f"nmq{i}", [P, 4], F32) for i in range(2)]
            rsq = [psb(f"rsq{i}", [P, 4], F32) for i in range(2)]

            ps_big = pps("ps_big", [P, 3072], F32)
            ps_tp = [pps(f"ps_tp{i}", [P, 768], BF16) for i in range(2)]

            def BK(i, off=0, n=512):
                return ps_big[:, i * 512 + off:i * 512 + off + n]

            x_need = [[], []]
            x_need[0] = (E.dma("sync", "sx", lambda s: s.dma_start(
                             xbuf[0][0][:, :], x_h[0:P, :]))
                         + E.dma("sync", "sx", lambda s: s.dma_start(
                             xbuf[0][1][:, :], x_h[P:2 * P, :])))

            ev_bank = [("sdv", 0)] * 6
            evtp = [("sdv", 0), ("sac", 0)]
            nm_rd = [0, 0]
            pp_rd = [0, 0]
            y_rd = [0, 0]
            buf_free = {}
            xm_guard = [[[], []], [[], []]]
            y_proj_rd = 0
            v_y_rd = 0
            t128_rd = 0
            rqk_rd = [0, 0]

            def tbatch(srcs, psum_bank, dst_ap_fn, eng, sem):
                pst = ps_tp[psum_bank]
                n = len(srcs)
                for i in range(n - 1):
                    E.do("tensor", lambda tt, i=i: tt.transpose(
                        pst[:, i * P:(i + 1) * P], srcs[i](), idb[:, :]))
                tp = E.do("tensor", lambda tt, i=n - 1: tt.transpose(
                    pst[:, i * P:(i + 1) * P], srcs[i](), idb[:, :]), inc="smm")
                E.wait(eng, "smm", tp)
                if eng == "scalar":
                    ev = E.do(eng, lambda e: e.copy(
                        dst_ap_fn(), pst[:, 0:n * P]), inc=sem)
                else:
                    ev = E.do(eng, lambda e: e.tensor_copy(
                        dst_ap_fn(), pst[:, 0:n * P]), inc=sem)
                return (sem, ev)

            ln1_tok = [0, 0]

            def emit_ln1(gg, t):
                ppb = gg % 2
                xt = xbuf[ppb][t]
                E.wait_tok("vector", x_need[ppb])
                E.do("vector", lambda v, xt=xt: v.bn_stats(st6[:, 0:6], xt[:, 0:512]))
                E.do("vector", lambda v, xt=xt: v.bn_stats(st6[:, 6:12],
                                                           xt[:, 512:1024]))
                E.do("vector", lambda v: v.bn_aggr(mv2[:, :], st6[:, :]), inc="sdv")
                E.wait("scalar", "sdv")
                E.wait("scalar", "sdv", t128_rd)
                E.do("scalar", lambda s: s.activation(
                    t128a[:, :], mv2[:, 1:2], AF.Ln, bias=EPS_LN))
                E.do("scalar", lambda s: s.activation(
                    t128a[:, :], t128a[:, :], AF.Exp, scale=-0.5), inc="sac")
                E.wait("vector", "sac")
                E.do("vector", lambda v: v.tensor_mul(
                    t128b[:, :], mv2[:, 0:1], t128a[:, :]))
                for sem, tk in buf_free.get(("x1c", t), []):
                    E.wait("vector", sem, tk)
                tok = E.do("vector", lambda v, xt=xt, t=t: v.tensor_scalar(
                    x1c[t][:, :], xt[:, :], t128a[:, :], t128b[:, :],
                    ALU.mult, ALU.subtract), inc="sdv")
                ln1_tok[t] = tok
                return tok

            t128_rd = emit_ln1(0, 0)
            t128_rd = emit_ln1(0, 1)

            for g in range(NG):
                pb = g % 2
                if g + 1 < NG:
                    npb = (g + 1) % 2
                    E.wait("sync", "sdv")
                    tks = []
                    for t in range(2):
                        tks += E.dma("sync", "sx", lambda s, g=g, t=t, npb=npb: s.dma_start(
                            xbuf[npb][t][:, :],
                            x_h[(g + 1) * 256 + t * P:(g + 1) * 256 + (t + 1) * P, :]))
                    x_need[npb] = tks

                qn_tok = [0, 0]
                kn_tok = [0, 0]
                for t in range(2):
                    xt = xbuf[pb][t]
                    x1_tok = ln1_tok[t]

                    evs = []
                    for half in range(2):
                        bank = half
                        E.wait("tensor", "sdv", x1_tok)
                        E.wait("tensor", evtp[bank][0], evtp[bank][1])
                        eng = "vector" if half == 0 else "scalar"
                        sem = "sdv" if half == 0 else "sac"
                        evtp[bank] = tbatch(
                            [(lambda t=t, cc=half * 4 + i: x1c[t][:, cc * P:(cc + 1) * P])
                             for i in range(4)],
                            bank,
                            lambda t=t, half=half: x1T[:, t, half * 4:(half + 1) * 4, :],
                            eng, sem)
                        evs.append(evtp[bank])
                    buf_free[("x1c", t)] = list(evs)

                    x1T_ready = list(evs)
                    qkv_tok = [0] * 6
                    for pc in range(6):
                        E.wait("tensor", ev_bank[pc][0], ev_bank[pc][1])
                        for sem, tk in x1T_ready:
                            E.wait("tensor", sem, tk)
                        x1T_ready = []
                        for cc in range(NCH):
                            kw = dict(start=(cc == 0), stop=(cc == NCH - 1))
                            lam = (lambda tt, t=t, cc=cc, pc=pc, kw=kw:
                                   tt.matmul(BK(pc), x1T[:, t, cc, :],
                                             wqkv_sb[:, cc, pc * 512:(pc + 1) * 512],
                                             **kw))
                            if cc == NCH - 1:
                                qkv_tok[pc] = E.do("tensor", lam, inc="smm")
                            else:
                                E.do("tensor", lam)

                    E.wait("vector", "smm", qkv_tok[1])
                    E.do("vector", lambda v: v.bn_stats(st6[:, 0:6], BK(0)))
                    E.do("vector", lambda v: v.bn_stats(st6[:, 6:12], BK(1)))
                    E.do("vector", lambda v: v.bn_aggr(mv2[:, :], st6[:, :]))
                    E.do("vector", lambda v: v.tensor_mul(t128b[:, :], mv2[:, 0:1],
                                                          mv2[:, 0:1]))
                    E.do("vector", lambda v: v.tensor_add(tmx[:, :], mv2[:, 1:2],
                                                          t128b[:, :]), inc="sdv")
                    E.wait("scalar", "sdv")
                    E.wait("scalar", "sdv", rqk_rd[0])
                    E.do("scalar", lambda s: s.activation(
                        rqk[0][:, :], tmx[:, :], AF.Ln, bias=EPS_RMS))
                    E.do("scalar", lambda s: s.activation(
                        rqk[0][:, :], rqk[0][:, :], AF.Exp, scale=-0.5), inc="sac")
                    E.wait("vector", "sac")
                    for sem, tk in buf_free.get(("qnb", t), []):
                        E.wait("vector", sem, tk)
                    qn_tok[t] = E.do("vector", lambda v, t=t: v.scalar_tensor_tensor(
                        qnb[t][:, :], ps_big[:, 0:1024], rqk[0][:, :], lnwq_t[:, :],
                        ALU.mult, ALU.mult), inc="sdv")
                    rqk_rd[0] = qn_tok[t]
                    ev_bank[0] = ev_bank[1] = ("sdv", qn_tok[t])

                    E.wait("vector", "smm", qkv_tok[3])
                    E.do("vector", lambda v: v.bn_stats(st6[:, 0:6], BK(2)))
                    E.do("vector", lambda v: v.bn_stats(st6[:, 6:12], BK(3)))
                    E.do("vector", lambda v: v.bn_aggr(mv2[:, :], st6[:, :]))
                    E.do("vector", lambda v: v.tensor_mul(t128b[:, :], mv2[:, 0:1],
                                                          mv2[:, 0:1]))
                    E.do("vector", lambda v: v.tensor_add(tmx[:, :], mv2[:, 1:2],
                                                          t128b[:, :]), inc="sdv")
                    E.wait("scalar", "sdv")
                    E.wait("scalar", "sdv", rqk_rd[1])
                    E.do("scalar", lambda s: s.activation(
                        rqk[1][:, :], tmx[:, :], AF.Ln, bias=EPS_RMS))
                    E.do("scalar", lambda s: s.activation(
                        rqk[1][:, :], rqk[1][:, :], AF.Exp, scale=-0.5), inc="sac")
                    E.wait("vector", "sac")
                    for sem, tk in buf_free.get(("knb", t), []):
                        E.wait("vector", sem, tk)
                    kn_tok[t] = E.do("vector", lambda v, t=t: v.scalar_tensor_tensor(
                        knb[t][:, :], ps_big[:, 1024:2048], rqk[1][:, :], lnwk_t[:, :],
                        ALU.mult, ALU.mult), inc="sdv")
                    rqk_rd[1] = kn_tok[t]
                    ev_bank[2] = ev_bank[3] = ("sdv", kn_tok[t])

                    E.wait("scalar", "smm", qkv_tok[5])
                    E.wait("scalar", "smm", v_y_rd)
                    E.do("scalar", lambda s, t=t: s.copy(v_sb[:, t, 0:512], BK(4)))
                    vev = E.do("scalar", lambda s, t=t: s.copy(v_sb[:, t, 512:1024],
                                                               BK(5)), inc="sac")
                    ev_bank[4] = ev_bank[5] = ("sac", vev)

                qkT_toks = []
                for t in range(2):
                    for src_b, dstT, tok, bname, eng, q in (
                            (qnb, qT, qn_tok, "qnb", "sync", "sx"),
                            (knb, kT, kn_tok, "knb", "scalar", "sa")):
                        E.wait(eng, "sdv", tok[t])
                        tks = []
                        for cc in range(NCH):
                            tks += E.dma(eng, q, lambda s, src_b=src_b, dstT=dstT,
                                         t=t, cc=cc: s.dma_start(
                                             dstT[:, cc, t * P:(t + 1) * P],
                                             src_b[t][:, cc * P:(cc + 1) * P],
                                             transpose=True))
                        buf_free[(bname, t)] = tks
                        qkT_toks += tks

                qkT_done = qkT_toks

                def emit_scores(hp):
                    for sem, tk in qkT_done:
                        E.wait("tensor", sem, tk)
                    qkT_done.clear()
                    E.wait("tensor", ev_bank[0][0], ev_bank[0][1])
                    E.wait("tensor", ev_bank[1][0], ev_bank[1][1])
                    E.do("tensor", lambda tt, hp=hp: tt.matmul(
                        BK(0, 0, P), qT[0:HD, hp, 0:P], kT[0:HD, hp, 0:P],
                        start=True, stop=True, tile_position=(0, 0)))
                    E.do("tensor", lambda tt, hp=hp: tt.matmul(
                        BK(1, 0, P), qT[HD:2 * HD, hp, 0:P], kT[HD:2 * HD, hp, 0:P],
                        start=True, stop=True, tile_position=(HD, 0)))
                    E.do("tensor", lambda tt, hp=hp: tt.matmul(
                        BK(0, P, 256), qT[0:HD, hp, P:256], kT[0:HD, hp, 0:256],
                        start=True, stop=True, tile_position=(0, 0)))
                    return E.do("tensor", lambda tt, hp=hp: tt.matmul(
                        BK(1, P, 256), qT[HD:2 * HD, hp, P:256],
                        kT[HD:2 * HD, hp, 0:256],
                        start=True, stop=True, tile_position=(HD, 0)), inc="smm")

                s_tok = emit_scores(0)
                pend = [None]

                def emit_tail(ctx):
                    hp, par, pE, pnp, rs, exp_tok = ctx
                    E.wait("vector", "sac", exp_tok)
                    E.do("vector", lambda v, rs=rs: v.reciprocal(rs[:, :], rs[:, :]),
                         inc="sdv", drain=False)
                    E.wait("scalar", "sdv")
                    E.do("vector", lambda v, pE=pE, pnp=pnp, rs=rs: v.tensor_scalar_mul(
                        pnp[:, 0, 0:P], pE[:, 0, 0:P], rs[:, 0:1]))
                    nv = E.do("vector", lambda v, pE=pE, pnp=pnp, rs=rs:
                              v.tensor_scalar_mul(
                                  pnp[:, 0, P:384], pE[:, 0, P:384], rs[:, 2:3]),
                              inc="sdv", drain=False)
                    E.do("scalar", lambda s, pE=pE, pnp=pnp, rs=rs: s.activation(
                        pnp[:, 1, 0:P], pE[:, 1, 0:P], AF.Copy, scale=rs[:, 1:2]),
                        drain=False)
                    na = E.do("scalar", lambda s, pE=pE, pnp=pnp, rs=rs: s.activation(
                        pnp[:, 1, P:384], pE[:, 1, P:384], AF.Copy, scale=rs[:, 3:4]),
                        inc="sac", drain=False)
                    pp_rd[par] = nv

                    E.wait("tensor", "sdv", nv)
                    E.wait("tensor", "sac", na)
                    E.wait("tensor", evtp[par][0], evtp[par][1])
                    pst = ps_tp[par]
                    for i, srcf in enumerate((
                            lambda pnp=pnp: pnp[:, 0, 0:P],
                            lambda pnp=pnp: pnp[:, 1, 0:P],
                            lambda pnp=pnp: pnp[:, 0, P:256],
                            lambda pnp=pnp: pnp[:, 0, 256:384],
                            lambda pnp=pnp: pnp[:, 1, P:256])):
                        E.do("tensor", lambda tt, i=i, srcf=srcf, pst=pst: tt.transpose(
                            pst[:, i * P:(i + 1) * P], srcf(), idb[:, :]))
                    tp = E.do("tensor", lambda tt, pst=pst, pnp=pnp: tt.transpose(
                        pst[:, 5 * P:6 * P], pnp[:, 1, 256:384], idb[:, :]), inc="smm")
                    eng = "vector" if par == 0 else "scalar"
                    sem = "sdv" if par == 0 else "sac"
                    E.wait(eng, "smm", tp)
                    E.wait(eng, "smm", y_rd[par])
                    if eng == "scalar":
                        ev = E.do(eng, lambda e, par=par, pst=pst: e.copy(
                            pTb[par][:, :], pst[:, :]), inc=sem)
                    else:
                        ev = E.do(eng, lambda e, par=par, pst=pst: e.tensor_copy(
                            pTb[par][:, :], pst[:, :]), inc=sem)
                    evtp[par] = (sem, ev)

                    yb = 2 + par
                    E.wait("tensor", evtp[par][0], evtp[par][1])
                    E.wait("tensor", ev_bank[yb][0], ev_bank[yb][1])
                    for j in range(2):
                        h = 2 * hp + j
                        E.do("tensor", lambda tt, h=h, j=j, par=par, yb=yb: tt.matmul(
                            ps_big[j * HD:(j + 1) * HD, yb * 512:yb * 512 + P],
                            v_sb[:, 0, h * HD:(h + 1) * HD],
                            pTb[par][:, j * P:(j + 1) * P],
                            start=True, stop=True, tile_position=(0, j * HD)))
                    y_tok = None
                    for j in range(2):
                        h = 2 * hp + j
                        for kt in range(2):
                            kw = dict(start=(kt == 0), stop=(kt == 1),
                                      tile_position=(0, j * HD))
                            lam = (lambda tt, h=h, j=j, kt=kt, kw=kw, par=par, yb=yb:
                                   tt.matmul(
                                       ps_big[j * HD:(j + 1) * HD,
                                              yb * 512 + P:yb * 512 + 256],
                                       v_sb[:, kt, h * HD:(h + 1) * HD],
                                       pTb[par][:, 256 + (2 * j + kt) * P:
                                                256 + (2 * j + kt + 1) * P], **kw))
                            if j == 1 and kt == 1:
                                y_tok = E.do("tensor", lam, inc="smm")
                            else:
                                E.do("tensor", lam)
                    nonlocal_state["v_y_rd"] = y_tok
                    y_rd[par] = y_tok
                    if par == 1:
                        E.wait("scalar", "smm", y_tok)
                        E.wait("scalar", "smm", y_proj_rd)
                        yev = E.do("scalar", lambda s, hp=hp: s.copy(
                            y_sb[:, hp - 1:hp + 1, :],
                            bass.AP(ps_big, 1024, [[3072, P], [512, 2], [1, 256]])),
                            inc="sac")
                        ev_bank[2] = ev_bank[3] = ("sac", yev)

                nonlocal_state = {"v_y_rd": v_y_rd}
                for hp in range(NCH):
                    par = hp % 2
                    sP = pmP[par]
                    pE = pP[par]
                    pnp = pn[par]
                    nm = nmq[par]
                    rs = rsq[par]
                    E.wait("vector", "smm", s_tok)
                    E.wait("vector", "sac", nm_rd[par])
                    E.wait("vector", "sdv", pp_rd[par])
                    E.do("vector", lambda v, sP=sP: v.tensor_tensor(
                        bass.AP(sP, 0, [[768, P], [384, 2], [1, P]]),
                        bass.AP(ps_big, 0, [[3072, P], [512, 2], [1, P]]),
                        bass.AP(m01, 0, [[P, P], [0, 2], [1, P]]), ALU.add))
                    mk_tok = E.do("vector", lambda v, sP=sP: v.tensor_tensor(
                        bass.AP(sP, P, [[768, P], [384, 2], [1, 256]]),
                        bass.AP(ps_big, P, [[3072, P], [512, 2], [1, 256]]),
                        bass.AP(m01q1, 0, [[256, P], [0, 2], [1, 256]]), ALU.add),
                        inc="sdv", drain=False)
                    ev_bank[0] = ev_bank[1] = ("sdv", mk_tok)

                    if hp + 1 < NCH:
                        s_tok = emit_scores(hp + 1)

                    E.do("vector", lambda v, sP=sP, nm=nm: v.tensor_reduce(
                        nm[:, 0:2], bass.AP(sP, 0, [[768, P], [384, 2], [1, P]]),
                        AX.X, ALU.max, negate=True))
                    nm_tok = E.do("vector", lambda v, sP=sP, nm=nm: v.tensor_reduce(
                        nm[:, 2:4], bass.AP(sP, P, [[768, P], [384, 2], [1, 256]]),
                        AX.X, ALU.max, negate=True), inc="sdv", drain=False)

                    E.wait("scalar", "sdv", nm_tok)
                    E.do("scalar", lambda s, sP=sP, pE=pE, nm=nm, rs=rs: s.activation(
                        pE[:, 0, 0:P], sP[:, 0, 0:P], AF.Exp, bias=nm[:, 0:1],
                        accum_out=rs[:, 0:1]), drain=False)
                    E.do("scalar", lambda s, sP=sP, pE=pE, nm=nm, rs=rs: s.activation(
                        pE[:, 1, 0:P], sP[:, 1, 0:P], AF.Exp, bias=nm[:, 1:2],
                        accum_out=rs[:, 1:2]), drain=False)
                    E.do("scalar", lambda s, sP=sP, pE=pE, nm=nm, rs=rs: s.activation(
                        pE[:, 0, P:384], sP[:, 0, P:384], AF.Exp, bias=nm[:, 2:3],
                        accum_out=rs[:, 2:3]), drain=False)
                    exp_tok = E.do("scalar", lambda s, sP=sP, pE=pE, nm=nm, rs=rs:
                                   s.activation(
                                       pE[:, 1, P:384], sP[:, 1, P:384], AF.Exp,
                                       bias=nm[:, 3:4], accum_out=rs[:, 3:4]),
                                   inc="sac", drain=False)
                    nm_rd[par] = exp_tok

                    if pend[0] is not None:
                        emit_tail(pend[0])
                    pend[0] = (hp, par, pE, pnp, rs, exp_tok)
                emit_tail(pend[0])
                pend[0] = None
                v_y_rd = nonlocal_state["v_y_rd"]

                if g + 1 < NG:
                    t128_rd = emit_ln1(g + 1, 0)
                    t128_rd = emit_ln1(g + 1, 1)

                y_ready = ("sac", E.tok("sac"))
                proj_tok = [0, 0]
                proj_banks = [(4, 5), (2, 3)]
                for t in range(2):
                    ba, bb = proj_banks[t]
                    E.wait("tensor", y_ready[0], y_ready[1])
                    E.wait("tensor", ev_bank[ba][0], ev_bank[ba][1])
                    E.wait("tensor", ev_bank[bb][0], ev_bank[bb][1])
                    for hp in range(NCH):
                        last = hp == NCH - 1
                        kw0 = dict(start=(hp == 0), stop=last)
                        E.do("tensor", lambda tt, hp=hp, t=t, ba=ba, kw=kw0: tt.matmul(
                            BK(ba), y_sb[:, hp, t * P:(t + 1) * P],
                            wproj_sb[:, hp, 0:512], **kw))
                        lam = (lambda tt, hp=hp, t=t, bb=bb, kw=kw0: tt.matmul(
                            BK(bb), y_sb[:, hp, t * P:(t + 1) * P],
                            wproj_sb[:, hp, 512:1024], **kw))
                        if last:
                            proj_tok[t] = E.do("tensor", lam, inc="smm")
                        else:
                            E.do("tensor", lam)
                y_proj_rd = proj_tok[1]

                for t in range(2):
                    ba, bb = proj_banks[t]
                    tl = g * 2 + t
                    E.wait("vector", "smm", proj_tok[t])
                    E.do("vector", lambda v, ba=ba: v.bn_stats(st6[:, 0:6], BK(ba)))
                    E.do("vector", lambda v, bb=bb: v.bn_stats(st6[:, 6:12], BK(bb)))
                    E.do("vector", lambda v: v.bn_aggr(mv2[:, :], st6[:, :]), inc="sdv")
                    E.wait("scalar", "sdv")
                    E.wait("scalar", "sdv", t128_rd)
                    E.do("scalar", lambda s: s.activation(
                        t128a[:, :], mv2[:, 1:2], AF.Ln, bias=EPS_LN))
                    E.do("scalar", lambda s: s.activation(
                        t128a[:, :], t128a[:, :], AF.Exp, scale=-0.5), inc="sac")
                    E.wait("vector", "sac")
                    E.do("vector", lambda v: v.tensor_mul(
                        t128b[:, :], mv2[:, 0:1], t128a[:, :]))
                    E.do("vector", lambda v, pb=pb, t=t: v.tensor_scalar(
                        xp_t[t][:, :], xbuf[pb][t][:, :], 1.0, t128b[:, :],
                        ALU.mult, ALU.subtract))
                    for sem, tk in xm_guard[pb][t]:
                        E.wait("vector", sem, tk)
                    E.do("vector", lambda v, pb=pb, t=t, ba=ba: v.scalar_tensor_tensor(
                        xm_sb[pb][t][:, 0:512], BK(ba), t128a[:, :],
                        xp_t[t][:, 0:512], ALU.mult, ALU.add), inc="sdv")
                    ev_bank[ba] = ("sdv", E.tok("sdv"))
                    stt2 = E.do("vector", lambda v, pb=pb, t=t, bb=bb:
                                v.scalar_tensor_tensor(
                                    xm_sb[pb][t][:, 512:1024], BK(bb), t128a[:, :],
                                    xp_t[t][:, 512:1024], ALU.mult, ALU.add), inc="sdv")
                    ev_bank[bb] = ("sdv", stt2)
                    t128_rd = stt2
                    E.wait("gpsimd", "sdv", stt2)
                    dtok = E.dma("gpsimd", "sgd",
                                 lambda gp, g=g, t=t, pb=pb: gp.dma_start(
                                     xmid_h[g * 256 + t * P:g * 256 + (t + 1) * P, :],
                                     xm_sb[pb][t][:, :]))
                    E.do("vector", lambda v, pb=pb, t=t: v.bn_stats(
                        st6[:, 0:6], xm_sb[pb][t][:, 0:512]))
                    E.do("vector", lambda v, pb=pb, t=t: v.bn_stats(
                        st6[:, 6:12], xm_sb[pb][t][:, 512:1024]))
                    E.do("vector", lambda v, tl=tl: v.bn_aggr(
                        mvst[:, 2 * tl:2 * tl + 2], st6[:, :]))
                    E.do("vector", lambda v, pb=pb, t=t: v.tensor_reduce(
                        tmx[:, :], xm_sb[pb][t][:, :], AX.X, ALU.max))
                    E.do("vector", lambda v, pb=pb, t=t: v.tensor_reduce(
                        tmn[:, :], xm_sb[pb][t][:, :], AX.X, ALU.min))
                    E.do("vector", lambda v, tl=tl: v.tensor_sub(
                        tmx[:, :], tmx[:, :], mvst[:, 2 * tl:2 * tl + 1]))
                    E.do("vector", lambda v, tl=tl: v.tensor_sub(
                        tmn[:, :], mvst[:, 2 * tl:2 * tl + 1], tmn[:, :]))
                    sttok = E.do("vector", lambda v, tl=tl: v.tensor_tensor(
                        gmax[:, tl:tl + 1], tmx[:, :], tmn[:, :], ALU.max), inc="sdv")
                    xm_guard[pb][t] = dtok + [("sdv", sttok)]

            E.wait("scalar", "sdv")
            E.do("scalar", lambda s: s.activation(
                rln[:, :], bass.AP(mvst, 1, [[2 * NT, P], [2, NT]]), AF.Ln,
                bias=EPS_LN))
            rln_tok = E.do("scalar", lambda s: s.activation(
                rln[:, :], rln[:, :], AF.Exp, scale=-0.5), inc="sac")
            E.wait("vector", "sac", rln_tok)
            E.do("vector", lambda v: v.tensor_tensor(
                mrln[:, :], bass.AP(mvst, 0, [[2 * NT, P], [2, NT]]), rln[:, :],
                ALU.mult))
            E.do("vector", lambda v: v.tensor_mul(gmax[:, :], gmax[:, :], rln[:, :]))
            E.do("vector", lambda v: v.tensor_reduce(
                am_acc[:, :], gmax[:, :], AX.X, ALU.max), inc="sdv")

            E.wait_tok("gpsimd", E.queue_tok("sgd"))
            E.wait_tok("sync", E.queue_tok("sx"))
            E.flush_block()

        if phases < 2:
            return nc
        with contextlib.ExitStack() as s2k:
            psb = lambda n, s, d: s2k.enter_context(nc.sbuf_tensor(n, s, d))
            xmb = [psb(f"xmb{i}", [P, C], F32) for i in range(2)]
            ta = psb("ta", [P, 1], F32)

            def emit_collective(cin, cout, s_pp, si_pp):
                t1 = E.dma("gpsimd", "sgd", lambda g: g.dma_start(
                    am_row[0:1, :], am_acc[:, 0:1]))
                E.wait_tok("vector", t1)
                E.do("vector", lambda v: v.tensor_reduce(
                    ccsb[:, :], bass.AP(am_row, 0, [[P, 1], [16, 8], [1, 16]]),
                    AX.X, ALU.max), inc="sdv")
                E.wait("gpsimd", "sdv")
                t2 = E.dma("gpsimd", "sgd", lambda g: g.dma_start(cin[:, :], ccsb[:, :]))
                E.wait_tok("gpsimd", t2)
                cc_tok = E.do("gpsimd", lambda g: g.collective_compute(
                    "AllReduce", ALU.max, replica_groups=[list(range(N_CORES))],
                    ins=[cin.ap()], outs=[cout.ap()]), inc="scc")
                E.wait("gpsimd", "scc", cc_tok)
                t3 = E.dma("gpsimd", "sgd", lambda g: g.dma_start(ccsb[:, :], cout[:, :]))
                E.wait_tok("vector", t3)
                E.do("vector", lambda v: v.tensor_reduce(
                    sc_a[:, :], ccsb[:, :], AX.X, ALU.max))
                E.do("vector", lambda v: v.tensor_scalar_max(sc_a[:, :], sc_a[:, :], 1.0e-8))
                E.do("vector", lambda v: v.reciprocal(sc_b[:, :], sc_a[:, :]))
                E.do("vector", lambda v: v.tensor_scalar_mul(
                    sc_b[:, :], sc_b[:, :], 32767.0))
                E.do("vector", lambda v: v.tensor_scalar_mul(
                    sc_c[:, :], sc_a[:, :], 1.0 / 32767.0), inc="sdv")
                bs1 = E.do("vector", lambda v: v.tensor_scalar(
                    am_row[0:1, :], am_row[0:1, :], 0.0, sc_b[0:1, 0:1],
                    ALU.mult, ALU.add), inc="sdv")
                E.wait("gpsimd", "sdv", bs1)
                t4 = E.dma("gpsimd", "sgd", lambda g: g.dma_start(
                    s_pp[:, 0:1], am_row[0:1, 0:P]))
                E.wait_tok("vector", t4)
                bs2 = E.do("vector", lambda v: v.tensor_scalar(
                    am_row[0:1, :], am_row[0:1, :], 0.0, sc_c[0:1, 0:1],
                    ALU.mult, ALU.add), inc="sdv")
                E.wait("gpsimd", "sdv", bs2)
                t5 = E.dma("gpsimd", "sgd", lambda g: g.dma_start(
                    si_pp[:, 0:1], am_row[0:1, 0:P]))
                E.wait_tok("gpsimd", t5)
                E.do("gpsimd", lambda g: g.memset(am_row[0:1, 0:1], 0.0), inc="sgp")
                E.flush_block()

            emit_collective(cc1_in, cc1_out, s1_pp, s1i_pp)
            if phases < 3:
                return nc

            x2qT = psb("x2qT", [P, NCH, T], BF16)
            u2 = [psb(f"u2_{i}", [P, C], BF16) for i in range(2)]
            u_sb = psb("u_sb", [P, C], BF16)
            tq = [psb(f"tq{i}", [P, C], F32) for i in range(2)]
            rs_ = psb("rs_", [P, NT], F32)
            mrs = psb("mrs", [P, NT], F32)
            ps_h = [s2k.enter_context(nc.psum_tensor(f"ps_h{i}", [P, 512], F32))
                    for i in range(2)]
            x2stack = contextlib.ExitStack()
            ps_t2 = [x2stack.enter_context(nc.psum_tensor(f"ps_t2{i}", [P, 512], BF16))
                     for i in range(2)]

            xm_tok = [None, None]
            xm_tok[0] = E.dma("sync", "sx", lambda s: s.dma_start(
                xmb[0][:, :], xmid_h[0:P, :]))
            xm_tok[1] = E.dma("sync", "sx", lambda s: s.dma_start(
                xmb[1][:, :], xmid_h[P:2 * P, :]))
            ev_t2 = [("sdv", 0), ("sac", 0)]
            u_free = [[], []]
            E.wait("vector", "sgp")
            E.do("vector", lambda v: v.tensor_scalar_mul(
                rs_[:, :], rln[:, :], s1_pp[:, 0:1]))
            E.do("vector", lambda v: v.tensor_scalar(
                mrs[:, :], mrln[:, :], s1_pp[:, 0:1], -1.0, ALU.mult, ALU.mult),
                inc="sdv")
            tq_free = [0, 0]
            for tl in range(NT):
                b = tl % 2
                E.wait_tok("scalar", xm_tok[b])
                E.wait("scalar", "sdv")
                E.wait("scalar", "sdv", tq_free[b])
                tq_tok = E.do("scalar", lambda s, b=b, tl=tl: s.activation(
                    tq[b][:, :], xmb[b][:, :], AF.Identity, scale=rs_[:, tl:tl + 1],
                    bias=mrs[:, tl:tl + 1]), inc="sac")
                for sem, tk in u_free[b]:
                    E.wait("vector", sem, tk)
                E.wait("vector", "sac", tq_tok)
                ld_tok = E.do("vector", lambda v, b=b: v.tensor_scalar(
                    u2[b][:, :], tq[b][:, :], MR, MR, ALU.add, ALU.subtract),
                    inc="sdv")
                tq_free[b] = ld_tok
                if tl + 2 < NT:
                    E.wait("sync", "sdv", ld_tok)
                    xm_tok[b] = E.dma("sync", "sx", lambda s, tl=tl, b=b: s.dma_start(
                        xmb[b][:, :], xmid_h[(tl + 2) * P:(tl + 3) * P, :]))
                evs = []
                for half in range(2):
                    bank = half
                    E.wait("tensor", "sdv", ld_tok)
                    E.wait("tensor", ev_t2[bank][0], ev_t2[bank][1])
                    for i in range(3):
                        cc = half * 4 + i
                        E.do("tensor", lambda tt, bank=bank, i=i, cc=cc, b=b:
                             tt.transpose(
                                 ps_t2[bank][:, i * P:(i + 1) * P],
                                 u2[b][:, cc * P:(cc + 1) * P], idb[:, :]))
                    tp = E.do("tensor", lambda tt, bank=bank, half=half, b=b:
                              tt.transpose(
                                  ps_t2[bank][:, 3 * P:4 * P],
                                  u2[b][:, (half * 4 + 3) * P:(half * 4 + 4) * P],
                                  idb[:, :]), inc="smm")
                    eng = "scalar"
                    sem = "sac"
                    E.wait(eng, "smm", tp)

                    def _evac(e, half=half, tl=tl, bank=bank):
                        dst = bass.AP(x2qT, (half * 4) * T + tl * P,
                                      [[NCH * T, P], [T, 4], [1, P]])
                        if hasattr(e, "tensor_copy"):
                            return e.tensor_copy(dst, ps_t2[bank][:, :])
                        return e.copy(dst, ps_t2[bank][:, :])
                    ev = E.do(eng, _evac, inc=sem)
                    ev_t2[bank] = (sem, ev)
                    evs.append((sem, ev))
                u_free[b] = evs
            E.wait_tok("sync", E.queue_tok("sx"))
            E.flush_block()
            x2stack.close()

            ps_w = [s2k.enter_context(nc.psum_tensor(f"ps_w{i}", [P, 512], F32))
                    for i in range(4)]
            winst = [psb(f"winst{i}", [P, NCH, P], F32) for i in range(2)]
            wtq = psb("wtq", [P, C], F32)
            wq_fc = [psb(f"wqfc{i}", [P, NCH, P], BF16) for i in range(2)]
            h_sb = [psb(f"hsb{i}", [P, 512], BF16) for i in range(3)]
            aw = psb("aw", [P, 1], F32)
            swv = psb("swv", [P, 1], F32)
            schf = psb("schf", [P, 1], F32)
            ps_wb = s2k.enter_context(nc.psum_tensor("ps_wb", [P, 512], BF16))

            E.do("vector", lambda v: v.memset(am_acc[:, :], 0.0))

            def win_slice_ap(fc):
                return bass.AP(win_h, fc * P, [[FF, P], [P * FF, NCH], [1, P]])

            win_tok = [None, None]
            win_rd = [0, 0]
            wt_tp = [0, 0]
            wq_rd = [0, 0]
            ev_w = [("sdv", 0)] * 2
            ev_wb = ("sdv", 0)
            cast_prev = 0
            gelu_prev = 0
            ev_h = [0, 0]
            h_dma = [[], [], []]

            def emit_wt_trans(fc):
                b = fc % 2
                toks = []
                for half in range(2):
                    bank = b * 2 + half
                    E.wait_tok("tensor", win_tok[b])
                    E.wait("tensor", ev_w[b][0], ev_w[b][1])
                    for i in range(3):
                        cc = half * 4 + i
                        E.do("tensor", lambda tt, bank=bank, i=i, cc=cc, b=b:
                             tt.transpose(ps_w[bank][:, i * P:(i + 1) * P],
                                          winst[b][:, cc, :], idf[:, :]))
                    tp = E.do("tensor", lambda tt, bank=bank, b=b, half=half:
                              tt.transpose(ps_w[bank][:, 3 * P:4 * P],
                                           winst[b][:, half * 4 + 3, :], idf[:, :]),
                              inc="smm")
                    toks.append(tp)
                    if half == 1:
                        win_rd[b] = tp
                wt_tp[b] = toks[1]

            win_tok[0] = E.dma("sync", "sx", lambda s: s.dma_start(
                winst[0][:, :, :], win_slice_ap(0)))
            win_tok[1] = E.dma("sync", "sx", lambda s: s.dma_start(
                winst[1][:, :, :], win_slice_ap(1)))
            emit_wt_trans(0)

            for fc in range(NF):
                b = fc % 2
                if fc + 2 < NF:
                    nb = fc % 2
                    E.wait("sync", "smm", win_rd[nb])
                    win_tok[nb] = E.dma("sync", "sx", lambda s, fc=fc, nb=nb:
                                        s.dma_start(winst[nb][:, :, :],
                                                    win_slice_ap(fc + 2)))
                if fc + 1 < NF:
                    emit_wt_trans(fc + 1)
                E.wait("vector", "smm", wt_tp[b])
                E.wait("vector", "sac", gelu_prev)
                E.do("vector", lambda v, b=b: v.tensor_reduce(
                    aw[:, :], ps_w[2 * b][:, :], AX.X, ALU.max,
                    apply_absolute_value=True), drain=False)
                E.do("vector", lambda v, b=b: v.tensor_reduce(
                    swv[:, :], ps_w[2 * b + 1][:, :], AX.X, ALU.max,
                    apply_absolute_value=True), drain=False)
                E.do("vector", lambda v: v.tensor_max(aw[:, :], aw[:, :], swv[:, :]))
                E.do("vector", lambda v: v.tensor_scalar_max(aw[:, :], aw[:, :], 1.0e-8))
                E.do("vector", lambda v: v.reciprocal(swv[:, :], aw[:, :]))
                E.do("vector", lambda v: v.tensor_scalar_mul(swv[:, :], swv[:, :], 127.0))
                E.do("vector", lambda v: v.tensor_scalar_mul(
                    schf[:, :], aw[:, :], 1.0 / 127.0), drain=False)
                E.wait("vector", "sgp")
                sch_tok = E.do("vector", lambda v: v.tensor_mul(
                    schf[:, :], schf[:, :], s1i_pp[:, :]), inc="sdv")
                E.wait("vector", "sac", cast_prev)
                E.do("vector", lambda v, b=b: v.tensor_scalar(
                    wtq[:, 0:512], ps_w[2 * b][:, :], swv[:, 0:1], None, ALU.mult))
                E.do("vector", lambda v, b=b: v.tensor_scalar(
                    wtq[:, 512:1024], ps_w[2 * b + 1][:, :], swv[:, 0:1], None,
                    ALU.mult), drain=False)
                uq_tok = E.do("vector", lambda v: v.tensor_scalar(
                    wtq[:, :], wtq[:, :], MR, MR, ALU.add, ALU.subtract), inc="sdv")
                ev_w[b] = ("sdv", uq_tok)
                E.wait("scalar", "sdv", uq_tok)
                E.wait("scalar", ev_wb[0], ev_wb[1])
                cast_prev = E.do("scalar", lambda s: s.copy(u_sb[:, :], wtq[:, :]),
                                 inc="sac")
                for half in range(2):
                    E.wait("tensor", "sac", cast_prev)
                    E.wait("tensor", ev_wb[0], ev_wb[1])
                    E.wait("tensor", "smm", wq_rd[b])
                    for i in range(3):
                        cc = half * 4 + i
                        E.do("tensor", lambda tt, i=i, cc=cc: tt.transpose(
                            ps_wb[:, i * P:(i + 1) * P], u_sb[:, cc * P:(cc + 1) * P],
                            idb[:, :]))
                    tp = E.do("tensor", lambda tt, half=half: tt.transpose(
                        ps_wb[:, 3 * P:4 * P],
                        u_sb[:, (half * 4 + 3) * P:(half * 4 + 4) * P], idb[:, :]),
                        inc="smm")
                    E.wait("vector", "smm", tp)
                    ev = E.do("vector", lambda v, half=half, b=b: v.tensor_copy(
                        bass.AP(wq_fc[b], (half * 4) * P, [[NCH * P, P], [P, 4], [1, P]]),
                        ps_wb[:, :]), inc="sdv")
                    ev_wb = ("sdv", ev)
                wq_ready = E.tok("sdv")

                for tb_ in range(8):
                    hb = (fc * 8 + tb_) % 3
                    bank = tb_ % 2
                    E.wait("tensor", "sdv", wq_ready)
                    E.wait("tensor", "sac", ev_h[bank])
                    mm_tok = None
                    for cc in range(NCH):
                        kw = dict(start=(cc == 0), stop=(cc == NCH - 1))
                        lam = (lambda tt, b=b, cc=cc, tb_=tb_, bank=bank, kw=kw:
                               tt.matmul(ps_h[bank][:, :], wq_fc[b][:, cc, :],
                                         x2qT[:, cc, tb_ * 512:(tb_ + 1) * 512], **kw))
                        if cc == NCH - 1:
                            mm_tok = E.do("tensor", lam, inc="smm")
                        else:
                            E.do("tensor", lam)
                    if tb_ == 7:
                        wq_rd[b] = mm_tok
                    E.wait("scalar", "smm", mm_tok)
                    E.wait("scalar", "sdv", sch_tok)
                    E.wait_tok("scalar", h_dma[hb])
                    ge_tok = E.do("scalar", lambda s, hb=hb, fc=fc, bank=bank: s.activation(
                        h_sb[hb][:, :], ps_h[bank][:, :], AF.Gelu,
                        bias=binp[:, fc:fc + 1], scale=schf[:, 0:1]), inc="sac")
                    ev_h[bank] = ge_tok
                    gelu_prev = ge_tok
                    E.wait("vector", "sac", ge_tok)
                    E.do("vector", lambda v, hb=hb: v.tensor_reduce(
                        ta[:, :], h_sb[hb][:, :], AX.X, ALU.max,
                        apply_absolute_value=True))
                    am_tok = E.do("vector", lambda v: v.tensor_max(
                        am_acc[:, :], am_acc[:, :], ta[:, :]), inc="sdv")
                    E.wait("gpsimd", "sdv", am_tok)
                    h_dma[hb] = E.dma("gpsimd", "sgd",
                                      lambda g, fc=fc, tb_=tb_, hb=hb: g.dma_start(
                                          bass.AP(hscr_h,
                                                  2 * tb_ * P * NF * 256 + fc * 256,
                                                  [[NF * 256, P], [P * NF * 256, 2],
                                                   [1, 256]]),
                                          bass.AP(h_sb[hb], 0,
                                                  [[512, P], [256, 2], [1, 256]])))

            E.wait_tok("gpsimd", E.queue_tok("sgd"))
            E.wait_tok("sync", E.queue_tok("sx"))
            E.flush_block()

            emit_collective(cc2_in, cc2_out, s2_pp, s2i_pp)

        if phases < 4:
            return nc
        with contextlib.ExitStack() as s3k:
            psb = lambda n, s, d: s3k.enter_context(nc.sbuf_tensor(n, s, d))
            wqout = psb("wqout", [P, NF, C], BF16)
            wst = [psb(f"wst{i}", [P, C], F32) for i in range(2)]
            hin = [psb(f"hin{i}", [P, NF, 256], BF16) for i in range(2)]
            hq = [psb(f"hq{i}", [P, NF, 256], BF16) for i in range(2)]
            thq = psb("thq", [P, NF * 256], F32)
            xmb2 = [psb(f"xmb2{i}", [P, C], F32) for i in range(2)]
            ffn = psb("ffn", [P, C], F32)
            outb = [psb(f"outb{i}", [P, C], F32) for i in range(2)]
            st6 = psb("st6c", [P, 12], F32)
            mv2 = psb("mv2c", [P, 2], F32)
            ta = psb("tac", [P, 1], F32)
            tb = psb("tbc", [P, 1], F32)
            aw = psb("awc", [P, 1], F32)
            swo = psb("swoc", [P, 1], F32)
            swoi = psb("swoic", [P, 1], F32)
            ps_f = [s3k.enter_context(nc.psum_tensor(f"ps_f{i}", [P, C], F32))
                    for i in range(2)]

            wst_tok = [None, None]
            wst_tok[0] = E.dma("sync", "sx", lambda s: s.dma_start(
                wst[0][:, :], wout_h[0:P, :]))
            used_tok = [0, 0]
            cast_tok = 0
            for fc in range(NF):
                b = fc % 2
                if fc + 1 < NF:
                    nb = (fc + 1) % 2
                    E.wait("sync", "sdv", used_tok[nb])
                    wst_tok[nb] = E.dma("sync", "sx", lambda s, fc=fc, nb=nb: s.dma_start(
                        wst[nb][:, :], wout_h[(fc + 1) * P:(fc + 2) * P, :]))
                E.wait_tok("vector", wst_tok[b])
                E.wait("vector", "sac", cast_tok)
                E.do("vector", lambda v, b=b: v.tensor_reduce(
                    aw[:, :], wst[b][:, :], AX.X, ALU.max, apply_absolute_value=True))
                E.do("vector", lambda v: v.tensor_scalar_max(aw[:, :], aw[:, :], 1.0e-8))
                E.do("vector", lambda v: v.reciprocal(swo[:, :], aw[:, :]))
                E.do("vector", lambda v: v.tensor_scalar_mul(swo[:, :], swo[:, :], 127.0))
                E.do("vector", lambda v: v.tensor_scalar_mul(
                    swoi[:, :], aw[:, :], 1.0 / 127.0))
                E.do("vector", lambda v, b=b: v.tensor_scalar(
                    thq[:, 0:C], wst[b][:, :], swo[:, 0:1], None, ALU.mult))
                used_tok[b] = E.do("vector", lambda v: v.tensor_scalar(
                    thq[:, 0:C], thq[:, 0:C], MR, MR, ALU.add, ALU.subtract), inc="sdv")
                E.wait("scalar", "sdv", used_tok[b])
                cast_tok = E.do("scalar", lambda s, fc=fc: s.activation(
                    wqout[:, fc, :], thq[:, 0:C], AF.Copy, scale=swoi[:, 0:1]), inc="sac")
            wq_done = cast_tok
            E.flush_block()

            NB = 16
            hin_tok = [None, None]
            hin_tok[0] = E.dma("sync", "sx", lambda s: s.dma_start(
                bass.AP(hin[0], 0, [[NF * 256, P], [1, NF * 256]]),
                hscr_h[0, :, :]))
            xm2_tok = [None, None]
            xm2_tok[0] = E.dma("sync", "sx", lambda s: s.dma_start(
                xmb2[0][:, :], xmid_h[0:P, :]))
            hin_used = [0, 0]
            hq_rd = [0, 0]
            tq_prev = 0
            ev_f = [0, 0]
            out_dma = [[], []]
            xm_used = [0, 0]

            def emit_quant(blk):
                nonlocal tq_prev
                b = blk % 2
                E.wait_tok("scalar", hin_tok[b])
                E.wait("scalar", "sgp")
                E.wait("scalar", "sdv", tq_prev)
                hin_used[b] = E.do("scalar", lambda s, b=b: s.activation(
                    thq[:, :], bass.AP(hin[b], 0, [[NF * 256, P], [1, NF * 256]]),
                    AF.Copy, scale=s2_pp[:, 0:1]), inc="sac")
                E.wait("vector", "sac", hin_used[b])
                E.wait("vector", "smm", hq_rd[b])
                tq_prev = E.do("vector", lambda v, b=b: v.tensor_scalar(
                    bass.AP(hq[b], 0, [[NF * 256, P], [1, NF * 256]]),
                    thq[:, :], MR, MR, ALU.add, ALU.subtract), inc="sdv")
                return tq_prev

            hq_toks = [0, 0]
            hq_toks[0] = emit_quant(0)
            for blk in range(NB):
                b = blk % 2
                nb = (blk + 1) % 2
                if blk + 1 < NB:
                    E.wait("sync", "sac", hin_used[nb])
                    hin_tok[nb] = E.dma("sync", "sx",
                                        lambda s, blk=blk, nb=nb: s.dma_start(
                                            bass.AP(hin[nb], 0,
                                                    [[NF * 256, P], [1, NF * 256]]),
                                            hscr_h[blk + 1, :, :]))
                    E.wait("sync", "sdv", xm_used[nb])
                    xm2_tok[nb] = E.dma("sync", "sx", lambda s, blk=blk, nb=nb: s.dma_start(
                        xmb2[nb][:, :],
                        xmid_h[(blk + 1) * 2 * P:((blk + 1) * 2 + 1) * P, :]))
                    hq_toks[nb] = emit_quant(blk + 1)
                hq_tok = hq_toks[b]

                for sub in range(2):
                    tl = blk * 2 + sub
                    bank = sub
                    E.wait("tensor", "sdv", hq_tok)
                    E.wait("tensor", "sac", wq_done)
                    E.wait("tensor", "sdv", ev_f[bank])
                    mm_tok = None
                    for fc in range(NF):
                        for pc in range(2):
                            kw = dict(start=(fc == 0), stop=(fc == NF - 1))
                            lam = (lambda tt, fc=fc, sub=sub, pc=pc, bank=bank, kw=kw,
                                   b=b: tt.matmul(
                                       ps_f[bank][:, pc * 512:(pc + 1) * 512],
                                       hq[b][:, fc, sub * P:(sub + 1) * P],
                                       wqout[:, fc, pc * 512:(pc + 1) * 512], **kw))
                            if fc == NF - 1 and pc == 1:
                                mm_tok = E.do("tensor", lam, inc="smm")
                            else:
                                E.do("tensor", lam)
                    if sub == 1:
                        hq_rd[b] = mm_tok
                    E.wait("vector", "smm", mm_tok)
                    ev_f[bank] = E.do("vector", lambda v, bank=bank: v.scalar_tensor_tensor(
                        ffn[:, :], ps_f[bank][:, :], s2i_pp[:, 0:1], bout_t[:, :],
                        ALU.mult, ALU.add), inc="sdv")
                    E.do("vector", lambda v: v.bn_stats(st6[:, 0:6], ffn[:, 0:512]))
                    E.do("vector", lambda v: v.bn_stats(st6[:, 6:12], ffn[:, 512:1024]))
                    E.do("vector", lambda v: v.bn_aggr(mv2[:, :], st6[:, :]), inc="sdv")
                    E.wait("scalar", "sdv")
                    E.do("scalar", lambda s: s.activation(
                        ta[:, :], mv2[:, 1:2], AF.Ln, bias=EPS_LN))
                    E.do("scalar", lambda s: s.activation(
                        ta[:, :], ta[:, :], AF.Exp, scale=-0.5), inc="sac")
                    E.wait("vector", "sac")
                    E.do("vector", lambda v: v.tensor_mul(tb[:, :], mv2[:, 0:1], ta[:, :]))
                    E.wait_tok("vector", xm2_tok[b])
                    E.wait_tok("vector", out_dma[b])
                    E.do("vector", lambda v, b=b: v.tensor_scalar(
                        outb[b][:, :], xmb2[b][:, :], 1.0, tb[:, :],
                        ALU.mult, ALU.subtract))
                    out_tok = E.do("vector", lambda v, b=b: v.scalar_tensor_tensor(
                        outb[b][:, :], ffn[:, :], ta[:, 0:1], outb[b][:, :],
                        ALU.mult, ALU.add), inc="sdv")
                    if sub == 1:
                        xm_used[b] = out_tok
                    E.wait("gpsimd", "sdv", out_tok)
                    out_dma[b] = E.dma("gpsimd", "sgd", lambda g, tl=tl, b=b: g.dma_start(
                        out_h[tl * P:(tl + 1) * P, :], outb[b][:, :]))
                    if sub == 0:
                        E.wait("sync", "sdv", out_tok)
                        xm2_tok[b] = E.dma("sync", "sx", lambda s, blk=blk, b=b:
                                           s.dma_start(
                                               xmb2[b][:, :],
                                               xmid_h[(blk * 2 + 1) * P:
                                                      (blk * 2 + 2) * P, :]))

            E.wait_tok("sync", E.queue_tok("sx"))
            E.wait_tok("gpsimd", E.queue_tok("sgd"))
            E.flush_block()

    return nc


_NC_CACHE = {}


def kernel(L0_attn_x, L0_attn_w_qkv, L0_attn_ln_w, L0_attn_w_proj,
           L0_ffn_W_in, L0_ffn_b_in, L0_ffn_W_out, L0_ffn_b_out):
    x = np.ascontiguousarray(np.asarray(L0_attn_x))
    wqkv = np.ascontiguousarray(np.asarray(L0_attn_w_qkv))
    lnw = np.ascontiguousarray(np.asarray(L0_attn_ln_w)).reshape(1, C)
    wproj = np.ascontiguousarray(np.asarray(L0_attn_w_proj))
    win = np.ascontiguousarray(np.asarray(L0_ffn_W_in))
    bin_ = np.ascontiguousarray(np.asarray(L0_ffn_b_in)).reshape(1, FF)
    wout = np.ascontiguousarray(np.asarray(L0_ffn_W_out))
    bout = np.ascontiguousarray(np.asarray(L0_ffn_b_out)).reshape(1, C)

    if "nc" not in _NC_CACHE:
        _NC_CACHE["nc"] = build_nc()
    nc = _NC_CACHE["nc"]

    B = x.shape[0]
    per = B // N_CORES
    in_maps = []
    for c in range(N_CORES):
        shard = np.ascontiguousarray(x[c * per:(c + 1) * per].reshape(T, C))
        in_maps.append({
            "x": shard, "wqkv": wqkv, "lnw": lnw, "wproj": wproj,
            "win": win, "bin": bin_, "wout": wout, "bout": bout,
        })
    res = run_bass_kernel_spmd(nc, in_maps, core_ids=list(range(N_CORES)))
    out = np.concatenate(
        [np.asarray(r["out"]).reshape(per, 2048, C) for r in res.results], axis=0)
    return out


# version 13
# speedup vs baseline: 1.0704x; 1.0704x over previous
import contextlib
import numpy as np

import concourse.bass as bass
import concourse.mybir as mybir
from concourse.bass_utils import run_bass_kernel_spmd

F32 = mybir.dt.float32
BF16 = mybir.dt.bfloat16
AF = mybir.ActivationFunctionType
ALU = mybir.AluOpType
AX = mybir.AxisListType

P = 128
C = 1024
NCH = 8
T = 4096
NT = 32
NG = 16
H = 16
HD = 64
FF = 4096
NF = 32
MR = 12582912.0
NEGBIG = -1.0e30
EPS_LN = 1.0e-5
EPS_RMS = 1.0e-6
N_CORES = 8


class Emitter:

    ENGINES = ("sync", "tensor", "vector", "scalar", "gpsimd")

    def __init__(self, nc, stack):
        self.nc = nc
        self.q = {e: [] for e in self.ENGINES}
        self.sems = {}
        self.cnt = {}
        self.stack = stack
        self.dmaq = {}
        self.dmaidx = {}

    def sem(self, name):
        self.sems[name] = self.stack.enter_context(self.nc.semaphore(name))
        self.cnt[name] = 0

    def inc(self, name, n=1):
        self.cnt[name] += n
        return (self.sems[name], n)

    def tok(self, name):
        return self.cnt[name]

    def wait(self, eng, name, value=None):
        v = self.cnt[name] if value is None else value
        assert v <= self.cnt[name], (name, v, self.cnt[name])
        if v <= 0:
            return
        h = self.sems[name]
        self.q[eng].append(lambda e, h=h, v=v: e.wait_ge(h, v))

    DRAIN_ENGINES = ("vector", "scalar", "gpsimd")

    def do(self, eng, build, inc=None, n=1, drain=None):
        if drain is None:
            drain = eng in self.DRAIN_ENGINES
        if drain:
            self.q[eng].append(lambda e: e.drain())
        tgt = self.inc(inc, n) if inc else None

        def fn(e, build=build, tgt=tgt):
            ins = build(e)
            if tgt is not None:
                ins.then_inc(*tgt)

        self.q[eng].append(fn)
        return self.cnt[inc] if inc else None

    def dma_queue(self, qname, nlanes=4):
        lanes = []
        for i in range(nlanes):
            nm = f"{qname}{i}"
            self.sem(nm)
            lanes.append(nm)
        self.dmaq[qname] = lanes
        self.dmaidx[qname] = 0

    def dma(self, eng, qname, build):
        lanes = self.dmaq[qname]
        lane = lanes[self.dmaidx[qname] % len(lanes)]
        self.dmaidx[qname] += 1
        self.wait(eng, lane)
        v = self.do(eng, build, inc=lane, n=16)
        return [(lane, v)]

    def queue_tok(self, qname):
        return [(lane, self.cnt[lane]) for lane in self.dmaq[qname]]

    def wait_tok(self, eng, tok):
        for nm, v in tok:
            self.wait(eng, nm, v)

    def flush_block(self):
        qs = {e: fns for e, fns in self.q.items() if fns}
        with self.nc.Block() as blk:
            for e, fns in qs.items():
                def runner(engine, fns=fns):
                    for f in fns:
                        f(engine)
                getattr(blk, e)(runner)
        self.q = {e: [] for e in self.ENGINES}


def build_nc(phases=4):
    nc = bass.Bass("TRN2", target_bir_lowering=False)

    x_h = nc.dram_tensor("x", [T, C], F32, kind="ExternalInput")
    wqkv_h = nc.dram_tensor("wqkv", [C, 3 * C], F32, kind="ExternalInput")
    lnw_h = nc.dram_tensor("lnw", [1, C], F32, kind="ExternalInput")
    wproj_h = nc.dram_tensor("wproj", [C, C], F32, kind="ExternalInput")
    win_h = nc.dram_tensor("win", [C, FF], F32, kind="ExternalInput")
    bin_h = nc.dram_tensor("bin", [1, FF], F32, kind="ExternalInput")
    wout_h = nc.dram_tensor("wout", [FF, C], F32, kind="ExternalInput")
    bout_h = nc.dram_tensor("bout", [1, C], F32, kind="ExternalInput")
    out_h = nc.dram_tensor("out", [T, C], F32, kind="ExternalOutput")

    xmid_h = nc.dram_tensor("xmid_scr", [T, C], F32)
    hscr_h = nc.dram_tensor("h_scr", [16, P, NF * 256], BF16)
    cc1_in = nc.dram_tensor("cc1_in", [1, 8], F32)
    cc1_out = nc.dram_tensor("cc1_out", [1, 8], F32)
    cc2_in = nc.dram_tensor("cc2_in", [1, 8], F32)
    cc2_out = nc.dram_tensor("cc2_out", [1, 8], F32)

    with contextlib.ExitStack() as gstack:
        E = Emitter(nc, gstack)
        for s in ("smm", "sdv", "sac", "sgp", "scc"):
            E.sem(s)
        E.dma_queue("sx", 6)
        E.dma_queue("sa", 3)
        E.dma_queue("sgd", 4)

        sbg = lambda name, shape, dt: gstack.enter_context(nc.sbuf_tensor(name, shape, dt))

        lnwq_t = sbg("lnwq_t", [P, C], F32)
        lnwk_t = sbg("lnwk_t", [P, C], F32)
        bout_t = sbg("bout_t", [P, C], F32)
        binp = sbg("binp", [P, NF], F32)
        idb = sbg("idb", [P, P], BF16)
        idf = sbg("idf", [P, P], F32)
        m01 = sbg("m01", [P, P], F32)
        m01q1 = sbg("m01q1", [P, 256], F32)
        c127 = sbg("c127", [P, 1], F32)
        c32k = sbg("c32k", [1, 1], F32)
        ceps1 = sbg("ceps1", [P, 1], F32)
        ceps2 = sbg("ceps2", [P, 1], F32)
        cm05 = sbg("cm05", [P, 1], F32)
        nc.const_aps.register(F32, EPS_LN, ceps1[:, :])
        nc.const_aps.register(F32, EPS_RMS, ceps2[:, :])
        nc.const_aps.register(F32, -0.5, cm05[:, :])
        s1_pp = sbg("s1_pp", [P, 1], F32)
        s1i_pp = sbg("s1i_pp", [P, 1], F32)
        s2_pp = sbg("s2_pp", [P, 1], F32)
        s2i_pp = sbg("s2i_pp", [P, 1], F32)
        am_acc = sbg("am_acc", [P, 1], F32)
        am_row = sbg("am_row", [1, P], F32)
        ccsb = sbg("ccsb", [1, 8], F32)
        sc_a = sbg("sc_a", [1, 1], F32)
        sc_b = sbg("sc_b", [1, 1], F32)
        sc_c = sbg("sc_c", [1, 1], F32)
        rln = sbg("rln", [P, NT], F32)
        mrln = sbg("mrln", [P, NT], F32)
        mvst = sbg("mvst", [P, 2 * NT], F32)
        gmax = sbg("gmax", [P, NT], F32)

        with contextlib.ExitStack() as pstack:
            psb = lambda n, s, d: pstack.enter_context(nc.sbuf_tensor(n, s, d))
            onesf = psb("onesf", [P, P], F32)
            zerof = psb("zerof", [P, P], F32)
            bstage = psb("bstage", [NF, P], F32)
            bps = pstack.enter_context(nc.psum_tensor("bps", [P, NF], F32))

            tok_ld = []
            tok_ld += E.dma("sync", "sx", lambda s: s.dma_start(
                lnwq_t[:, :], bass.AP(lnw_h, 0, [[0, P], [1, C]])))
            tok_ld += E.dma("sync", "sx", lambda s: s.dma_start(
                lnwk_t[:, :], bass.AP(lnw_h, 0, [[0, P], [1, C]])))
            tok_ld += E.dma("sync", "sx", lambda s: s.dma_start(
                bout_t[:, :], bass.AP(bout_h, 0, [[0, P], [1, C]])))
            tok_ld += E.dma("sync", "sx", lambda s: s.dma_start(
                bstage[:, :], bass.AP(bin_h, 0, [[P, NF], [1, P]])))

            E.do("gpsimd", lambda g: g.memset(onesf[:, :], 1.0), inc="sgp")
            ms_tok = E.do("gpsimd", lambda g: g.memset(zerof[:, :], 0.0), inc="sgp")
            E.wait("gpsimd", "sgp", ms_tok)
            E.do("gpsimd", lambda g: g.affine_select(
                idf[:, :], onesf[:, :], [[1, P]], ALU.is_equal, 0.0,
                channel_multiplier=-1), inc="sgp")
            E.do("gpsimd", lambda g: g.affine_select(
                m01[:, :], zerof[:, :], [[-1, P]], ALU.is_ge, NEGBIG,
                channel_multiplier=1), inc="sgp")
            E.do("gpsimd", lambda g: g.memset(m01q1[:, 0:P], 0.0), inc="sgp")
            E.do("gpsimd", lambda g: g.affine_select(
                m01q1[:, P:256], zerof[:, :], [[-1, P]], ALU.is_ge, NEGBIG,
                channel_multiplier=1), inc="sgp")
            E.do("gpsimd", lambda g: g.memset(c127[:, :], 127.0), inc="sgp")
            E.do("gpsimd", lambda g: g.memset(ceps1[:, :], EPS_LN), inc="sgp")
            E.do("gpsimd", lambda g: g.memset(ceps2[:, :], EPS_RMS), inc="sgp")
            E.do("gpsimd", lambda g: g.memset(cm05[:, :], -0.5), inc="sgp")
            E.do("gpsimd", lambda g: g.memset(c32k[:, :], 32767.0), inc="sgp")
            tok_gp = E.tok("sgp")

            E.wait("vector", "sgp", tok_gp)
            E.wait_tok("vector", tok_ld)
            E.do("vector", lambda v: v.tensor_copy(idb[:, :], idf[:, :]))
            E.do("vector", lambda v: v.tensor_scalar_mul(
                lnwq_t[:, :], lnwq_t[:, :], 0.125), inc="sdv")

            E.wait("tensor", "sgp", tok_gp)
            E.wait_tok("tensor", tok_ld)
            E.do("tensor", lambda t: t.transpose(
                bps[:, :], bstage[:, :], idf[0:NF, 0:NF]), inc="smm")
            E.wait("vector", "smm")
            E.do("vector", lambda v: v.tensor_copy(binp[:, :], bps[:, :]), inc="sdv")
            E.flush_block()

        with contextlib.ExitStack() as s1k:
            psb = lambda n, s, d: s1k.enter_context(nc.sbuf_tensor(n, s, d))
            pps = lambda n, s, d: s1k.enter_context(nc.psum_tensor(n, s, d))

            wqkv_sb = psb("wqkv_sb", [P, NCH, 3 * C], BF16)
            wproj_sb = psb("wproj_sb", [P, NCH, C], BF16)
            wstage = [psb(f"wstage{i}", [P, 3 * C], F32) for i in range(2)]

            cast_tok = [0, 0]
            for cc in range(NCH):
                b = cc % 2
                E.wait("sync", "sdv", cast_tok[b])
                wt = E.dma("sync", "sx", lambda s, cc=cc, b=b: s.dma_start(
                    wstage[b][:, :], wqkv_h[cc * P:(cc + 1) * P, :]))
                E.wait_tok("vector", wt)
                cast_tok[b] = E.do("vector", lambda v, cc=cc, b=b: v.tensor_copy(
                    wqkv_sb[:, cc, :], wstage[b][:, :]), inc="sdv")
            for cc in range(NCH):
                b = cc % 2
                E.wait("sync", "sdv", cast_tok[b])
                wt = E.dma("sync", "sx", lambda s, cc=cc, b=b: s.dma_start(
                    wstage[b][:, 0:C], wproj_h[cc * P:(cc + 1) * P, :]))
                E.wait_tok("vector", wt)
                cast_tok[b] = E.do("vector", lambda v, cc=cc, b=b: v.tensor_copy(
                    wproj_sb[:, cc, :], wstage[b][:, 0:C]), inc="sdv")
            E.flush_block()

            xbuf = [[psb(f"xb{pb}_{t}", [P, C], F32) for t in range(2)] for pb in range(2)]
            x1c = [psb(f"x1c{t}", [P, C], BF16) for t in range(2)]
            qnb = [psb(f"qnb{t}", [P, C], BF16) for t in range(2)]
            knb = [psb(f"knb{t}", [P, C], BF16) for t in range(2)]
            x1T = psb("x1T", [P, 2, NCH, P], BF16)
            v_sb = psb("v_sb", [P, 2, C], BF16)
            qT = psb("qT", [P, NCH, 256], BF16)
            kT = psb("kT", [P, NCH, 256], BF16)
            pP = [psb(f"pP{i}", [P, 2, 384], F32) for i in range(2)]
            pmP = [psb(f"pmP{i}", [P, 2, 384], F32) for i in range(2)]
            pn = [psb(f"pn{i}", [P, 2, 384], BF16) for i in range(2)]
            pTb = [psb(f"pTb{i}", [P, 768], BF16) for i in range(2)]
            y_sb = psb("y_sb", [P, NCH, 256], BF16)
            xm_sb = [[psb(f"xm{pb}_{t}", [P, C], F32) for t in range(2)] for pb in range(2)]
            xp_t = [psb(f"xp{t}", [P, C], F32) for t in range(2)]
            st6 = psb("st6", [P, 12], F32)
            mv2 = psb("mv2", [P, 2], F32)
            t128a = psb("t128a", [P, 1], F32)
            t128b = psb("t128b", [P, 1], F32)
            tmx = psb("tmx", [P, 1], F32)
            tmn = psb("tmn", [P, 1], F32)
            rqk = [psb(f"rqk{i}", [P, 1], F32) for i in range(2)]
            nmq = [psb(f"nmq{i}", [P, 4], F32) for i in range(2)]
            rsq = [psb(f"rsq{i}", [P, 4], F32) for i in range(2)]

            ps_big = pps("ps_big", [P, 3072], F32)
            ps_tp = [pps(f"ps_tp{i}", [P, 768], BF16) for i in range(2)]

            def BK(i, off=0, n=512):
                return ps_big[:, i * 512 + off:i * 512 + off + n]

            x_need = [[], []]
            x_need[0] = (E.dma("sync", "sx", lambda s: s.dma_start(
                             xbuf[0][0][:, :], x_h[0:P, :]))
                         + E.dma("sync", "sx", lambda s: s.dma_start(
                             xbuf[0][1][:, :], x_h[P:2 * P, :])))

            ev_bank = [("sdv", 0)] * 6
            evtp = [("sdv", 0), ("sac", 0)]
            nm_rd = [0, 0]
            pp_rd = [0, 0]
            y_rd = [0, 0]
            buf_free = {}
            xm_guard = [[[], []], [[], []]]
            y_proj_rd = 0
            v_y_rd = 0
            t128_rd = 0
            rqk_rd = [0, 0]

            def tbatch(srcs, psum_bank, dst_ap_fn, eng, sem):
                pst = ps_tp[psum_bank]
                n = len(srcs)
                for i in range(n - 1):
                    E.do("tensor", lambda tt, i=i: tt.transpose(
                        pst[:, i * P:(i + 1) * P], srcs[i](), idb[:, :]))
                tp = E.do("tensor", lambda tt, i=n - 1: tt.transpose(
                    pst[:, i * P:(i + 1) * P], srcs[i](), idb[:, :]), inc="smm")
                E.wait(eng, "smm", tp)
                if eng == "scalar":
                    ev = E.do(eng, lambda e: e.copy(
                        dst_ap_fn(), pst[:, 0:n * P]), inc=sem)
                else:
                    ev = E.do(eng, lambda e: e.tensor_copy(
                        dst_ap_fn(), pst[:, 0:n * P]), inc=sem)
                return (sem, ev)

            ln1_tok = [0, 0]

            def emit_ln1(gg, t):
                ppb = gg % 2
                xt = xbuf[ppb][t]
                E.wait_tok("vector", x_need[ppb])
                E.do("vector", lambda v, xt=xt: v.bn_stats(st6[:, 0:6], xt[:, 0:512]))
                E.do("vector", lambda v, xt=xt: v.bn_stats(st6[:, 6:12],
                                                           xt[:, 512:1024]))
                E.do("vector", lambda v: v.bn_aggr(mv2[:, :], st6[:, :]), inc="sdv")
                E.wait("scalar", "sdv")
                E.wait("scalar", "sdv", t128_rd)
                E.do("scalar", lambda s: s.activation(
                    t128a[:, :], mv2[:, 1:2], AF.Ln, bias=EPS_LN))
                E.do("scalar", lambda s: s.activation(
                    t128a[:, :], t128a[:, :], AF.Exp, scale=-0.5), inc="sac")
                E.wait("vector", "sac")
                E.do("vector", lambda v: v.tensor_mul(
                    t128b[:, :], mv2[:, 0:1], t128a[:, :]))
                for sem, tk in buf_free.get(("x1c", t), []):
                    E.wait("vector", sem, tk)
                tok = E.do("vector", lambda v, xt=xt, t=t: v.tensor_scalar(
                    x1c[t][:, :], xt[:, :], t128a[:, :], t128b[:, :],
                    ALU.mult, ALU.subtract), inc="sdv")
                ln1_tok[t] = tok
                return tok

            t128_rd = emit_ln1(0, 0)
            t128_rd = emit_ln1(0, 1)

            for g in range(NG):
                pb = g % 2
                if g + 1 < NG:
                    npb = (g + 1) % 2
                    E.wait("sync", "sdv")
                    tks = []
                    for t in range(2):
                        tks += E.dma("sync", "sx", lambda s, g=g, t=t, npb=npb: s.dma_start(
                            xbuf[npb][t][:, :],
                            x_h[(g + 1) * 256 + t * P:(g + 1) * 256 + (t + 1) * P, :]))
                    x_need[npb] = tks

                qn_tok = [0, 0]
                kn_tok = [0, 0]
                for t in range(2):
                    xt = xbuf[pb][t]
                    x1_tok = ln1_tok[t]

                    evs = []
                    for half in range(2):
                        bank = half
                        E.wait("tensor", "sdv", x1_tok)
                        E.wait("tensor", evtp[bank][0], evtp[bank][1])
                        eng = "vector" if half == 0 else "scalar"
                        sem = "sdv" if half == 0 else "sac"
                        evtp[bank] = tbatch(
                            [(lambda t=t, cc=half * 4 + i: x1c[t][:, cc * P:(cc + 1) * P])
                             for i in range(4)],
                            bank,
                            lambda t=t, half=half: x1T[:, t, half * 4:(half + 1) * 4, :],
                            eng, sem)
                        evs.append(evtp[bank])
                    buf_free[("x1c", t)] = list(evs)

                    x1T_ready = list(evs)
                    qkv_tok = [0] * 6
                    for pc in range(6):
                        E.wait("tensor", ev_bank[pc][0], ev_bank[pc][1])
                        for sem, tk in x1T_ready:
                            E.wait("tensor", sem, tk)
                        x1T_ready = []
                        for cc in range(NCH):
                            kw = dict(start=(cc == 0), stop=(cc == NCH - 1))
                            lam = (lambda tt, t=t, cc=cc, pc=pc, kw=kw:
                                   tt.matmul(BK(pc), x1T[:, t, cc, :],
                                             wqkv_sb[:, cc, pc * 512:(pc + 1) * 512],
                                             **kw))
                            if cc == NCH - 1:
                                qkv_tok[pc] = E.do("tensor", lam, inc="smm")
                            else:
                                E.do("tensor", lam)

                    E.wait("vector", "smm", qkv_tok[1])
                    E.do("vector", lambda v: v.bn_stats(st6[:, 0:6], BK(0)))
                    E.do("vector", lambda v: v.bn_stats(st6[:, 6:12], BK(1)))
                    E.do("vector", lambda v: v.bn_aggr(mv2[:, :], st6[:, :]))
                    E.do("vector", lambda v: v.tensor_mul(t128b[:, :], mv2[:, 0:1],
                                                          mv2[:, 0:1]))
                    E.do("vector", lambda v: v.tensor_add(tmx[:, :], mv2[:, 1:2],
                                                          t128b[:, :]), inc="sdv")
                    E.wait("scalar", "sdv")
                    E.wait("scalar", "sdv", rqk_rd[0])
                    E.do("scalar", lambda s: s.activation(
                        rqk[0][:, :], tmx[:, :], AF.Ln, bias=EPS_RMS))
                    E.do("scalar", lambda s: s.activation(
                        rqk[0][:, :], rqk[0][:, :], AF.Exp, scale=-0.5), inc="sac")
                    E.wait("vector", "sac")
                    for sem, tk in buf_free.get(("qnb", t), []):
                        E.wait("vector", sem, tk)
                    qn_tok[t] = E.do("vector", lambda v, t=t: v.scalar_tensor_tensor(
                        qnb[t][:, :], ps_big[:, 0:1024], rqk[0][:, :], lnwq_t[:, :],
                        ALU.mult, ALU.mult), inc="sdv")
                    rqk_rd[0] = qn_tok[t]
                    ev_bank[0] = ev_bank[1] = ("sdv", qn_tok[t])

                    E.wait("vector", "smm", qkv_tok[3])
                    E.do("vector", lambda v: v.bn_stats(st6[:, 0:6], BK(2)))
                    E.do("vector", lambda v: v.bn_stats(st6[:, 6:12], BK(3)))
                    E.do("vector", lambda v: v.bn_aggr(mv2[:, :], st6[:, :]))
                    E.do("vector", lambda v: v.tensor_mul(t128b[:, :], mv2[:, 0:1],
                                                          mv2[:, 0:1]))
                    E.do("vector", lambda v: v.tensor_add(tmx[:, :], mv2[:, 1:2],
                                                          t128b[:, :]), inc="sdv")
                    E.wait("scalar", "sdv")
                    E.wait("scalar", "sdv", rqk_rd[1])
                    E.do("scalar", lambda s: s.activation(
                        rqk[1][:, :], tmx[:, :], AF.Ln, bias=EPS_RMS))
                    E.do("scalar", lambda s: s.activation(
                        rqk[1][:, :], rqk[1][:, :], AF.Exp, scale=-0.5), inc="sac")
                    E.wait("vector", "sac")
                    for sem, tk in buf_free.get(("knb", t), []):
                        E.wait("vector", sem, tk)
                    kn_tok[t] = E.do("vector", lambda v, t=t: v.scalar_tensor_tensor(
                        knb[t][:, :], ps_big[:, 1024:2048], rqk[1][:, :], lnwk_t[:, :],
                        ALU.mult, ALU.mult), inc="sdv")
                    rqk_rd[1] = kn_tok[t]
                    ev_bank[2] = ev_bank[3] = ("sdv", kn_tok[t])

                    E.wait("scalar", "smm", qkv_tok[5])
                    E.wait("scalar", "smm", v_y_rd)
                    E.do("scalar", lambda s, t=t: s.copy(v_sb[:, t, 0:512], BK(4)))
                    vev = E.do("scalar", lambda s, t=t: s.copy(v_sb[:, t, 512:1024],
                                                               BK(5)), inc="sac")
                    ev_bank[4] = ev_bank[5] = ("sac", vev)

                qkT_toks = []
                for t in range(2):
                    E.wait("sync", "sdv", qn_tok[t])
                    tks = []
                    for cc in range(NCH):
                        tks += E.dma("sync", "sx", lambda s, t=t, cc=cc: s.dma_start(
                            qT[:, cc, t * P:(t + 1) * P],
                            qnb[t][:, cc * P:(cc + 1) * P], transpose=True))
                    buf_free[("qnb", t)] = tks
                    qkT_toks += tks
                for t in range(2):
                    evs = []
                    for half in range(2):
                        bank = half
                        E.wait("tensor", "sdv", kn_tok[t])
                        E.wait("tensor", evtp[bank][0], evtp[bank][1])
                        eng = "vector" if half == 0 else "scalar"
                        sem = "sdv" if half == 0 else "sac"
                        evtp[bank] = tbatch(
                            [(lambda t=t, cc=half * 4 + i:
                              knb[t][:, cc * P:(cc + 1) * P]) for i in range(4)],
                            bank,
                            lambda half=half, t=t: bass.AP(
                                kT, (half * 4) * 256 + t * P,
                                [[NCH * 256, P], [256, 4], [1, P]]),
                            eng, sem)
                        evs.append(evtp[bank])
                    buf_free[("knb", t)] = list(evs)
                    qkT_toks.append(evtp[0])
                    qkT_toks.append(evtp[1])

                qkT_done = qkT_toks

                def emit_scores(hp):
                    for sem, tk in qkT_done:
                        E.wait("tensor", sem, tk)
                    qkT_done.clear()
                    E.wait("tensor", ev_bank[0][0], ev_bank[0][1])
                    E.wait("tensor", ev_bank[1][0], ev_bank[1][1])
                    E.do("tensor", lambda tt, hp=hp: tt.matmul(
                        BK(0, 0, P), qT[0:HD, hp, 0:P], kT[0:HD, hp, 0:P],
                        start=True, stop=True, tile_position=(0, 0)))
                    E.do("tensor", lambda tt, hp=hp: tt.matmul(
                        BK(1, 0, P), qT[HD:2 * HD, hp, 0:P], kT[HD:2 * HD, hp, 0:P],
                        start=True, stop=True, tile_position=(HD, 0)))
                    E.do("tensor", lambda tt, hp=hp: tt.matmul(
                        BK(0, P, 256), qT[0:HD, hp, P:256], kT[0:HD, hp, 0:256],
                        start=True, stop=True, tile_position=(0, 0)))
                    return E.do("tensor", lambda tt, hp=hp: tt.matmul(
                        BK(1, P, 256), qT[HD:2 * HD, hp, P:256],
                        kT[HD:2 * HD, hp, 0:256],
                        start=True, stop=True, tile_position=(HD, 0)), inc="smm")

                s_tok = emit_scores(0)
                pend = [None]

                def emit_tail(ctx):
                    hp, par, pE, pnp, rs, exp_tok = ctx
                    E.wait("vector", "sac", exp_tok)
                    E.do("vector", lambda v, rs=rs: v.reciprocal(rs[:, :], rs[:, :]),
                         inc="sdv", drain=False)
                    E.wait("scalar", "sdv")
                    E.do("vector", lambda v, pE=pE, pnp=pnp, rs=rs: v.tensor_scalar_mul(
                        pnp[:, 0, 0:P], pE[:, 0, 0:P], rs[:, 0:1]))
                    nv = E.do("vector", lambda v, pE=pE, pnp=pnp, rs=rs:
                              v.tensor_scalar_mul(
                                  pnp[:, 0, P:384], pE[:, 0, P:384], rs[:, 2:3]),
                              inc="sdv", drain=False)
                    E.do("scalar", lambda s, pE=pE, pnp=pnp, rs=rs: s.activation(
                        pnp[:, 1, 0:P], pE[:, 1, 0:P], AF.Copy, scale=rs[:, 1:2]),
                        drain=False)
                    na = E.do("scalar", lambda s, pE=pE, pnp=pnp, rs=rs: s.activation(
                        pnp[:, 1, P:384], pE[:, 1, P:384], AF.Copy, scale=rs[:, 3:4]),
                        inc="sac", drain=False)
                    pp_rd[par] = nv

                    E.wait("tensor", "sdv", nv)
                    E.wait("tensor", "sac", na)
                    E.wait("tensor", evtp[par][0], evtp[par][1])
                    pst = ps_tp[par]
                    for i, srcf in enumerate((
                            lambda pnp=pnp: pnp[:, 0, 0:P],
                            lambda pnp=pnp: pnp[:, 1, 0:P],
                            lambda pnp=pnp: pnp[:, 0, P:256],
                            lambda pnp=pnp: pnp[:, 0, 256:384],
                            lambda pnp=pnp: pnp[:, 1, P:256])):
                        E.do("tensor", lambda tt, i=i, srcf=srcf, pst=pst: tt.transpose(
                            pst[:, i * P:(i + 1) * P], srcf(), idb[:, :]))
                    tp = E.do("tensor", lambda tt, pst=pst, pnp=pnp: tt.transpose(
                        pst[:, 5 * P:6 * P], pnp[:, 1, 256:384], idb[:, :]), inc="smm")
                    eng = "vector" if par == 0 else "scalar"
                    sem = "sdv" if par == 0 else "sac"
                    E.wait(eng, "smm", tp)
                    E.wait(eng, "smm", y_rd[par])
                    if eng == "scalar":
                        ev = E.do(eng, lambda e, par=par, pst=pst: e.copy(
                            pTb[par][:, :], pst[:, :]), inc=sem)
                    else:
                        ev = E.do(eng, lambda e, par=par, pst=pst: e.tensor_copy(
                            pTb[par][:, :], pst[:, :]), inc=sem)
                    evtp[par] = (sem, ev)

                    yb = 2 + par
                    E.wait("tensor", evtp[par][0], evtp[par][1])
                    E.wait("tensor", ev_bank[yb][0], ev_bank[yb][1])
                    for j in range(2):
                        h = 2 * hp + j
                        E.do("tensor", lambda tt, h=h, j=j, par=par, yb=yb: tt.matmul(
                            ps_big[j * HD:(j + 1) * HD, yb * 512:yb * 512 + P],
                            v_sb[:, 0, h * HD:(h + 1) * HD],
                            pTb[par][:, j * P:(j + 1) * P],
                            start=True, stop=True, tile_position=(0, j * HD)))
                    y_tok = None
                    for j in range(2):
                        h = 2 * hp + j
                        for kt in range(2):
                            kw = dict(start=(kt == 0), stop=(kt == 1),
                                      tile_position=(0, j * HD))
                            lam = (lambda tt, h=h, j=j, kt=kt, kw=kw, par=par, yb=yb:
                                   tt.matmul(
                                       ps_big[j * HD:(j + 1) * HD,
                                              yb * 512 + P:yb * 512 + 256],
                                       v_sb[:, kt, h * HD:(h + 1) * HD],
                                       pTb[par][:, 256 + (2 * j + kt) * P:
                                                256 + (2 * j + kt + 1) * P], **kw))
                            if j == 1 and kt == 1:
                                y_tok = E.do("tensor", lam, inc="smm")
                            else:
                                E.do("tensor", lam)
                    nonlocal_state["v_y_rd"] = y_tok
                    y_rd[par] = y_tok
                    if par == 1:
                        E.wait("scalar", "smm", y_tok)
                        E.wait("scalar", "smm", y_proj_rd)
                        yev = E.do("scalar", lambda s, hp=hp: s.copy(
                            y_sb[:, hp - 1:hp + 1, :],
                            bass.AP(ps_big, 1024, [[3072, P], [512, 2], [1, 256]])),
                            inc="sac")
                        ev_bank[2] = ev_bank[3] = ("sac", yev)

                nonlocal_state = {"v_y_rd": v_y_rd}
                for hp in range(NCH):
                    par = hp % 2
                    sP = pmP[par]
                    pE = pP[par]
                    pnp = pn[par]
                    nm = nmq[par]
                    rs = rsq[par]
                    E.wait("vector", "smm", s_tok)
                    E.wait("vector", "sac", nm_rd[par])
                    E.wait("vector", "sdv", pp_rd[par])
                    E.do("vector", lambda v, sP=sP: v.tensor_tensor(
                        bass.AP(sP, 0, [[768, P], [384, 2], [1, P]]),
                        bass.AP(ps_big, 0, [[3072, P], [512, 2], [1, P]]),
                        bass.AP(m01, 0, [[P, P], [0, 2], [1, P]]), ALU.add))
                    mk_tok = E.do("vector", lambda v, sP=sP: v.tensor_tensor(
                        bass.AP(sP, P, [[768, P], [384, 2], [1, 256]]),
                        bass.AP(ps_big, P, [[3072, P], [512, 2], [1, 256]]),
                        bass.AP(m01q1, 0, [[256, P], [0, 2], [1, 256]]), ALU.add),
                        inc="sdv", drain=False)
                    ev_bank[0] = ev_bank[1] = ("sdv", mk_tok)

                    if hp + 1 < NCH:
                        s_tok = emit_scores(hp + 1)

                    E.do("vector", lambda v, sP=sP, nm=nm: v.tensor_reduce(
                        nm[:, 0:2], bass.AP(sP, 0, [[768, P], [384, 2], [1, P]]),
                        AX.X, ALU.max, negate=True))
                    nm_tok = E.do("vector", lambda v, sP=sP, nm=nm: v.tensor_reduce(
                        nm[:, 2:4], bass.AP(sP, P, [[768, P], [384, 2], [1, 256]]),
                        AX.X, ALU.max, negate=True), inc="sdv", drain=False)

                    E.wait("scalar", "sdv", nm_tok)
                    E.do("scalar", lambda s, sP=sP, pE=pE, nm=nm, rs=rs: s.activation(
                        pE[:, 0, 0:P], sP[:, 0, 0:P], AF.Exp, bias=nm[:, 0:1],
                        accum_out=rs[:, 0:1]), drain=False)
                    E.do("scalar", lambda s, sP=sP, pE=pE, nm=nm, rs=rs: s.activation(
                        pE[:, 1, 0:P], sP[:, 1, 0:P], AF.Exp, bias=nm[:, 1:2],
                        accum_out=rs[:, 1:2]), drain=False)
                    E.do("scalar", lambda s, sP=sP, pE=pE, nm=nm, rs=rs: s.activation(
                        pE[:, 0, P:384], sP[:, 0, P:384], AF.Exp, bias=nm[:, 2:3],
                        accum_out=rs[:, 2:3]), drain=False)
                    exp_tok = E.do("scalar", lambda s, sP=sP, pE=pE, nm=nm, rs=rs:
                                   s.activation(
                                       pE[:, 1, P:384], sP[:, 1, P:384], AF.Exp,
                                       bias=nm[:, 3:4], accum_out=rs[:, 3:4]),
                                   inc="sac", drain=False)
                    nm_rd[par] = exp_tok

                    if pend[0] is not None:
                        emit_tail(pend[0])
                    pend[0] = (hp, par, pE, pnp, rs, exp_tok)
                emit_tail(pend[0])
                pend[0] = None
                v_y_rd = nonlocal_state["v_y_rd"]

                if g + 1 < NG:
                    t128_rd = emit_ln1(g + 1, 0)
                    t128_rd = emit_ln1(g + 1, 1)

                y_ready = ("sac", E.tok("sac"))
                proj_tok = [0, 0]
                proj_banks = [(4, 5), (2, 3)]
                for t in range(2):
                    ba, bb = proj_banks[t]
                    E.wait("tensor", y_ready[0], y_ready[1])
                    E.wait("tensor", ev_bank[ba][0], ev_bank[ba][1])
                    E.wait("tensor", ev_bank[bb][0], ev_bank[bb][1])
                    for hp in range(NCH):
                        last = hp == NCH - 1
                        kw0 = dict(start=(hp == 0), stop=last)
                        E.do("tensor", lambda tt, hp=hp, t=t, ba=ba, kw=kw0: tt.matmul(
                            BK(ba), y_sb[:, hp, t * P:(t + 1) * P],
                            wproj_sb[:, hp, 0:512], **kw))
                        lam = (lambda tt, hp=hp, t=t, bb=bb, kw=kw0: tt.matmul(
                            BK(bb), y_sb[:, hp, t * P:(t + 1) * P],
                            wproj_sb[:, hp, 512:1024], **kw))
                        if last:
                            proj_tok[t] = E.do("tensor", lam, inc="smm")
                        else:
                            E.do("tensor", lam)
                y_proj_rd = proj_tok[1]

                for t in range(2):
                    ba, bb = proj_banks[t]
                    tl = g * 2 + t
                    E.wait("vector", "smm", proj_tok[t])
                    E.do("vector", lambda v, ba=ba: v.bn_stats(st6[:, 0:6], BK(ba)))
                    E.do("vector", lambda v, bb=bb: v.bn_stats(st6[:, 6:12], BK(bb)))
                    E.do("vector", lambda v: v.bn_aggr(mv2[:, :], st6[:, :]), inc="sdv")
                    E.wait("scalar", "sdv")
                    E.wait("scalar", "sdv", t128_rd)
                    E.do("scalar", lambda s: s.activation(
                        t128a[:, :], mv2[:, 1:2], AF.Ln, bias=EPS_LN))
                    E.do("scalar", lambda s: s.activation(
                        t128a[:, :], t128a[:, :], AF.Exp, scale=-0.5), inc="sac")
                    E.wait("vector", "sac")
                    E.do("vector", lambda v: v.tensor_mul(
                        t128b[:, :], mv2[:, 0:1], t128a[:, :]))
                    E.do("vector", lambda v, pb=pb, t=t: v.tensor_scalar(
                        xp_t[t][:, :], xbuf[pb][t][:, :], 1.0, t128b[:, :],
                        ALU.mult, ALU.subtract))
                    for sem, tk in xm_guard[pb][t]:
                        E.wait("vector", sem, tk)
                    E.do("vector", lambda v, pb=pb, t=t, ba=ba: v.scalar_tensor_tensor(
                        xm_sb[pb][t][:, 0:512], BK(ba), t128a[:, :],
                        xp_t[t][:, 0:512], ALU.mult, ALU.add), inc="sdv")
                    ev_bank[ba] = ("sdv", E.tok("sdv"))
                    stt2 = E.do("vector", lambda v, pb=pb, t=t, bb=bb:
                                v.scalar_tensor_tensor(
                                    xm_sb[pb][t][:, 512:1024], BK(bb), t128a[:, :],
                                    xp_t[t][:, 512:1024], ALU.mult, ALU.add), inc="sdv")
                    ev_bank[bb] = ("sdv", stt2)
                    t128_rd = stt2
                    E.wait("gpsimd", "sdv", stt2)
                    dtok = E.dma("gpsimd", "sgd",
                                 lambda gp, g=g, t=t, pb=pb: gp.dma_start(
                                     xmid_h[g * 256 + t * P:g * 256 + (t + 1) * P, :],
                                     xm_sb[pb][t][:, :]))
                    E.do("vector", lambda v, pb=pb, t=t: v.bn_stats(
                        st6[:, 0:6], xm_sb[pb][t][:, 0:512]))
                    E.do("vector", lambda v, pb=pb, t=t: v.bn_stats(
                        st6[:, 6:12], xm_sb[pb][t][:, 512:1024]))
                    E.do("vector", lambda v, tl=tl: v.bn_aggr(
                        mvst[:, 2 * tl:2 * tl + 2], st6[:, :]))
                    E.do("vector", lambda v, pb=pb, t=t: v.tensor_reduce(
                        tmx[:, :], xm_sb[pb][t][:, :], AX.X, ALU.max))
                    E.do("vector", lambda v, pb=pb, t=t: v.tensor_reduce(
                        tmn[:, :], xm_sb[pb][t][:, :], AX.X, ALU.min))
                    E.do("vector", lambda v, tl=tl: v.tensor_sub(
                        tmx[:, :], tmx[:, :], mvst[:, 2 * tl:2 * tl + 1]))
                    E.do("vector", lambda v, tl=tl: v.tensor_sub(
                        tmn[:, :], mvst[:, 2 * tl:2 * tl + 1], tmn[:, :]))
                    sttok = E.do("vector", lambda v, tl=tl: v.tensor_tensor(
                        gmax[:, tl:tl + 1], tmx[:, :], tmn[:, :], ALU.max), inc="sdv")
                    xm_guard[pb][t] = dtok + [("sdv", sttok)]

            E.wait("scalar", "sdv")
            E.do("scalar", lambda s: s.activation(
                rln[:, :], bass.AP(mvst, 1, [[2 * NT, P], [2, NT]]), AF.Ln,
                bias=EPS_LN))
            rln_tok = E.do("scalar", lambda s: s.activation(
                rln[:, :], rln[:, :], AF.Exp, scale=-0.5), inc="sac")
            E.wait("vector", "sac", rln_tok)
            E.do("vector", lambda v: v.tensor_tensor(
                mrln[:, :], bass.AP(mvst, 0, [[2 * NT, P], [2, NT]]), rln[:, :],
                ALU.mult))
            E.do("vector", lambda v: v.tensor_mul(gmax[:, :], gmax[:, :], rln[:, :]))
            E.do("vector", lambda v: v.tensor_reduce(
                am_acc[:, :], gmax[:, :], AX.X, ALU.max), inc="sdv")

            E.wait_tok("gpsimd", E.queue_tok("sgd"))
            E.wait_tok("sync", E.queue_tok("sx"))
            E.flush_block()

        if phases < 2:
            return nc
        with contextlib.ExitStack() as s2k:
            psb = lambda n, s, d: s2k.enter_context(nc.sbuf_tensor(n, s, d))
            xmb = [psb(f"xmb{i}", [P, C], F32) for i in range(2)]
            ta = psb("ta", [P, 1], F32)

            def emit_collective(cin, cout, s_pp, si_pp):
                t1 = E.dma("gpsimd", "sgd", lambda g: g.dma_start(
                    am_row[0:1, :], am_acc[:, 0:1]))
                E.wait_tok("vector", t1)
                E.do("vector", lambda v: v.tensor_reduce(
                    ccsb[:, :], bass.AP(am_row, 0, [[P, 1], [16, 8], [1, 16]]),
                    AX.X, ALU.max), inc="sdv")
                E.wait("gpsimd", "sdv")
                t2 = E.dma("gpsimd", "sgd", lambda g: g.dma_start(cin[:, :], ccsb[:, :]))
                E.wait_tok("gpsimd", t2)
                cc_tok = E.do("gpsimd", lambda g: g.collective_compute(
                    "AllReduce", ALU.max, replica_groups=[list(range(N_CORES))],
                    ins=[cin.ap()], outs=[cout.ap()]), inc="scc")
                E.wait("gpsimd", "scc", cc_tok)
                t3 = E.dma("gpsimd", "sgd", lambda g: g.dma_start(ccsb[:, :], cout[:, :]))
                E.wait_tok("vector", t3)
                E.do("vector", lambda v: v.tensor_reduce(
                    sc_a[:, :], ccsb[:, :], AX.X, ALU.max))
                E.do("vector", lambda v: v.tensor_scalar_max(sc_a[:, :], sc_a[:, :], 1.0e-8))
                E.do("vector", lambda v: v.reciprocal(sc_b[:, :], sc_a[:, :]))
                E.do("vector", lambda v: v.tensor_scalar_mul(
                    sc_b[:, :], sc_b[:, :], 32767.0))
                E.do("vector", lambda v: v.tensor_scalar_mul(
                    sc_c[:, :], sc_a[:, :], 1.0 / 32767.0), inc="sdv")
                bs1 = E.do("vector", lambda v: v.tensor_scalar(
                    am_row[0:1, :], am_row[0:1, :], 0.0, sc_b[0:1, 0:1],
                    ALU.mult, ALU.add), inc="sdv")
                E.wait("gpsimd", "sdv", bs1)
                t4 = E.dma("gpsimd", "sgd", lambda g: g.dma_start(
                    s_pp[:, 0:1], am_row[0:1, 0:P]))
                E.wait_tok("vector", t4)
                bs2 = E.do("vector", lambda v: v.tensor_scalar(
                    am_row[0:1, :], am_row[0:1, :], 0.0, sc_c[0:1, 0:1],
                    ALU.mult, ALU.add), inc="sdv")
                E.wait("gpsimd", "sdv", bs2)
                t5 = E.dma("gpsimd", "sgd", lambda g: g.dma_start(
                    si_pp[:, 0:1], am_row[0:1, 0:P]))
                E.wait_tok("gpsimd", t5)
                E.do("gpsimd", lambda g: g.memset(am_row[0:1, 0:1], 0.0), inc="sgp")
                E.flush_block()

            emit_collective(cc1_in, cc1_out, s1_pp, s1i_pp)
            if phases < 3:
                return nc

            x2qT = psb("x2qT", [P, NCH, T], BF16)
            u2 = [psb(f"u2_{i}", [P, C], BF16) for i in range(2)]
            u_sb = psb("u_sb", [P, C], BF16)
            tq = [psb(f"tq{i}", [P, C], F32) for i in range(2)]
            rs_ = psb("rs_", [P, NT], F32)
            mrs = psb("mrs", [P, NT], F32)
            ps_h = [s2k.enter_context(nc.psum_tensor(f"ps_h{i}", [P, 512], F32))
                    for i in range(2)]
            x2stack = contextlib.ExitStack()
            ps_t2 = [x2stack.enter_context(nc.psum_tensor(f"ps_t2{i}", [P, 512], BF16))
                     for i in range(2)]

            xm_tok = [None, None]
            xm_tok[0] = E.dma("sync", "sx", lambda s: s.dma_start(
                xmb[0][:, :], xmid_h[0:P, :]))
            xm_tok[1] = E.dma("sync", "sx", lambda s: s.dma_start(
                xmb[1][:, :], xmid_h[P:2 * P, :]))
            ev_t2 = [("sdv", 0), ("sac", 0)]
            u_free = [[], []]
            E.wait("vector", "sgp")
            E.do("vector", lambda v: v.tensor_scalar_mul(
                rs_[:, :], rln[:, :], s1_pp[:, 0:1]))
            E.do("vector", lambda v: v.tensor_scalar(
                mrs[:, :], mrln[:, :], s1_pp[:, 0:1], -1.0, ALU.mult, ALU.mult),
                inc="sdv")
            tq_free = [0, 0]
            for tl in range(NT):
                b = tl % 2
                E.wait_tok("scalar", xm_tok[b])
                E.wait("scalar", "sdv")
                E.wait("scalar", "sdv", tq_free[b])
                tq_tok = E.do("scalar", lambda s, b=b, tl=tl: s.activation(
                    tq[b][:, :], xmb[b][:, :], AF.Identity, scale=rs_[:, tl:tl + 1],
                    bias=mrs[:, tl:tl + 1]), inc="sac")
                for sem, tk in u_free[b]:
                    E.wait("vector", sem, tk)
                E.wait("vector", "sac", tq_tok)
                ld_tok = E.do("vector", lambda v, b=b: v.tensor_scalar(
                    u2[b][:, :], tq[b][:, :], MR, MR, ALU.add, ALU.subtract),
                    inc="sdv")
                tq_free[b] = ld_tok
                if tl + 2 < NT:
                    E.wait("sync", "sdv", ld_tok)
                    xm_tok[b] = E.dma("sync", "sx", lambda s, tl=tl, b=b: s.dma_start(
                        xmb[b][:, :], xmid_h[(tl + 2) * P:(tl + 3) * P, :]))
                evs = []
                for half in range(2):
                    bank = half
                    E.wait("tensor", "sdv", ld_tok)
                    E.wait("tensor", ev_t2[bank][0], ev_t2[bank][1])
                    for i in range(3):
                        cc = half * 4 + i
                        E.do("tensor", lambda tt, bank=bank, i=i, cc=cc, b=b:
                             tt.transpose(
                                 ps_t2[bank][:, i * P:(i + 1) * P],
                                 u2[b][:, cc * P:(cc + 1) * P], idb[:, :]))
                    tp = E.do("tensor", lambda tt, bank=bank, half=half, b=b:
                              tt.transpose(
                                  ps_t2[bank][:, 3 * P:4 * P],
                                  u2[b][:, (half * 4 + 3) * P:(half * 4 + 4) * P],
                                  idb[:, :]), inc="smm")
                    eng = "scalar"
                    sem = "sac"
                    E.wait(eng, "smm", tp)

                    def _evac(e, half=half, tl=tl, bank=bank):
                        dst = bass.AP(x2qT, (half * 4) * T + tl * P,
                                      [[NCH * T, P], [T, 4], [1, P]])
                        if hasattr(e, "tensor_copy"):
                            return e.tensor_copy(dst, ps_t2[bank][:, :])
                        return e.copy(dst, ps_t2[bank][:, :])
                    ev = E.do(eng, _evac, inc=sem)
                    ev_t2[bank] = (sem, ev)
                    evs.append((sem, ev))
                u_free[b] = evs
            E.wait_tok("sync", E.queue_tok("sx"))
            E.flush_block()
            x2stack.close()

            ps_w = [s2k.enter_context(nc.psum_tensor(f"ps_w{i}", [P, 512], F32))
                    for i in range(4)]
            winst = [psb(f"winst{i}", [P, NCH, P], F32) for i in range(2)]
            wtq = psb("wtq", [P, C], F32)
            wq_fc = [psb(f"wqfc{i}", [P, NCH, P], BF16) for i in range(2)]
            h_sb = [psb(f"hsb{i}", [P, 512], BF16) for i in range(3)]
            aw = psb("aw", [P, 1], F32)
            swv = psb("swv", [P, 1], F32)
            schf = psb("schf", [P, 1], F32)
            ps_wb = s2k.enter_context(nc.psum_tensor("ps_wb", [P, 512], BF16))

            E.do("vector", lambda v: v.memset(am_acc[:, :], 0.0))

            def win_slice_ap(fc):
                return bass.AP(win_h, fc * P, [[FF, P], [P * FF, NCH], [1, P]])

            win_tok = [None, None]
            win_rd = [0, 0]
            wt_tp = [0, 0]
            wq_rd = [0, 0]
            ev_w = [("sdv", 0)] * 2
            ev_wb = ("sdv", 0)
            cast_prev = 0
            gelu_prev = 0
            ev_h = [0, 0]
            h_dma = [[], [], []]

            def emit_wt_trans(fc):
                b = fc % 2
                toks = []
                for half in range(2):
                    bank = b * 2 + half
                    E.wait_tok("tensor", win_tok[b])
                    E.wait("tensor", ev_w[b][0], ev_w[b][1])
                    for i in range(3):
                        cc = half * 4 + i
                        E.do("tensor", lambda tt, bank=bank, i=i, cc=cc, b=b:
                             tt.transpose(ps_w[bank][:, i * P:(i + 1) * P],
                                          winst[b][:, cc, :], idf[:, :]))
                    tp = E.do("tensor", lambda tt, bank=bank, b=b, half=half:
                              tt.transpose(ps_w[bank][:, 3 * P:4 * P],
                                           winst[b][:, half * 4 + 3, :], idf[:, :]),
                              inc="smm")
                    toks.append(tp)
                    if half == 1:
                        win_rd[b] = tp
                wt_tp[b] = toks[1]

            win_tok[0] = E.dma("sync", "sx", lambda s: s.dma_start(
                winst[0][:, :, :], win_slice_ap(0)))
            win_tok[1] = E.dma("sync", "sx", lambda s: s.dma_start(
                winst[1][:, :, :], win_slice_ap(1)))
            emit_wt_trans(0)

            for fc in range(NF):
                b = fc % 2
                if fc + 2 < NF:
                    nb = fc % 2
                    E.wait("sync", "smm", win_rd[nb])
                    win_tok[nb] = E.dma("sync", "sx", lambda s, fc=fc, nb=nb:
                                        s.dma_start(winst[nb][:, :, :],
                                                    win_slice_ap(fc + 2)))
                if fc + 1 < NF:
                    emit_wt_trans(fc + 1)
                E.wait("vector", "smm", wt_tp[b])
                E.wait("vector", "sac", gelu_prev)
                E.do("vector", lambda v, b=b: v.tensor_reduce(
                    aw[:, :], ps_w[2 * b][:, :], AX.X, ALU.max,
                    apply_absolute_value=True), drain=False)
                E.do("vector", lambda v, b=b: v.tensor_reduce(
                    swv[:, :], ps_w[2 * b + 1][:, :], AX.X, ALU.max,
                    apply_absolute_value=True), drain=False)
                E.do("vector", lambda v: v.tensor_max(aw[:, :], aw[:, :], swv[:, :]))
                E.do("vector", lambda v: v.tensor_scalar_max(aw[:, :], aw[:, :], 1.0e-8))
                E.do("vector", lambda v: v.reciprocal(swv[:, :], aw[:, :]))
                E.do("vector", lambda v: v.tensor_scalar_mul(swv[:, :], swv[:, :], 127.0))
                E.do("vector", lambda v: v.tensor_scalar_mul(
                    schf[:, :], aw[:, :], 1.0 / 127.0), drain=False)
                E.wait("vector", "sgp")
                sch_tok = E.do("vector", lambda v: v.tensor_mul(
                    schf[:, :], schf[:, :], s1i_pp[:, :]), inc="sdv")
                E.wait("vector", "sac", cast_prev)
                E.do("vector", lambda v, b=b: v.tensor_scalar(
                    wtq[:, 0:512], ps_w[2 * b][:, :], swv[:, 0:1], None, ALU.mult))
                E.do("vector", lambda v, b=b: v.tensor_scalar(
                    wtq[:, 512:1024], ps_w[2 * b + 1][:, :], swv[:, 0:1], None,
                    ALU.mult), drain=False)
                uq_tok = E.do("vector", lambda v: v.tensor_scalar(
                    wtq[:, :], wtq[:, :], MR, MR, ALU.add, ALU.subtract), inc="sdv")
                ev_w[b] = ("sdv", uq_tok)
                E.wait("scalar", "sdv", uq_tok)
                E.wait("scalar", ev_wb[0], ev_wb[1])
                cast_prev = E.do("scalar", lambda s: s.copy(u_sb[:, :], wtq[:, :]),
                                 inc="sac")
                for half in range(2):
                    E.wait("tensor", "sac", cast_prev)
                    E.wait("tensor", ev_wb[0], ev_wb[1])
                    E.wait("tensor", "smm", wq_rd[b])
                    for i in range(3):
                        cc = half * 4 + i
                        E.do("tensor", lambda tt, i=i, cc=cc: tt.transpose(
                            ps_wb[:, i * P:(i + 1) * P], u_sb[:, cc * P:(cc + 1) * P],
                            idb[:, :]))
                    tp = E.do("tensor", lambda tt, half=half: tt.transpose(
                        ps_wb[:, 3 * P:4 * P],
                        u_sb[:, (half * 4 + 3) * P:(half * 4 + 4) * P], idb[:, :]),
                        inc="smm")
                    E.wait("vector", "smm", tp)
                    ev = E.do("vector", lambda v, half=half, b=b: v.tensor_copy(
                        bass.AP(wq_fc[b], (half * 4) * P, [[NCH * P, P], [P, 4], [1, P]]),
                        ps_wb[:, :]), inc="sdv")
                    ev_wb = ("sdv", ev)
                wq_ready = E.tok("sdv")

                for tb_ in range(8):
                    hb = (fc * 8 + tb_) % 3
                    bank = tb_ % 2
                    E.wait("tensor", "sdv", wq_ready)
                    E.wait("tensor", "sac", ev_h[bank])
                    mm_tok = None
                    for cc in range(NCH):
                        kw = dict(start=(cc == 0), stop=(cc == NCH - 1))
                        lam = (lambda tt, b=b, cc=cc, tb_=tb_, bank=bank, kw=kw:
                               tt.matmul(ps_h[bank][:, :], wq_fc[b][:, cc, :],
                                         x2qT[:, cc, tb_ * 512:(tb_ + 1) * 512], **kw))
                        if cc == NCH - 1:
                            mm_tok = E.do("tensor", lam, inc="smm")
                        else:
                            E.do("tensor", lam)
                    if tb_ == 7:
                        wq_rd[b] = mm_tok
                    E.wait("scalar", "smm", mm_tok)
                    E.wait("scalar", "sdv", sch_tok)
                    E.wait_tok("scalar", h_dma[hb])
                    ge_tok = E.do("scalar", lambda s, hb=hb, fc=fc, bank=bank: s.activation(
                        h_sb[hb][:, :], ps_h[bank][:, :], AF.Gelu,
                        bias=binp[:, fc:fc + 1], scale=schf[:, 0:1]), inc="sac")
                    ev_h[bank] = ge_tok
                    gelu_prev = ge_tok
                    E.wait("vector", "sac", ge_tok)
                    E.do("vector", lambda v, hb=hb: v.tensor_reduce(
                        ta[:, :], h_sb[hb][:, :], AX.X, ALU.max,
                        apply_absolute_value=True))
                    am_tok = E.do("vector", lambda v: v.tensor_max(
                        am_acc[:, :], am_acc[:, :], ta[:, :]), inc="sdv")
                    E.wait("gpsimd", "sdv", am_tok)
                    h_dma[hb] = E.dma("gpsimd", "sgd",
                                      lambda g, fc=fc, tb_=tb_, hb=hb: g.dma_start(
                                          bass.AP(hscr_h,
                                                  2 * tb_ * P * NF * 256 + fc * 256,
                                                  [[NF * 256, P], [P * NF * 256, 2],
                                                   [1, 256]]),
                                          bass.AP(h_sb[hb], 0,
                                                  [[512, P], [256, 2], [1, 256]])))

            E.wait_tok("gpsimd", E.queue_tok("sgd"))
            E.wait_tok("sync", E.queue_tok("sx"))
            E.flush_block()

            emit_collective(cc2_in, cc2_out, s2_pp, s2i_pp)

        if phases < 4:
            return nc
        with contextlib.ExitStack() as s3k:
            psb = lambda n, s, d: s3k.enter_context(nc.sbuf_tensor(n, s, d))
            wqout = psb("wqout", [P, NF, C], BF16)
            wst = [psb(f"wst{i}", [P, C], F32) for i in range(2)]
            hin = [psb(f"hin{i}", [P, NF, 256], BF16) for i in range(2)]
            hq = [psb(f"hq{i}", [P, NF, 256], BF16) for i in range(2)]
            thq = psb("thq", [P, NF * 256], F32)
            xmb2 = [psb(f"xmb2{i}", [P, C], F32) for i in range(2)]
            ffn = psb("ffn", [P, C], F32)
            outb = [psb(f"outb{i}", [P, C], F32) for i in range(2)]
            st6 = psb("st6c", [P, 12], F32)
            mv2 = psb("mv2c", [P, 2], F32)
            ta = psb("tac", [P, 1], F32)
            tb = psb("tbc", [P, 1], F32)
            aw = psb("awc", [P, 1], F32)
            swo = psb("swoc", [P, 1], F32)
            swoi = psb("swoic", [P, 1], F32)
            ps_f = [s3k.enter_context(nc.psum_tensor(f"ps_f{i}", [P, C], F32))
                    for i in range(2)]

            wst_tok = [None, None]
            wst_tok[0] = E.dma("sync", "sx", lambda s: s.dma_start(
                wst[0][:, :], wout_h[0:P, :]))
            used_tok = [0, 0]
            cast_tok = 0
            for fc in range(NF):
                b = fc % 2
                if fc + 1 < NF:
                    nb = (fc + 1) % 2
                    E.wait("sync", "sdv", used_tok[nb])
                    wst_tok[nb] = E.dma("sync", "sx", lambda s, fc=fc, nb=nb: s.dma_start(
                        wst[nb][:, :], wout_h[(fc + 1) * P:(fc + 2) * P, :]))
                E.wait_tok("vector", wst_tok[b])
                E.wait("vector", "sac", cast_tok)
                E.do("vector", lambda v, b=b: v.tensor_reduce(
                    aw[:, :], wst[b][:, :], AX.X, ALU.max, apply_absolute_value=True))
                E.do("vector", lambda v: v.tensor_scalar_max(aw[:, :], aw[:, :], 1.0e-8))
                E.do("vector", lambda v: v.reciprocal(swo[:, :], aw[:, :]))
                E.do("vector", lambda v: v.tensor_scalar_mul(swo[:, :], swo[:, :], 127.0))
                E.do("vector", lambda v: v.tensor_scalar_mul(
                    swoi[:, :], aw[:, :], 1.0 / 127.0))
                E.do("vector", lambda v, b=b: v.tensor_scalar(
                    thq[:, 0:C], wst[b][:, :], swo[:, 0:1], None, ALU.mult))
                used_tok[b] = E.do("vector", lambda v: v.tensor_scalar(
                    thq[:, 0:C], thq[:, 0:C], MR, MR, ALU.add, ALU.subtract), inc="sdv")
                E.wait("scalar", "sdv", used_tok[b])
                cast_tok = E.do("scalar", lambda s, fc=fc: s.activation(
                    wqout[:, fc, :], thq[:, 0:C], AF.Copy, scale=swoi[:, 0:1]), inc="sac")
            wq_done = cast_tok
            E.flush_block()

            NB = 16
            hin_tok = [None, None]
            hin_tok[0] = E.dma("sync", "sx", lambda s: s.dma_start(
                bass.AP(hin[0], 0, [[NF * 256, P], [1, NF * 256]]),
                hscr_h[0, :, :]))
            xm2_tok = [None, None]
            xm2_tok[0] = E.dma("sync", "sx", lambda s: s.dma_start(
                xmb2[0][:, :], xmid_h[0:P, :]))
            hin_used = [0, 0]
            hq_rd = [0, 0]
            tq_prev = 0
            ev_f = [0, 0]
            out_dma = [[], []]
            xm_used = [0, 0]

            def emit_quant(blk):
                nonlocal tq_prev
                b = blk % 2
                E.wait_tok("scalar", hin_tok[b])
                E.wait("scalar", "sgp")
                E.wait("scalar", "sdv", tq_prev)
                hin_used[b] = E.do("scalar", lambda s, b=b: s.activation(
                    thq[:, :], bass.AP(hin[b], 0, [[NF * 256, P], [1, NF * 256]]),
                    AF.Copy, scale=s2_pp[:, 0:1]), inc="sac")
                E.wait("vector", "sac", hin_used[b])
                E.wait("vector", "smm", hq_rd[b])
                tq_prev = E.do("vector", lambda v, b=b: v.tensor_scalar(
                    bass.AP(hq[b], 0, [[NF * 256, P], [1, NF * 256]]),
                    thq[:, :], MR, MR, ALU.add, ALU.subtract), inc="sdv")
                return tq_prev

            hq_toks = [0, 0]
            hq_toks[0] = emit_quant(0)
            for blk in range(NB):
                b = blk % 2
                nb = (blk + 1) % 2
                if blk + 1 < NB:
                    E.wait("sync", "sac", hin_used[nb])
                    hin_tok[nb] = E.dma("sync", "sx",
                                        lambda s, blk=blk, nb=nb: s.dma_start(
                                            bass.AP(hin[nb], 0,
                                                    [[NF * 256, P], [1, NF * 256]]),
                                            hscr_h[blk + 1, :, :]))
                    E.wait("sync", "sdv", xm_used[nb])
                    xm2_tok[nb] = E.dma("sync", "sx", lambda s, blk=blk, nb=nb: s.dma_start(
                        xmb2[nb][:, :],
                        xmid_h[(blk + 1) * 2 * P:((blk + 1) * 2 + 1) * P, :]))
                    hq_toks[nb] = emit_quant(blk + 1)
                hq_tok = hq_toks[b]

                for sub in range(2):
                    tl = blk * 2 + sub
                    bank = sub
                    E.wait("tensor", "sdv", hq_tok)
                    E.wait("tensor", "sac", wq_done)
                    E.wait("tensor", "sdv", ev_f[bank])
                    mm_tok = None
                    for fc in range(NF):
                        for pc in range(2):
                            kw = dict(start=(fc == 0), stop=(fc == NF - 1))
                            lam = (lambda tt, fc=fc, sub=sub, pc=pc, bank=bank, kw=kw,
                                   b=b: tt.matmul(
                                       ps_f[bank][:, pc * 512:(pc + 1) * 512],
                                       hq[b][:, fc, sub * P:(sub + 1) * P],
                                       wqout[:, fc, pc * 512:(pc + 1) * 512], **kw))
                            if fc == NF - 1 and pc == 1:
                                mm_tok = E.do("tensor", lam, inc="smm")
                            else:
                                E.do("tensor", lam)
                    if sub == 1:
                        hq_rd[b] = mm_tok
                    E.wait("vector", "smm", mm_tok)
                    ev_f[bank] = E.do("vector", lambda v, bank=bank: v.scalar_tensor_tensor(
                        ffn[:, :], ps_f[bank][:, :], s2i_pp[:, 0:1], bout_t[:, :],
                        ALU.mult, ALU.add), inc="sdv")
                    E.do("vector", lambda v: v.bn_stats(st6[:, 0:6], ffn[:, 0:512]))
                    E.do("vector", lambda v: v.bn_stats(st6[:, 6:12], ffn[:, 512:1024]))
                    E.do("vector", lambda v: v.bn_aggr(mv2[:, :], st6[:, :]), inc="sdv")
                    E.wait("scalar", "sdv")
                    E.do("scalar", lambda s: s.activation(
                        ta[:, :], mv2[:, 1:2], AF.Ln, bias=EPS_LN))
                    E.do("scalar", lambda s: s.activation(
                        ta[:, :], ta[:, :], AF.Exp, scale=-0.5), inc="sac")
                    E.wait("vector", "sac")
                    E.do("vector", lambda v: v.tensor_mul(tb[:, :], mv2[:, 0:1], ta[:, :]))
                    E.wait_tok("vector", xm2_tok[b])
                    E.wait_tok("vector", out_dma[b])
                    E.do("vector", lambda v, b=b: v.tensor_scalar(
                        outb[b][:, :], xmb2[b][:, :], 1.0, tb[:, :],
                        ALU.mult, ALU.subtract))
                    out_tok = E.do("vector", lambda v, b=b: v.scalar_tensor_tensor(
                        outb[b][:, :], ffn[:, :], ta[:, 0:1], outb[b][:, :],
                        ALU.mult, ALU.add), inc="sdv")
                    if sub == 1:
                        xm_used[b] = out_tok
                    E.wait("gpsimd", "sdv", out_tok)
                    out_dma[b] = E.dma("gpsimd", "sgd", lambda g, tl=tl, b=b: g.dma_start(
                        out_h[tl * P:(tl + 1) * P, :], outb[b][:, :]))
                    if sub == 0:
                        E.wait("sync", "sdv", out_tok)
                        xm2_tok[b] = E.dma("sync", "sx", lambda s, blk=blk, b=b:
                                           s.dma_start(
                                               xmb2[b][:, :],
                                               xmid_h[(blk * 2 + 1) * P:
                                                      (blk * 2 + 2) * P, :]))

            E.wait_tok("sync", E.queue_tok("sx"))
            E.wait_tok("gpsimd", E.queue_tok("sgd"))
            E.flush_block()

    return nc


_NC_CACHE = {}


def kernel(L0_attn_x, L0_attn_w_qkv, L0_attn_ln_w, L0_attn_w_proj,
           L0_ffn_W_in, L0_ffn_b_in, L0_ffn_W_out, L0_ffn_b_out):
    x = np.ascontiguousarray(np.asarray(L0_attn_x))
    wqkv = np.ascontiguousarray(np.asarray(L0_attn_w_qkv))
    lnw = np.ascontiguousarray(np.asarray(L0_attn_ln_w)).reshape(1, C)
    wproj = np.ascontiguousarray(np.asarray(L0_attn_w_proj))
    win = np.ascontiguousarray(np.asarray(L0_ffn_W_in))
    bin_ = np.ascontiguousarray(np.asarray(L0_ffn_b_in)).reshape(1, FF)
    wout = np.ascontiguousarray(np.asarray(L0_ffn_W_out))
    bout = np.ascontiguousarray(np.asarray(L0_ffn_b_out)).reshape(1, C)

    if "nc" not in _NC_CACHE:
        _NC_CACHE["nc"] = build_nc()
    nc = _NC_CACHE["nc"]

    B = x.shape[0]
    per = B // N_CORES
    in_maps = []
    for c in range(N_CORES):
        shard = np.ascontiguousarray(x[c * per:(c + 1) * per].reshape(T, C))
        in_maps.append({
            "x": shard, "wqkv": wqkv, "lnw": lnw, "wproj": wproj,
            "win": win, "bin": bin_, "wout": wout, "bout": bout,
        })
    res = run_bass_kernel_spmd(nc, in_maps, core_ids=list(range(N_CORES)))
    out = np.concatenate(
        [np.asarray(r["out"]).reshape(per, 2048, C) for r in res.results], axis=0)
    return out


# version 15
# speedup vs baseline: 1.0712x; 1.0007x over previous
import contextlib
import numpy as np

import concourse.bass as bass
import concourse.mybir as mybir
from concourse.bass_utils import run_bass_kernel_spmd

F32 = mybir.dt.float32
BF16 = mybir.dt.bfloat16
AF = mybir.ActivationFunctionType
ALU = mybir.AluOpType
AX = mybir.AxisListType

P = 128
C = 1024
NCH = 8
T = 4096
NT = 32
NG = 16
H = 16
HD = 64
FF = 4096
NF = 32
MR = 12582912.0
NEGBIG = -1.0e30
EPS_LN = 1.0e-5
EPS_RMS = 1.0e-6
N_CORES = 8


class Emitter:

    ENGINES = ("sync", "tensor", "vector", "scalar", "gpsimd")

    def __init__(self, nc, stack):
        self.nc = nc
        self.q = {e: [] for e in self.ENGINES}
        self.sems = {}
        self.cnt = {}
        self.stack = stack
        self.dmaq = {}
        self.dmaidx = {}

    def sem(self, name):
        self.sems[name] = self.stack.enter_context(self.nc.semaphore(name))
        self.cnt[name] = 0

    def inc(self, name, n=1):
        self.cnt[name] += n
        return (self.sems[name], n)

    def tok(self, name):
        return self.cnt[name]

    def wait(self, eng, name, value=None):
        v = self.cnt[name] if value is None else value
        assert v <= self.cnt[name], (name, v, self.cnt[name])
        if v <= 0:
            return
        h = self.sems[name]
        self.q[eng].append(lambda e, h=h, v=v: e.wait_ge(h, v))

    DRAIN_ENGINES = ("vector", "scalar", "gpsimd")

    def do(self, eng, build, inc=None, n=1, drain=None):
        if drain is None:
            drain = eng in self.DRAIN_ENGINES
        if drain:
            self.q[eng].append(lambda e: e.drain())
        tgt = self.inc(inc, n) if inc else None

        def fn(e, build=build, tgt=tgt):
            ins = build(e)
            if tgt is not None:
                ins.then_inc(*tgt)

        self.q[eng].append(fn)
        return self.cnt[inc] if inc else None

    def dma_queue(self, qname, nlanes=4):
        lanes = []
        for i in range(nlanes):
            nm = f"{qname}{i}"
            self.sem(nm)
            lanes.append(nm)
        self.dmaq[qname] = lanes
        self.dmaidx[qname] = 0

    def dma(self, eng, qname, build):
        lanes = self.dmaq[qname]
        lane = lanes[self.dmaidx[qname] % len(lanes)]
        self.dmaidx[qname] += 1
        self.wait(eng, lane)
        v = self.do(eng, build, inc=lane, n=16)
        return [(lane, v)]

    def queue_tok(self, qname):
        return [(lane, self.cnt[lane]) for lane in self.dmaq[qname]]

    def wait_tok(self, eng, tok):
        for nm, v in tok:
            self.wait(eng, nm, v)

    def flush_block(self):
        qs = {e: fns for e, fns in self.q.items() if fns}
        with self.nc.Block() as blk:
            for e, fns in qs.items():
                def runner(engine, fns=fns):
                    for f in fns:
                        f(engine)
                getattr(blk, e)(runner)
        self.q = {e: [] for e in self.ENGINES}


def build_nc(phases=4):
    nc = bass.Bass("TRN2", target_bir_lowering=False)

    x_h = nc.dram_tensor("x", [T, C], F32, kind="ExternalInput")
    wqkv_h = nc.dram_tensor("wqkv", [C, 3 * C], F32, kind="ExternalInput")
    lnw_h = nc.dram_tensor("lnw", [1, C], F32, kind="ExternalInput")
    wproj_h = nc.dram_tensor("wproj", [C, C], F32, kind="ExternalInput")
    win_h = nc.dram_tensor("win", [C, FF], F32, kind="ExternalInput")
    bin_h = nc.dram_tensor("bin", [1, FF], F32, kind="ExternalInput")
    wout_h = nc.dram_tensor("wout", [FF, C], F32, kind="ExternalInput")
    bout_h = nc.dram_tensor("bout", [1, C], F32, kind="ExternalInput")
    out_h = nc.dram_tensor("out", [T, C], F32, kind="ExternalOutput")

    xmid_h = nc.dram_tensor("xmid_scr", [T, C], F32)
    hscr_h = nc.dram_tensor("h_scr", [16, P, NF * 256], BF16)
    cc1_in = nc.dram_tensor("cc1_in", [1, 8], F32)
    cc1_out = nc.dram_tensor("cc1_out", [1, 8], F32)
    cc2_in = nc.dram_tensor("cc2_in", [1, 8], F32)
    cc2_out = nc.dram_tensor("cc2_out", [1, 8], F32)

    with contextlib.ExitStack() as gstack:
        E = Emitter(nc, gstack)
        for s in ("smm", "sdv", "sac", "sgp", "scc"):
            E.sem(s)
        E.dma_queue("sx", 6)
        E.dma_queue("sa", 3)
        E.dma_queue("sgd", 4)

        sbg = lambda name, shape, dt: gstack.enter_context(nc.sbuf_tensor(name, shape, dt))

        lnwq_t = sbg("lnwq_t", [P, C], F32)
        lnwk_t = sbg("lnwk_t", [P, C], F32)
        bout_t = sbg("bout_t", [P, C], F32)
        binp = sbg("binp", [P, NF], F32)
        idb = sbg("idb", [P, P], BF16)
        idf = sbg("idf", [P, P], F32)
        m01 = sbg("m01", [P, P], F32)
        m01q1 = sbg("m01q1", [P, 256], F32)
        c127 = sbg("c127", [P, 1], F32)
        c32k = sbg("c32k", [1, 1], F32)
        ceps1 = sbg("ceps1", [P, 1], F32)
        ceps2 = sbg("ceps2", [P, 1], F32)
        cm05 = sbg("cm05", [P, 1], F32)
        nc.const_aps.register(F32, EPS_LN, ceps1[:, :])
        nc.const_aps.register(F32, EPS_RMS, ceps2[:, :])
        nc.const_aps.register(F32, -0.5, cm05[:, :])
        s1_pp = sbg("s1_pp", [P, 1], F32)
        s1i_pp = sbg("s1i_pp", [P, 1], F32)
        s2_pp = sbg("s2_pp", [P, 1], F32)
        s2i_pp = sbg("s2i_pp", [P, 1], F32)
        am_acc = sbg("am_acc", [P, 1], F32)
        am_row = sbg("am_row", [1, P], F32)
        ccsb = sbg("ccsb", [1, 8], F32)
        sc_a = sbg("sc_a", [1, 1], F32)
        sc_b = sbg("sc_b", [1, 1], F32)
        sc_c = sbg("sc_c", [1, 1], F32)
        rln = sbg("rln", [P, NT], F32)
        mrln = sbg("mrln", [P, NT], F32)
        mvst = sbg("mvst", [P, 2 * NT], F32)
        gmax = sbg("gmax", [P, NT], F32)

        with contextlib.ExitStack() as pstack:
            psb = lambda n, s, d: pstack.enter_context(nc.sbuf_tensor(n, s, d))
            onesf = psb("onesf", [P, P], F32)
            zerof = psb("zerof", [P, P], F32)
            bstage = psb("bstage", [NF, P], F32)
            bps = pstack.enter_context(nc.psum_tensor("bps", [P, NF], F32))

            tok_ld = []
            tok_ld += E.dma("sync", "sx", lambda s: s.dma_start(
                lnwq_t[:, :], bass.AP(lnw_h, 0, [[0, P], [1, C]])))
            tok_ld += E.dma("sync", "sx", lambda s: s.dma_start(
                lnwk_t[:, :], bass.AP(lnw_h, 0, [[0, P], [1, C]])))
            tok_ld += E.dma("sync", "sx", lambda s: s.dma_start(
                bout_t[:, :], bass.AP(bout_h, 0, [[0, P], [1, C]])))
            tok_ld += E.dma("sync", "sx", lambda s: s.dma_start(
                bstage[:, :], bass.AP(bin_h, 0, [[P, NF], [1, P]])))

            E.do("gpsimd", lambda g: g.memset(onesf[:, :], 1.0), inc="sgp")
            ms_tok = E.do("gpsimd", lambda g: g.memset(zerof[:, :], 0.0), inc="sgp")
            E.wait("gpsimd", "sgp", ms_tok)
            E.do("gpsimd", lambda g: g.affine_select(
                idf[:, :], onesf[:, :], [[1, P]], ALU.is_equal, 0.0,
                channel_multiplier=-1), inc="sgp")
            E.do("gpsimd", lambda g: g.affine_select(
                m01[:, :], zerof[:, :], [[-1, P]], ALU.is_ge, NEGBIG,
                channel_multiplier=1), inc="sgp")
            E.do("gpsimd", lambda g: g.memset(m01q1[:, 0:P], 0.0), inc="sgp")
            E.do("gpsimd", lambda g: g.affine_select(
                m01q1[:, P:256], zerof[:, :], [[-1, P]], ALU.is_ge, NEGBIG,
                channel_multiplier=1), inc="sgp")
            E.do("gpsimd", lambda g: g.memset(c127[:, :], 127.0), inc="sgp")
            E.do("gpsimd", lambda g: g.memset(ceps1[:, :], EPS_LN), inc="sgp")
            E.do("gpsimd", lambda g: g.memset(ceps2[:, :], EPS_RMS), inc="sgp")
            E.do("gpsimd", lambda g: g.memset(cm05[:, :], -0.5), inc="sgp")
            E.do("gpsimd", lambda g: g.memset(c32k[:, :], 32767.0), inc="sgp")
            tok_gp = E.tok("sgp")

            E.wait("vector", "sgp", tok_gp)
            E.wait_tok("vector", tok_ld)
            E.do("vector", lambda v: v.tensor_copy(idb[:, :], idf[:, :]))
            E.do("vector", lambda v: v.tensor_scalar_mul(
                lnwq_t[:, :], lnwq_t[:, :], 0.125), inc="sdv")

            E.wait("tensor", "sgp", tok_gp)
            E.wait_tok("tensor", tok_ld)
            E.do("tensor", lambda t: t.transpose(
                bps[:, :], bstage[:, :], idf[0:NF, 0:NF]), inc="smm")
            E.wait("vector", "smm")
            E.do("vector", lambda v: v.tensor_copy(binp[:, :], bps[:, :]), inc="sdv")
            E.flush_block()

        with contextlib.ExitStack() as s1k:
            psb = lambda n, s, d: s1k.enter_context(nc.sbuf_tensor(n, s, d))
            pps = lambda n, s, d: s1k.enter_context(nc.psum_tensor(n, s, d))

            wqkv_sb = psb("wqkv_sb", [P, NCH, 3 * C], BF16)
            wproj_sb = psb("wproj_sb", [P, NCH, C], BF16)
            wstage = [psb(f"wstage{i}", [P, 3 * C], F32) for i in range(2)]

            cast_tok = [0, 0]
            for cc in range(NCH):
                b = cc % 2
                E.wait("sync", "sdv", cast_tok[b])
                wt = E.dma("sync", "sx", lambda s, cc=cc, b=b: s.dma_start(
                    wstage[b][:, :], wqkv_h[cc * P:(cc + 1) * P, :]))
                E.wait_tok("vector", wt)
                cast_tok[b] = E.do("vector", lambda v, cc=cc, b=b: v.tensor_copy(
                    wqkv_sb[:, cc, :], wstage[b][:, :]), inc="sdv")
            for cc in range(NCH):
                b = cc % 2
                E.wait("sync", "sdv", cast_tok[b])
                wt = E.dma("sync", "sx", lambda s, cc=cc, b=b: s.dma_start(
                    wstage[b][:, 0:C], wproj_h[cc * P:(cc + 1) * P, :]))
                E.wait_tok("vector", wt)
                cast_tok[b] = E.do("vector", lambda v, cc=cc, b=b: v.tensor_copy(
                    wproj_sb[:, cc, :], wstage[b][:, 0:C]), inc="sdv")
            E.flush_block()

            xbuf = [[psb(f"xb{pb}_{t}", [P, C], F32) for t in range(2)] for pb in range(2)]
            x1c = [psb(f"x1c{t}", [P, C], BF16) for t in range(2)]
            qnb = [psb(f"qnb{t}", [P, C], BF16) for t in range(2)]
            knb = [psb(f"knb{t}", [P, C], BF16) for t in range(2)]
            x1T = psb("x1T", [P, 2, NCH, P], BF16)
            v_sb = psb("v_sb", [P, 2, C], BF16)
            qT = psb("qT", [P, NCH, 256], BF16)
            kT = psb("kT", [P, NCH, 256], BF16)
            pP = [psb(f"pP{i}", [P, 2, 384], F32) for i in range(2)]
            pmP = [psb(f"pmP{i}", [P, 2, 384], F32) for i in range(2)]
            pn = [psb(f"pn{i}", [P, 2, 384], BF16) for i in range(2)]
            pTb = [psb(f"pTb{i}", [P, 768], BF16) for i in range(2)]
            y_sb = psb("y_sb", [P, NCH, 256], BF16)
            xm_sb = [[psb(f"xm{pb}_{t}", [P, C], F32) for t in range(2)] for pb in range(2)]
            xp_t = [psb(f"xp{t}", [P, C], F32) for t in range(2)]
            st6 = psb("st6", [P, 12], F32)
            mv2 = psb("mv2", [P, 2], F32)
            t128a = psb("t128a", [P, 1], F32)
            t128b = psb("t128b", [P, 1], F32)
            tmx = psb("tmx", [P, 1], F32)
            tmn = psb("tmn", [P, 1], F32)
            rqk = [psb(f"rqk{i}", [P, 1], F32) for i in range(2)]
            nmq = [psb(f"nmq{i}", [P, 4], F32) for i in range(2)]
            rsq = [psb(f"rsq{i}", [P, 4], F32) for i in range(2)]

            ps_big = pps("ps_big", [P, 3072], F32)
            ps_tp = [pps(f"ps_tp{i}", [P, 768], BF16) for i in range(2)]

            def BK(i, off=0, n=512):
                return ps_big[:, i * 512 + off:i * 512 + off + n]

            x_need = [[], []]
            x_need[0] = (E.dma("sync", "sx", lambda s: s.dma_start(
                             xbuf[0][0][:, :], x_h[0:P, :]))
                         + E.dma("sync", "sx", lambda s: s.dma_start(
                             xbuf[0][1][:, :], x_h[P:2 * P, :])))

            ev_bank = [("sdv", 0)] * 6
            evtp = [("sdv", 0), ("sac", 0)]
            nm_rd = [0, 0]
            pp_rd = [0, 0]
            y_rd = [0, 0]
            buf_free = {}
            xm_guard = [[[], []], [[], []]]
            y_proj_rd = 0
            v_y_rd = 0
            t128_rd = 0
            rqk_rd = [0, 0]

            def tbatch(srcs, psum_bank, dst_ap_fn, eng, sem):
                pst = ps_tp[psum_bank]
                n = len(srcs)
                for i in range(n - 1):
                    E.do("tensor", lambda tt, i=i: tt.transpose(
                        pst[:, i * P:(i + 1) * P], srcs[i](), idb[:, :]))
                tp = E.do("tensor", lambda tt, i=n - 1: tt.transpose(
                    pst[:, i * P:(i + 1) * P], srcs[i](), idb[:, :]), inc="smm")
                E.wait(eng, "smm", tp)
                if eng == "scalar":
                    ev = E.do(eng, lambda e: e.copy(
                        dst_ap_fn(), pst[:, 0:n * P]), inc=sem)
                else:
                    ev = E.do(eng, lambda e: e.tensor_copy(
                        dst_ap_fn(), pst[:, 0:n * P]), inc=sem)
                return (sem, ev)

            ln1_tok = [0, 0]

            def emit_ln1(gg, t):
                ppb = gg % 2
                xt = xbuf[ppb][t]
                E.wait_tok("vector", x_need[ppb])
                E.do("vector", lambda v, xt=xt: v.bn_stats(st6[:, 0:6], xt[:, 0:512]))
                E.do("vector", lambda v, xt=xt: v.bn_stats(st6[:, 6:12],
                                                           xt[:, 512:1024]))
                E.do("vector", lambda v: v.bn_aggr(mv2[:, :], st6[:, :]), inc="sdv")
                E.wait("scalar", "sdv")
                E.wait("scalar", "sdv", t128_rd)
                E.do("scalar", lambda s: s.activation(
                    t128a[:, :], mv2[:, 1:2], AF.Ln, bias=EPS_LN))
                E.do("scalar", lambda s: s.activation(
                    t128a[:, :], t128a[:, :], AF.Exp, scale=-0.5), inc="sac")
                E.wait("vector", "sac")
                E.do("vector", lambda v: v.tensor_mul(
                    t128b[:, :], mv2[:, 0:1], t128a[:, :]))
                for sem, tk in buf_free.get(("x1c", t), []):
                    E.wait("vector", sem, tk)
                tok = E.do("vector", lambda v, xt=xt, t=t: v.tensor_scalar(
                    x1c[t][:, :], xt[:, :], t128a[:, :], t128b[:, :],
                    ALU.mult, ALU.subtract), inc="sdv")
                ln1_tok[t] = tok
                return tok

            t128_rd = emit_ln1(0, 0)
            t128_rd = emit_ln1(0, 1)

            for g in range(NG):
                pb = g % 2
                if g + 1 < NG:
                    npb = (g + 1) % 2
                    E.wait("sync", "sdv")
                    tks = []
                    for t in range(2):
                        tks += E.dma("sync", "sx", lambda s, g=g, t=t, npb=npb: s.dma_start(
                            xbuf[npb][t][:, :],
                            x_h[(g + 1) * 256 + t * P:(g + 1) * 256 + (t + 1) * P, :]))
                    x_need[npb] = tks

                qn_tok = [0, 0]
                kn_tok = [0, 0]
                for t in range(2):
                    xt = xbuf[pb][t]
                    x1_tok = ln1_tok[t]

                    evs = []
                    for half in range(2):
                        bank = half
                        E.wait("tensor", "sdv", x1_tok)
                        E.wait("tensor", evtp[bank][0], evtp[bank][1])
                        eng = "vector" if half == 0 else "scalar"
                        sem = "sdv" if half == 0 else "sac"
                        evtp[bank] = tbatch(
                            [(lambda t=t, cc=half * 4 + i: x1c[t][:, cc * P:(cc + 1) * P])
                             for i in range(4)],
                            bank,
                            lambda t=t, half=half: x1T[:, t, half * 4:(half + 1) * 4, :],
                            eng, sem)
                        evs.append(evtp[bank])
                    buf_free[("x1c", t)] = list(evs)

                    x1T_ready = list(evs)
                    qkv_tok = [0] * 6
                    for pc in range(6):
                        E.wait("tensor", ev_bank[pc][0], ev_bank[pc][1])
                        for sem, tk in x1T_ready:
                            E.wait("tensor", sem, tk)
                        x1T_ready = []
                        for cc in range(NCH):
                            kw = dict(start=(cc == 0), stop=(cc == NCH - 1))
                            lam = (lambda tt, t=t, cc=cc, pc=pc, kw=kw:
                                   tt.matmul(BK(pc), x1T[:, t, cc, :],
                                             wqkv_sb[:, cc, pc * 512:(pc + 1) * 512],
                                             **kw))
                            if cc == NCH - 1:
                                qkv_tok[pc] = E.do("tensor", lam, inc="smm")
                            else:
                                E.do("tensor", lam)

                    E.wait("vector", "smm", qkv_tok[1])
                    E.do("vector", lambda v: v.bn_stats(st6[:, 0:6], BK(0)))
                    E.do("vector", lambda v: v.bn_stats(st6[:, 6:12], BK(1)))
                    E.do("vector", lambda v: v.bn_aggr(mv2[:, :], st6[:, :]))
                    E.do("vector", lambda v: v.tensor_mul(t128b[:, :], mv2[:, 0:1],
                                                          mv2[:, 0:1]))
                    E.do("vector", lambda v: v.tensor_add(tmx[:, :], mv2[:, 1:2],
                                                          t128b[:, :]), inc="sdv")
                    E.wait("scalar", "sdv")
                    E.wait("scalar", "sdv", rqk_rd[0])
                    E.do("scalar", lambda s: s.activation(
                        rqk[0][:, :], tmx[:, :], AF.Ln, bias=EPS_RMS))
                    E.do("scalar", lambda s: s.activation(
                        rqk[0][:, :], rqk[0][:, :], AF.Exp, scale=-0.5), inc="sac")
                    E.wait("vector", "sac")
                    for sem, tk in buf_free.get(("qnb", t), []):
                        E.wait("vector", sem, tk)
                    qn_tok[t] = E.do("vector", lambda v, t=t: v.scalar_tensor_tensor(
                        qnb[t][:, :], ps_big[:, 0:1024], rqk[0][:, :], lnwq_t[:, :],
                        ALU.mult, ALU.mult), inc="sdv")
                    rqk_rd[0] = qn_tok[t]
                    ev_bank[0] = ev_bank[1] = ("sdv", qn_tok[t])

                    E.wait("vector", "smm", qkv_tok[3])
                    E.do("vector", lambda v: v.bn_stats(st6[:, 0:6], BK(2)))
                    E.do("vector", lambda v: v.bn_stats(st6[:, 6:12], BK(3)))
                    E.do("vector", lambda v: v.bn_aggr(mv2[:, :], st6[:, :]))
                    E.do("vector", lambda v: v.tensor_mul(t128b[:, :], mv2[:, 0:1],
                                                          mv2[:, 0:1]))
                    E.do("vector", lambda v: v.tensor_add(tmx[:, :], mv2[:, 1:2],
                                                          t128b[:, :]), inc="sdv")
                    E.wait("scalar", "sdv")
                    E.wait("scalar", "sdv", rqk_rd[1])
                    E.do("scalar", lambda s: s.activation(
                        rqk[1][:, :], tmx[:, :], AF.Ln, bias=EPS_RMS))
                    E.do("scalar", lambda s: s.activation(
                        rqk[1][:, :], rqk[1][:, :], AF.Exp, scale=-0.5), inc="sac")
                    E.wait("vector", "sac")
                    for sem, tk in buf_free.get(("knb", t), []):
                        E.wait("vector", sem, tk)
                    kn_tok[t] = E.do("vector", lambda v, t=t: v.scalar_tensor_tensor(
                        knb[t][:, :], ps_big[:, 1024:2048], rqk[1][:, :], lnwk_t[:, :],
                        ALU.mult, ALU.mult), inc="sdv")
                    rqk_rd[1] = kn_tok[t]
                    ev_bank[2] = ev_bank[3] = ("sdv", kn_tok[t])

                    E.wait("scalar", "smm", qkv_tok[5])
                    E.wait("scalar", "smm", v_y_rd)
                    E.do("scalar", lambda s, t=t: s.copy(v_sb[:, t, 0:512], BK(4)))
                    vev = E.do("scalar", lambda s, t=t: s.copy(v_sb[:, t, 512:1024],
                                                               BK(5)), inc="sac")
                    ev_bank[4] = ev_bank[5] = ("sac", vev)

                qkT_toks = []
                for t in range(2):
                    E.wait("sync", "sdv", qn_tok[t])
                    tks = []
                    for cc in range(NCH):
                        tks += E.dma("sync", "sx", lambda s, t=t, cc=cc: s.dma_start(
                            qT[:, cc, t * P:(t + 1) * P],
                            qnb[t][:, cc * P:(cc + 1) * P], transpose=True))
                    buf_free[("qnb", t)] = tks
                    qkT_toks += tks
                for t in range(2):
                    evs = []
                    for half in range(2):
                        bank = half
                        E.wait("tensor", "sdv", kn_tok[t])
                        E.wait("tensor", evtp[bank][0], evtp[bank][1])
                        eng = "vector" if half == 0 else "scalar"
                        sem = "sdv" if half == 0 else "sac"
                        evtp[bank] = tbatch(
                            [(lambda t=t, cc=half * 4 + i:
                              knb[t][:, cc * P:(cc + 1) * P]) for i in range(4)],
                            bank,
                            lambda half=half, t=t: bass.AP(
                                kT, (half * 4) * 256 + t * P,
                                [[NCH * 256, P], [256, 4], [1, P]]),
                            eng, sem)
                        evs.append(evtp[bank])
                    buf_free[("knb", t)] = list(evs)
                    qkT_toks.append(evtp[0])
                    qkT_toks.append(evtp[1])

                qkT_done = qkT_toks

                def emit_scores(hp):
                    for sem, tk in qkT_done:
                        E.wait("tensor", sem, tk)
                    qkT_done.clear()
                    E.wait("tensor", ev_bank[0][0], ev_bank[0][1])
                    E.wait("tensor", ev_bank[1][0], ev_bank[1][1])
                    E.do("tensor", lambda tt, hp=hp: tt.matmul(
                        BK(0, 0, P), qT[0:HD, hp, 0:P], kT[0:HD, hp, 0:P],
                        start=True, stop=True, tile_position=(0, 0)))
                    E.do("tensor", lambda tt, hp=hp: tt.matmul(
                        BK(1, 0, P), qT[HD:2 * HD, hp, 0:P], kT[HD:2 * HD, hp, 0:P],
                        start=True, stop=True, tile_position=(HD, 0)))
                    E.do("tensor", lambda tt, hp=hp: tt.matmul(
                        BK(0, P, 256), qT[0:HD, hp, P:256], kT[0:HD, hp, 0:256],
                        start=True, stop=True, tile_position=(0, 0)))
                    return E.do("tensor", lambda tt, hp=hp: tt.matmul(
                        BK(1, P, 256), qT[HD:2 * HD, hp, P:256],
                        kT[HD:2 * HD, hp, 0:256],
                        start=True, stop=True, tile_position=(HD, 0)), inc="smm")

                s_tok = emit_scores(0)
                pend = [None]

                def emit_tail(ctx):
                    hp, par, pE, pnp, rs, exp_tok = ctx
                    E.wait("vector", "sac", exp_tok)
                    E.do("vector", lambda v, rs=rs: v.reciprocal(rs[:, :], rs[:, :]),
                         inc="sdv", drain=False)
                    E.wait("scalar", "sdv")
                    E.do("vector", lambda v, pE=pE, pnp=pnp, rs=rs: v.tensor_scalar_mul(
                        pnp[:, 0, 0:P], pE[:, 0, 0:P], rs[:, 0:1]))
                    nv = E.do("vector", lambda v, pE=pE, pnp=pnp, rs=rs:
                              v.tensor_scalar_mul(
                                  pnp[:, 0, P:384], pE[:, 0, P:384], rs[:, 2:3]),
                              inc="sdv", drain=False)
                    E.do("scalar", lambda s, pE=pE, pnp=pnp, rs=rs: s.activation(
                        pnp[:, 1, 0:P], pE[:, 1, 0:P], AF.Copy, scale=rs[:, 1:2]),
                        drain=False)
                    na = E.do("scalar", lambda s, pE=pE, pnp=pnp, rs=rs: s.activation(
                        pnp[:, 1, P:384], pE[:, 1, P:384], AF.Copy, scale=rs[:, 3:4]),
                        inc="sac", drain=False)
                    pp_rd[par] = nv

                    E.wait("tensor", "sdv", nv)
                    E.wait("tensor", "sac", na)
                    E.wait("tensor", evtp[par][0], evtp[par][1])
                    pst = ps_tp[par]
                    for i, srcf in enumerate((
                            lambda pnp=pnp: pnp[:, 0, 0:P],
                            lambda pnp=pnp: pnp[:, 1, 0:P],
                            lambda pnp=pnp: pnp[:, 0, P:256],
                            lambda pnp=pnp: pnp[:, 0, 256:384],
                            lambda pnp=pnp: pnp[:, 1, P:256])):
                        E.do("tensor", lambda tt, i=i, srcf=srcf, pst=pst: tt.transpose(
                            pst[:, i * P:(i + 1) * P], srcf(), idb[:, :]))
                    tp = E.do("tensor", lambda tt, pst=pst, pnp=pnp: tt.transpose(
                        pst[:, 5 * P:6 * P], pnp[:, 1, 256:384], idb[:, :]), inc="smm")
                    eng = "vector" if par == 0 else "scalar"
                    sem = "sdv" if par == 0 else "sac"
                    E.wait(eng, "smm", tp)
                    E.wait(eng, "smm", y_rd[par])
                    if eng == "scalar":
                        ev = E.do(eng, lambda e, par=par, pst=pst: e.copy(
                            pTb[par][:, :], pst[:, :]), inc=sem)
                    else:
                        ev = E.do(eng, lambda e, par=par, pst=pst: e.tensor_copy(
                            pTb[par][:, :], pst[:, :]), inc=sem)
                    evtp[par] = (sem, ev)

                    yb = 2 + par
                    E.wait("tensor", evtp[par][0], evtp[par][1])
                    E.wait("tensor", ev_bank[yb][0], ev_bank[yb][1])
                    for j in range(2):
                        h = 2 * hp + j
                        E.do("tensor", lambda tt, h=h, j=j, par=par, yb=yb: tt.matmul(
                            ps_big[j * HD:(j + 1) * HD, yb * 512:yb * 512 + P],
                            v_sb[:, 0, h * HD:(h + 1) * HD],
                            pTb[par][:, j * P:(j + 1) * P],
                            start=True, stop=True, tile_position=(0, j * HD)))
                    y_tok = None
                    for j in range(2):
                        h = 2 * hp + j
                        for kt in range(2):
                            kw = dict(start=(kt == 0), stop=(kt == 1),
                                      tile_position=(0, j * HD))
                            lam = (lambda tt, h=h, j=j, kt=kt, kw=kw, par=par, yb=yb:
                                   tt.matmul(
                                       ps_big[j * HD:(j + 1) * HD,
                                              yb * 512 + P:yb * 512 + 256],
                                       v_sb[:, kt, h * HD:(h + 1) * HD],
                                       pTb[par][:, 256 + (2 * j + kt) * P:
                                                256 + (2 * j + kt + 1) * P], **kw))
                            if j == 1 and kt == 1:
                                y_tok = E.do("tensor", lam, inc="smm")
                            else:
                                E.do("tensor", lam)
                    nonlocal_state["v_y_rd"] = y_tok
                    y_rd[par] = y_tok
                    if par == 1:
                        E.wait("scalar", "smm", y_tok)
                        E.wait("scalar", "smm", y_proj_rd)
                        yev = E.do("scalar", lambda s, hp=hp: s.copy(
                            y_sb[:, hp - 1:hp + 1, :],
                            bass.AP(ps_big, 1024, [[3072, P], [512, 2], [1, 256]])),
                            inc="sac")
                        ev_bank[2] = ev_bank[3] = ("sac", yev)

                nonlocal_state = {"v_y_rd": v_y_rd}
                for hp in range(NCH):
                    par = hp % 2
                    sP = pmP[par]
                    pE = pP[par]
                    pnp = pn[par]
                    nm = nmq[par]
                    rs = rsq[par]
                    E.wait("vector", "smm", s_tok)
                    E.wait("vector", "sac", nm_rd[par])
                    E.wait("vector", "sdv", pp_rd[par])
                    E.do("vector", lambda v, sP=sP: v.tensor_tensor(
                        bass.AP(sP, 0, [[768, P], [384, 2], [1, P]]),
                        bass.AP(ps_big, 0, [[3072, P], [512, 2], [1, P]]),
                        bass.AP(m01, 0, [[P, P], [0, 2], [1, P]]), ALU.add))
                    mk_tok = E.do("vector", lambda v, sP=sP: v.tensor_tensor(
                        bass.AP(sP, P, [[768, P], [384, 2], [1, 256]]),
                        bass.AP(ps_big, P, [[3072, P], [512, 2], [1, 256]]),
                        bass.AP(m01q1, 0, [[256, P], [0, 2], [1, 256]]), ALU.add),
                        inc="sdv", drain=False)
                    ev_bank[0] = ev_bank[1] = ("sdv", mk_tok)

                    if hp + 1 < NCH:
                        s_tok = emit_scores(hp + 1)

                    E.do("vector", lambda v, sP=sP, nm=nm: v.tensor_reduce(
                        nm[:, 0:2], bass.AP(sP, 0, [[768, P], [384, 2], [1, P]]),
                        AX.X, ALU.max, negate=True))
                    nm_tok = E.do("vector", lambda v, sP=sP, nm=nm: v.tensor_reduce(
                        nm[:, 2:4], bass.AP(sP, P, [[768, P], [384, 2], [1, 256]]),
                        AX.X, ALU.max, negate=True), inc="sdv", drain=False)

                    E.wait("scalar", "sdv", nm_tok)
                    E.do("scalar", lambda s, sP=sP, pE=pE, nm=nm, rs=rs: s.activation(
                        pE[:, 0, 0:P], sP[:, 0, 0:P], AF.Exp, bias=nm[:, 0:1],
                        accum_out=rs[:, 0:1]), drain=False)
                    E.do("scalar", lambda s, sP=sP, pE=pE, nm=nm, rs=rs: s.activation(
                        pE[:, 1, 0:P], sP[:, 1, 0:P], AF.Exp, bias=nm[:, 1:2],
                        accum_out=rs[:, 1:2]), drain=False)
                    E.do("scalar", lambda s, sP=sP, pE=pE, nm=nm, rs=rs: s.activation(
                        pE[:, 0, P:384], sP[:, 0, P:384], AF.Exp, bias=nm[:, 2:3],
                        accum_out=rs[:, 2:3]), drain=False)
                    exp_tok = E.do("scalar", lambda s, sP=sP, pE=pE, nm=nm, rs=rs:
                                   s.activation(
                                       pE[:, 1, P:384], sP[:, 1, P:384], AF.Exp,
                                       bias=nm[:, 3:4], accum_out=rs[:, 3:4]),
                                   inc="sac", drain=False)
                    nm_rd[par] = exp_tok

                    if pend[0] is not None:
                        emit_tail(pend[0])
                    pend[0] = (hp, par, pE, pnp, rs, exp_tok)
                emit_tail(pend[0])
                pend[0] = None
                v_y_rd = nonlocal_state["v_y_rd"]

                if g + 1 < NG:
                    t128_rd = emit_ln1(g + 1, 0)
                    t128_rd = emit_ln1(g + 1, 1)

                y_ready = ("sac", E.tok("sac"))
                proj_tok = [0, 0]
                proj_banks = [(4, 5), (2, 3)]
                for t in range(2):
                    ba, bb = proj_banks[t]
                    E.wait("tensor", y_ready[0], y_ready[1])
                    E.wait("tensor", ev_bank[ba][0], ev_bank[ba][1])
                    E.wait("tensor", ev_bank[bb][0], ev_bank[bb][1])
                    for hp in range(NCH):
                        last = hp == NCH - 1
                        kw0 = dict(start=(hp == 0), stop=last)
                        E.do("tensor", lambda tt, hp=hp, t=t, ba=ba, kw=kw0: tt.matmul(
                            BK(ba), y_sb[:, hp, t * P:(t + 1) * P],
                            wproj_sb[:, hp, 0:512], **kw))
                        lam = (lambda tt, hp=hp, t=t, bb=bb, kw=kw0: tt.matmul(
                            BK(bb), y_sb[:, hp, t * P:(t + 1) * P],
                            wproj_sb[:, hp, 512:1024], **kw))
                        if last:
                            proj_tok[t] = E.do("tensor", lam, inc="smm")
                        else:
                            E.do("tensor", lam)
                y_proj_rd = proj_tok[1]

                for t in range(2):
                    ba, bb = proj_banks[t]
                    tl = g * 2 + t
                    E.wait("vector", "smm", proj_tok[t])
                    E.do("vector", lambda v, ba=ba: v.bn_stats(st6[:, 0:6], BK(ba)))
                    E.do("vector", lambda v, bb=bb: v.bn_stats(st6[:, 6:12], BK(bb)))
                    E.do("vector", lambda v: v.bn_aggr(mv2[:, :], st6[:, :]), inc="sdv")
                    E.wait("scalar", "sdv")
                    E.wait("scalar", "sdv", t128_rd)
                    E.do("scalar", lambda s: s.activation(
                        t128a[:, :], mv2[:, 1:2], AF.Ln, bias=EPS_LN))
                    E.do("scalar", lambda s: s.activation(
                        t128a[:, :], t128a[:, :], AF.Exp, scale=-0.5), inc="sac")
                    E.wait("vector", "sac")
                    E.do("vector", lambda v: v.tensor_mul(
                        t128b[:, :], mv2[:, 0:1], t128a[:, :]))
                    E.do("vector", lambda v, pb=pb, t=t: v.tensor_scalar(
                        xp_t[t][:, :], xbuf[pb][t][:, :], 1.0, t128b[:, :],
                        ALU.mult, ALU.subtract))
                    for sem, tk in xm_guard[pb][t]:
                        E.wait("vector", sem, tk)
                    E.do("vector", lambda v, pb=pb, t=t, ba=ba: v.scalar_tensor_tensor(
                        xm_sb[pb][t][:, 0:512], BK(ba), t128a[:, :],
                        xp_t[t][:, 0:512], ALU.mult, ALU.add), inc="sdv")
                    ev_bank[ba] = ("sdv", E.tok("sdv"))
                    stt2 = E.do("vector", lambda v, pb=pb, t=t, bb=bb:
                                v.scalar_tensor_tensor(
                                    xm_sb[pb][t][:, 512:1024], BK(bb), t128a[:, :],
                                    xp_t[t][:, 512:1024], ALU.mult, ALU.add), inc="sdv")
                    ev_bank[bb] = ("sdv", stt2)
                    t128_rd = stt2
                    E.wait("gpsimd", "sdv", stt2)
                    dtok = E.dma("gpsimd", "sgd",
                                 lambda gp, g=g, t=t, pb=pb: gp.dma_start(
                                     xmid_h[g * 256 + t * P:g * 256 + (t + 1) * P, :],
                                     xm_sb[pb][t][:, :]))
                    E.do("vector", lambda v, pb=pb, t=t: v.bn_stats(
                        st6[:, 0:6], xm_sb[pb][t][:, 0:512]))
                    E.do("vector", lambda v, pb=pb, t=t: v.bn_stats(
                        st6[:, 6:12], xm_sb[pb][t][:, 512:1024]))
                    E.do("vector", lambda v, tl=tl: v.bn_aggr(
                        mvst[:, 2 * tl:2 * tl + 2], st6[:, :]))
                    E.do("vector", lambda v, pb=pb, t=t: v.tensor_reduce(
                        tmx[:, :], xm_sb[pb][t][:, :], AX.X, ALU.max))
                    E.do("vector", lambda v, pb=pb, t=t: v.tensor_reduce(
                        tmn[:, :], xm_sb[pb][t][:, :], AX.X, ALU.min))
                    E.do("vector", lambda v, tl=tl: v.tensor_sub(
                        tmx[:, :], tmx[:, :], mvst[:, 2 * tl:2 * tl + 1]))
                    E.do("vector", lambda v, tl=tl: v.tensor_sub(
                        tmn[:, :], mvst[:, 2 * tl:2 * tl + 1], tmn[:, :]))
                    sttok = E.do("vector", lambda v, tl=tl: v.tensor_tensor(
                        gmax[:, tl:tl + 1], tmx[:, :], tmn[:, :], ALU.max), inc="sdv")
                    xm_guard[pb][t] = dtok + [("sdv", sttok)]

            E.wait("scalar", "sdv")
            E.do("scalar", lambda s: s.activation(
                rln[:, :], bass.AP(mvst, 1, [[2 * NT, P], [2, NT]]), AF.Ln,
                bias=EPS_LN))
            rln_tok = E.do("scalar", lambda s: s.activation(
                rln[:, :], rln[:, :], AF.Exp, scale=-0.5), inc="sac")
            E.wait("vector", "sac", rln_tok)
            E.do("vector", lambda v: v.tensor_tensor(
                mrln[:, :], bass.AP(mvst, 0, [[2 * NT, P], [2, NT]]), rln[:, :],
                ALU.mult))
            E.do("vector", lambda v: v.tensor_mul(gmax[:, :], gmax[:, :], rln[:, :]))
            E.do("vector", lambda v: v.tensor_reduce(
                am_acc[:, :], gmax[:, :], AX.X, ALU.max), inc="sdv")

            E.wait_tok("gpsimd", E.queue_tok("sgd"))
            E.wait_tok("sync", E.queue_tok("sx"))
            E.flush_block()

        if phases < 2:
            return nc
        with contextlib.ExitStack() as s2k:
            psb = lambda n, s, d: s2k.enter_context(nc.sbuf_tensor(n, s, d))
            xmb = [psb(f"xmb{i}", [P, C], F32) for i in range(2)]
            ta = psb("ta", [P, 1], F32)

            def emit_collective(cin, cout, s_pp, si_pp):
                t1 = E.dma("gpsimd", "sgd", lambda g: g.dma_start(
                    am_row[0:1, :], am_acc[:, 0:1]))
                E.wait_tok("vector", t1)
                E.do("vector", lambda v: v.tensor_reduce(
                    ccsb[:, :], bass.AP(am_row, 0, [[P, 1], [16, 8], [1, 16]]),
                    AX.X, ALU.max), inc="sdv")
                E.wait("gpsimd", "sdv")
                t2 = E.dma("gpsimd", "sgd", lambda g: g.dma_start(cin[:, :], ccsb[:, :]))
                E.wait_tok("gpsimd", t2)
                cc_tok = E.do("gpsimd", lambda g: g.collective_compute(
                    "AllReduce", ALU.max, replica_groups=[list(range(N_CORES))],
                    ins=[cin.ap()], outs=[cout.ap()]), inc="scc")
                E.wait("gpsimd", "scc", cc_tok)
                t3 = E.dma("gpsimd", "sgd", lambda g: g.dma_start(ccsb[:, :], cout[:, :]))
                E.wait_tok("vector", t3)
                E.do("vector", lambda v: v.tensor_reduce(
                    sc_a[:, :], ccsb[:, :], AX.X, ALU.max))
                E.do("vector", lambda v: v.tensor_scalar_max(sc_a[:, :], sc_a[:, :], 1.0e-8))
                E.do("vector", lambda v: v.reciprocal(sc_b[:, :], sc_a[:, :]))
                E.do("vector", lambda v: v.tensor_scalar_mul(
                    sc_b[:, :], sc_b[:, :], 32767.0))
                E.do("vector", lambda v: v.tensor_scalar_mul(
                    sc_c[:, :], sc_a[:, :], 1.0 / 32767.0), inc="sdv")
                bs1 = E.do("vector", lambda v: v.tensor_scalar(
                    am_row[0:1, :], am_row[0:1, :], 0.0, sc_b[0:1, 0:1],
                    ALU.mult, ALU.add), inc="sdv")
                E.wait("gpsimd", "sdv", bs1)
                t4 = E.dma("gpsimd", "sgd", lambda g: g.dma_start(
                    s_pp[:, 0:1], am_row[0:1, 0:P]))
                E.wait_tok("vector", t4)
                bs2 = E.do("vector", lambda v: v.tensor_scalar(
                    am_row[0:1, :], am_row[0:1, :], 0.0, sc_c[0:1, 0:1],
                    ALU.mult, ALU.add), inc="sdv")
                E.wait("gpsimd", "sdv", bs2)
                t5 = E.dma("gpsimd", "sgd", lambda g: g.dma_start(
                    si_pp[:, 0:1], am_row[0:1, 0:P]))
                E.wait_tok("gpsimd", t5)
                E.do("gpsimd", lambda g: g.memset(am_row[0:1, 0:1], 0.0), inc="sgp")
                E.flush_block()

            emit_collective(cc1_in, cc1_out, s1_pp, s1i_pp)
            if phases < 3:
                return nc

            x2qT = psb("x2qT", [P, NCH, T], BF16)
            u2 = [psb(f"u2_{i}", [P, C], BF16) for i in range(2)]
            u_sb = psb("u_sb", [P, C], BF16)
            tq = [psb(f"tq{i}", [P, C], F32) for i in range(2)]
            rs_ = psb("rs_", [P, NT], F32)
            mrs = psb("mrs", [P, NT], F32)
            ps_h = [s2k.enter_context(nc.psum_tensor(f"ps_h{i}", [P, 512], F32))
                    for i in range(2)]
            x2stack = contextlib.ExitStack()
            ps_t2 = [x2stack.enter_context(nc.psum_tensor(f"ps_t2{i}", [P, 512], BF16))
                     for i in range(2)]

            xm_tok = [None, None]
            xm_tok[0] = E.dma("sync", "sx", lambda s: s.dma_start(
                xmb[0][:, :], xmid_h[0:P, :]))
            xm_tok[1] = E.dma("sync", "sx", lambda s: s.dma_start(
                xmb[1][:, :], xmid_h[P:2 * P, :]))
            ev_t2 = [("sdv", 0), ("sac", 0)]
            u_free = [[], []]
            E.wait("vector", "sgp")
            E.do("vector", lambda v: v.tensor_scalar_mul(
                rs_[:, :], rln[:, :], s1_pp[:, 0:1]))
            rsm_tok = E.do("vector", lambda v: v.tensor_scalar(
                mrs[:, :], mrln[:, :], s1_pp[:, 0:1], -1.0, ALU.mult, ALU.mult),
                inc="sdv")
            tq_free = [0, 0]
            for tl in range(NT):
                b = tl % 2
                E.wait_tok("scalar", xm_tok[b])
                E.wait("scalar", "sdv", rsm_tok)
                E.wait("scalar", "sdv", tq_free[b])
                tq_tok = E.do("scalar", lambda s, b=b, tl=tl: s.activation(
                    tq[b][:, :], xmb[b][:, :], AF.Identity, scale=rs_[:, tl:tl + 1],
                    bias=mrs[:, tl:tl + 1]), inc="sac")
                for sem, tk in u_free[b]:
                    E.wait("vector", sem, tk)
                E.wait("vector", "sac", tq_tok)
                ld_tok = E.do("vector", lambda v, b=b: v.tensor_scalar(
                    u2[b][:, :], tq[b][:, :], MR, MR, ALU.add, ALU.subtract),
                    inc="sdv")
                tq_free[b] = ld_tok
                if tl + 2 < NT:
                    E.wait("sync", "sdv", ld_tok)
                    xm_tok[b] = E.dma("sync", "sx", lambda s, tl=tl, b=b: s.dma_start(
                        xmb[b][:, :], xmid_h[(tl + 2) * P:(tl + 3) * P, :]))
                evs = []
                for half in range(2):
                    bank = half
                    E.wait("tensor", "sdv", ld_tok)
                    E.wait("tensor", ev_t2[bank][0], ev_t2[bank][1])
                    for i in range(3):
                        cc = half * 4 + i
                        E.do("tensor", lambda tt, bank=bank, i=i, cc=cc, b=b:
                             tt.transpose(
                                 ps_t2[bank][:, i * P:(i + 1) * P],
                                 u2[b][:, cc * P:(cc + 1) * P], idb[:, :]))
                    tp = E.do("tensor", lambda tt, bank=bank, half=half, b=b:
                              tt.transpose(
                                  ps_t2[bank][:, 3 * P:4 * P],
                                  u2[b][:, (half * 4 + 3) * P:(half * 4 + 4) * P],
                                  idb[:, :]), inc="smm")
                    eng = "vector" if half == 0 else "scalar"
                    sem = "sdv" if half == 0 else "sac"
                    E.wait(eng, "smm", tp)

                    def _evac(e, half=half, tl=tl, bank=bank):
                        dst = bass.AP(x2qT, (half * 4) * T + tl * P,
                                      [[NCH * T, P], [T, 4], [1, P]])
                        if hasattr(e, "tensor_copy"):
                            return e.tensor_copy(dst, ps_t2[bank][:, :])
                        return e.copy(dst, ps_t2[bank][:, :])
                    ev = E.do(eng, _evac, inc=sem)
                    ev_t2[bank] = (sem, ev)
                    evs.append((sem, ev))
                u_free[b] = evs
            E.wait_tok("sync", E.queue_tok("sx"))
            E.flush_block()
            x2stack.close()

            ps_w = [s2k.enter_context(nc.psum_tensor(f"ps_w{i}", [P, 512], F32))
                    for i in range(4)]
            winst = [psb(f"winst{i}", [P, NCH, P], F32) for i in range(2)]
            wtq = psb("wtq", [P, C], F32)
            wq_fc = [psb(f"wqfc{i}", [P, NCH, P], BF16) for i in range(2)]
            h_sb = [psb(f"hsb{i}", [P, 512], BF16) for i in range(3)]
            aw = psb("aw", [P, 1], F32)
            swv = psb("swv", [P, 1], F32)
            schf = psb("schf", [P, 1], F32)
            ps_wb = s2k.enter_context(nc.psum_tensor("ps_wb", [P, 512], BF16))

            E.do("vector", lambda v: v.memset(am_acc[:, :], 0.0))

            def win_slice_ap(fc):
                return bass.AP(win_h, fc * P, [[FF, P], [P * FF, NCH], [1, P]])

            win_tok = [None, None]
            win_rd = [0, 0]
            wt_tp = [0, 0]
            wq_rd = [0, 0]
            ev_w = [("sdv", 0)] * 2
            ev_wb = ("sdv", 0)
            cast_prev = 0
            gelu_prev = 0
            ev_h = [0, 0]
            h_dma = [[], [], []]

            def emit_wt_trans(fc):
                b = fc % 2
                toks = []
                for half in range(2):
                    bank = b * 2 + half
                    E.wait_tok("tensor", win_tok[b])
                    E.wait("tensor", ev_w[b][0], ev_w[b][1])
                    for i in range(3):
                        cc = half * 4 + i
                        E.do("tensor", lambda tt, bank=bank, i=i, cc=cc, b=b:
                             tt.transpose(ps_w[bank][:, i * P:(i + 1) * P],
                                          winst[b][:, cc, :], idf[:, :]))
                    tp = E.do("tensor", lambda tt, bank=bank, b=b, half=half:
                              tt.transpose(ps_w[bank][:, 3 * P:4 * P],
                                           winst[b][:, half * 4 + 3, :], idf[:, :]),
                              inc="smm")
                    toks.append(tp)
                    if half == 1:
                        win_rd[b] = tp
                wt_tp[b] = toks[1]

            win_tok[0] = E.dma("sync", "sx", lambda s: s.dma_start(
                winst[0][:, :, :], win_slice_ap(0)))
            win_tok[1] = E.dma("sync", "sx", lambda s: s.dma_start(
                winst[1][:, :, :], win_slice_ap(1)))
            emit_wt_trans(0)

            for fc in range(NF):
                b = fc % 2
                if fc + 2 < NF:
                    nb = fc % 2
                    E.wait("sync", "smm", win_rd[nb])
                    win_tok[nb] = E.dma("sync", "sx", lambda s, fc=fc, nb=nb:
                                        s.dma_start(winst[nb][:, :, :],
                                                    win_slice_ap(fc + 2)))
                if fc + 1 < NF:
                    emit_wt_trans(fc + 1)
                E.wait("vector", "smm", wt_tp[b])
                E.wait("vector", "sac", gelu_prev)
                E.do("vector", lambda v, b=b: v.tensor_reduce(
                    aw[:, :], ps_w[2 * b][:, :], AX.X, ALU.max,
                    apply_absolute_value=True), drain=False)
                E.do("vector", lambda v, b=b: v.tensor_reduce(
                    swv[:, :], ps_w[2 * b + 1][:, :], AX.X, ALU.max,
                    apply_absolute_value=True), drain=False)
                E.do("vector", lambda v: v.tensor_max(aw[:, :], aw[:, :], swv[:, :]))
                E.do("vector", lambda v: v.tensor_scalar_max(aw[:, :], aw[:, :], 1.0e-8))
                E.do("vector", lambda v: v.reciprocal(swv[:, :], aw[:, :]))
                E.do("vector", lambda v: v.tensor_scalar_mul(swv[:, :], swv[:, :], 127.0))
                E.do("vector", lambda v: v.tensor_scalar_mul(
                    schf[:, :], aw[:, :], 1.0 / 127.0), drain=False)
                E.wait("vector", "sgp")
                sch_tok = E.do("vector", lambda v: v.tensor_mul(
                    schf[:, :], schf[:, :], s1i_pp[:, :]), inc="sdv")
                E.wait("vector", "sac", cast_prev)
                E.do("vector", lambda v, b=b: v.tensor_scalar(
                    wtq[:, 0:512], ps_w[2 * b][:, :], swv[:, 0:1], None, ALU.mult))
                E.do("vector", lambda v, b=b: v.tensor_scalar(
                    wtq[:, 512:1024], ps_w[2 * b + 1][:, :], swv[:, 0:1], None,
                    ALU.mult), drain=False)
                uq_tok = E.do("vector", lambda v: v.tensor_scalar(
                    wtq[:, :], wtq[:, :], MR, MR, ALU.add, ALU.subtract), inc="sdv")
                ev_w[b] = ("sdv", uq_tok)
                E.wait("scalar", "sdv", uq_tok)
                E.wait("scalar", ev_wb[0], ev_wb[1])
                cast_prev = E.do("scalar", lambda s: s.copy(u_sb[:, :], wtq[:, :]),
                                 inc="sac")
                for half in range(2):
                    E.wait("tensor", "sac", cast_prev)
                    E.wait("tensor", ev_wb[0], ev_wb[1])
                    E.wait("tensor", "smm", wq_rd[b])
                    for i in range(3):
                        cc = half * 4 + i
                        E.do("tensor", lambda tt, i=i, cc=cc: tt.transpose(
                            ps_wb[:, i * P:(i + 1) * P], u_sb[:, cc * P:(cc + 1) * P],
                            idb[:, :]))
                    tp = E.do("tensor", lambda tt, half=half: tt.transpose(
                        ps_wb[:, 3 * P:4 * P],
                        u_sb[:, (half * 4 + 3) * P:(half * 4 + 4) * P], idb[:, :]),
                        inc="smm")
                    E.wait("vector", "smm", tp)
                    ev = E.do("vector", lambda v, half=half, b=b: v.tensor_copy(
                        bass.AP(wq_fc[b], (half * 4) * P, [[NCH * P, P], [P, 4], [1, P]]),
                        ps_wb[:, :]), inc="sdv")
                    ev_wb = ("sdv", ev)
                wq_ready = E.tok("sdv")

                for tb_ in range(8):
                    hb = (fc * 8 + tb_) % 3
                    bank = tb_ % 2
                    E.wait("tensor", "sdv", wq_ready)
                    E.wait("tensor", "sac", ev_h[bank])
                    mm_tok = None
                    for cc in range(NCH):
                        kw = dict(start=(cc == 0), stop=(cc == NCH - 1))
                        lam = (lambda tt, b=b, cc=cc, tb_=tb_, bank=bank, kw=kw:
                               tt.matmul(ps_h[bank][:, :], wq_fc[b][:, cc, :],
                                         x2qT[:, cc, tb_ * 512:(tb_ + 1) * 512], **kw))
                        if cc == NCH - 1:
                            mm_tok = E.do("tensor", lam, inc="smm")
                        else:
                            E.do("tensor", lam)
                    if tb_ == 7:
                        wq_rd[b] = mm_tok
                    E.wait("scalar", "smm", mm_tok)
                    E.wait("scalar", "sdv", sch_tok)
                    E.wait_tok("scalar", h_dma[hb])
                    ge_tok = E.do("scalar", lambda s, hb=hb, fc=fc, bank=bank: s.activation(
                        h_sb[hb][:, :], ps_h[bank][:, :], AF.Gelu,
                        bias=binp[:, fc:fc + 1], scale=schf[:, 0:1]), inc="sac")
                    ev_h[bank] = ge_tok
                    gelu_prev = ge_tok
                    E.wait("vector", "sac", ge_tok)
                    E.do("vector", lambda v, hb=hb: v.tensor_reduce(
                        ta[:, :], h_sb[hb][:, :], AX.X, ALU.max,
                        apply_absolute_value=True))
                    am_tok = E.do("vector", lambda v: v.tensor_max(
                        am_acc[:, :], am_acc[:, :], ta[:, :]), inc="sdv")
                    E.wait("gpsimd", "sdv", am_tok)
                    h_dma[hb] = E.dma("gpsimd", "sgd",
                                      lambda g, fc=fc, tb_=tb_, hb=hb: g.dma_start(
                                          bass.AP(hscr_h,
                                                  2 * tb_ * P * NF * 256 + fc * 256,
                                                  [[NF * 256, P], [P * NF * 256, 2],
                                                   [1, 256]]),
                                          bass.AP(h_sb[hb], 0,
                                                  [[512, P], [256, 2], [1, 256]])))

            E.wait_tok("gpsimd", E.queue_tok("sgd"))
            E.wait_tok("sync", E.queue_tok("sx"))
            E.flush_block()

            emit_collective(cc2_in, cc2_out, s2_pp, s2i_pp)

        if phases < 4:
            return nc
        with contextlib.ExitStack() as s3k:
            psb = lambda n, s, d: s3k.enter_context(nc.sbuf_tensor(n, s, d))
            wqout = psb("wqout", [P, NF, C], BF16)
            wst = [psb(f"wst{i}", [P, C], F32) for i in range(2)]
            hin = [psb(f"hin{i}", [P, NF, 256], BF16) for i in range(2)]
            hq = [psb(f"hq{i}", [P, NF, 256], BF16) for i in range(2)]
            thq = psb("thq", [P, NF * 256], F32)
            xmb2 = [psb(f"xmb2{i}", [P, C], F32) for i in range(2)]
            ffn = psb("ffn", [P, C], F32)
            outb = [psb(f"outb{i}", [P, C], F32) for i in range(2)]
            st6 = psb("st6c", [P, 12], F32)
            mv2 = psb("mv2c", [P, 2], F32)
            ta = psb("tac", [P, 1], F32)
            tb = psb("tbc", [P, 1], F32)
            aw = psb("awc", [P, 1], F32)
            swo = psb("swoc", [P, 1], F32)
            swoi = psb("swoic", [P, 1], F32)
            ps_f = [s3k.enter_context(nc.psum_tensor(f"ps_f{i}", [P, C], F32))
                    for i in range(2)]

            wst_tok = [None, None]
            wst_tok[0] = E.dma("sync", "sx", lambda s: s.dma_start(
                wst[0][:, :], wout_h[0:P, :]))
            used_tok = [0, 0]
            cast_tok = 0
            for fc in range(NF):
                b = fc % 2
                if fc + 1 < NF:
                    nb = (fc + 1) % 2
                    E.wait("sync", "sdv", used_tok[nb])
                    wst_tok[nb] = E.dma("sync", "sx", lambda s, fc=fc, nb=nb: s.dma_start(
                        wst[nb][:, :], wout_h[(fc + 1) * P:(fc + 2) * P, :]))
                E.wait_tok("vector", wst_tok[b])
                E.wait("vector", "sac", cast_tok)
                E.do("vector", lambda v, b=b: v.tensor_reduce(
                    aw[:, :], wst[b][:, :], AX.X, ALU.max, apply_absolute_value=True))
                E.do("vector", lambda v: v.tensor_scalar_max(aw[:, :], aw[:, :], 1.0e-8))
                E.do("vector", lambda v: v.reciprocal(swo[:, :], aw[:, :]))
                E.do("vector", lambda v: v.tensor_scalar_mul(swo[:, :], swo[:, :], 127.0))
                E.do("vector", lambda v: v.tensor_scalar_mul(
                    swoi[:, :], aw[:, :], 1.0 / 127.0))
                E.do("vector", lambda v, b=b: v.tensor_scalar(
                    thq[:, 0:C], wst[b][:, :], swo[:, 0:1], None, ALU.mult))
                used_tok[b] = E.do("vector", lambda v: v.tensor_scalar(
                    thq[:, 0:C], thq[:, 0:C], MR, MR, ALU.add, ALU.subtract), inc="sdv")
                E.wait("scalar", "sdv", used_tok[b])
                cast_tok = E.do("scalar", lambda s, fc=fc: s.activation(
                    wqout[:, fc, :], thq[:, 0:C], AF.Copy, scale=swoi[:, 0:1]), inc="sac")
            wq_done = cast_tok
            E.flush_block()

            NB = 16
            hin_tok = [None, None]
            hin_tok[0] = E.dma("sync", "sx", lambda s: s.dma_start(
                bass.AP(hin[0], 0, [[NF * 256, P], [1, NF * 256]]),
                hscr_h[0, :, :]))
            xm2_tok = [None, None]
            xm2_tok[0] = E.dma("sync", "sx", lambda s: s.dma_start(
                xmb2[0][:, :], xmid_h[0:P, :]))
            hin_used = [0, 0]
            hq_rd = [0, 0]
            tq_prev = 0
            ev_f = [0, 0]
            out_dma = [[], []]
            xm_used = [0, 0]

            def emit_quant(blk):
                nonlocal tq_prev
                b = blk % 2
                E.wait_tok("scalar", hin_tok[b])
                E.wait("scalar", "sgp")
                E.wait("scalar", "sdv", tq_prev)
                hin_used[b] = E.do("scalar", lambda s, b=b: s.activation(
                    thq[:, :], bass.AP(hin[b], 0, [[NF * 256, P], [1, NF * 256]]),
                    AF.Copy, scale=s2_pp[:, 0:1]), inc="sac")
                E.wait("vector", "sac", hin_used[b])
                E.wait("vector", "smm", hq_rd[b])
                tq_prev = E.do("vector", lambda v, b=b: v.tensor_scalar(
                    bass.AP(hq[b], 0, [[NF * 256, P], [1, NF * 256]]),
                    thq[:, :], MR, MR, ALU.add, ALU.subtract), inc="sdv")
                return tq_prev

            hq_toks = [0, 0]
            hq_toks[0] = emit_quant(0)
            for blk in range(NB):
                b = blk % 2
                nb = (blk + 1) % 2
                if blk + 1 < NB:
                    E.wait("sync", "sac", hin_used[nb])
                    hin_tok[nb] = E.dma("sync", "sx",
                                        lambda s, blk=blk, nb=nb: s.dma_start(
                                            bass.AP(hin[nb], 0,
                                                    [[NF * 256, P], [1, NF * 256]]),
                                            hscr_h[blk + 1, :, :]))
                    E.wait("sync", "sdv", xm_used[nb])
                    xm2_tok[nb] = E.dma("sync", "sx", lambda s, blk=blk, nb=nb: s.dma_start(
                        xmb2[nb][:, :],
                        xmid_h[(blk + 1) * 2 * P:((blk + 1) * 2 + 1) * P, :]))
                    hq_toks[nb] = emit_quant(blk + 1)
                hq_tok = hq_toks[b]

                for sub in range(2):
                    tl = blk * 2 + sub
                    bank = sub
                    E.wait("tensor", "sdv", hq_tok)
                    E.wait("tensor", "sac", wq_done)
                    E.wait("tensor", "sdv", ev_f[bank])
                    mm_tok = None
                    for fc in range(NF):
                        for pc in range(2):
                            kw = dict(start=(fc == 0), stop=(fc == NF - 1))
                            lam = (lambda tt, fc=fc, sub=sub, pc=pc, bank=bank, kw=kw,
                                   b=b: tt.matmul(
                                       ps_f[bank][:, pc * 512:(pc + 1) * 512],
                                       hq[b][:, fc, sub * P:(sub + 1) * P],
                                       wqout[:, fc, pc * 512:(pc + 1) * 512], **kw))
                            if fc == NF - 1 and pc == 1:
                                mm_tok = E.do("tensor", lam, inc="smm")
                            else:
                                E.do("tensor", lam)
                    if sub == 1:
                        hq_rd[b] = mm_tok
                    E.wait("vector", "smm", mm_tok)
                    ev_f[bank] = E.do("vector", lambda v, bank=bank: v.scalar_tensor_tensor(
                        ffn[:, :], ps_f[bank][:, :], s2i_pp[:, 0:1], bout_t[:, :],
                        ALU.mult, ALU.add), inc="sdv")
                    E.do("vector", lambda v: v.bn_stats(st6[:, 0:6], ffn[:, 0:512]))
                    E.do("vector", lambda v: v.bn_stats(st6[:, 6:12], ffn[:, 512:1024]))
                    E.do("vector", lambda v: v.bn_aggr(mv2[:, :], st6[:, :]), inc="sdv")
                    E.wait("scalar", "sdv")
                    E.do("scalar", lambda s: s.activation(
                        ta[:, :], mv2[:, 1:2], AF.Ln, bias=EPS_LN))
                    E.do("scalar", lambda s: s.activation(
                        ta[:, :], ta[:, :], AF.Exp, scale=-0.5), inc="sac")
                    E.wait("vector", "sac")
                    E.do("vector", lambda v: v.tensor_mul(tb[:, :], mv2[:, 0:1], ta[:, :]))
                    E.wait_tok("vector", xm2_tok[b])
                    E.wait_tok("vector", out_dma[b])
                    E.do("vector", lambda v, b=b: v.tensor_scalar(
                        outb[b][:, :], xmb2[b][:, :], 1.0, tb[:, :],
                        ALU.mult, ALU.subtract))
                    out_tok = E.do("vector", lambda v, b=b: v.scalar_tensor_tensor(
                        outb[b][:, :], ffn[:, :], ta[:, 0:1], outb[b][:, :],
                        ALU.mult, ALU.add), inc="sdv")
                    if sub == 1:
                        xm_used[b] = out_tok
                    E.wait("gpsimd", "sdv", out_tok)
                    out_dma[b] = E.dma("gpsimd", "sgd", lambda g, tl=tl, b=b: g.dma_start(
                        out_h[tl * P:(tl + 1) * P, :], outb[b][:, :]))
                    if sub == 0:
                        E.wait("sync", "sdv", out_tok)
                        xm2_tok[b] = E.dma("sync", "sx", lambda s, blk=blk, b=b:
                                           s.dma_start(
                                               xmb2[b][:, :],
                                               xmid_h[(blk * 2 + 1) * P:
                                                      (blk * 2 + 2) * P, :]))

            E.wait_tok("sync", E.queue_tok("sx"))
            E.wait_tok("gpsimd", E.queue_tok("sgd"))
            E.flush_block()

    return nc


_NC_CACHE = {}


def kernel(L0_attn_x, L0_attn_w_qkv, L0_attn_ln_w, L0_attn_w_proj,
           L0_ffn_W_in, L0_ffn_b_in, L0_ffn_W_out, L0_ffn_b_out):
    x = np.ascontiguousarray(np.asarray(L0_attn_x))
    wqkv = np.ascontiguousarray(np.asarray(L0_attn_w_qkv))
    lnw = np.ascontiguousarray(np.asarray(L0_attn_ln_w)).reshape(1, C)
    wproj = np.ascontiguousarray(np.asarray(L0_attn_w_proj))
    win = np.ascontiguousarray(np.asarray(L0_ffn_W_in))
    bin_ = np.ascontiguousarray(np.asarray(L0_ffn_b_in)).reshape(1, FF)
    wout = np.ascontiguousarray(np.asarray(L0_ffn_W_out))
    bout = np.ascontiguousarray(np.asarray(L0_ffn_b_out)).reshape(1, C)

    if "nc" not in _NC_CACHE:
        _NC_CACHE["nc"] = build_nc()
    nc = _NC_CACHE["nc"]

    B = x.shape[0]
    per = B // N_CORES
    in_maps = []
    for c in range(N_CORES):
        shard = np.ascontiguousarray(x[c * per:(c + 1) * per].reshape(T, C))
        in_maps.append({
            "x": shard, "wqkv": wqkv, "lnw": lnw, "wproj": wproj,
            "win": win, "bin": bin_, "wout": wout, "bout": bout,
        })
    res = run_bass_kernel_spmd(nc, in_maps, core_ids=list(range(N_CORES)))
    out = np.concatenate(
        [np.asarray(r["out"]).reshape(per, 2048, C) for r in res.results], axis=0)
    return out


# version 16
# speedup vs baseline: 1.0970x; 1.0241x over previous
import contextlib
import numpy as np

import concourse.bass as bass
import concourse.mybir as mybir
from concourse.bass_utils import run_bass_kernel_spmd

F32 = mybir.dt.float32
BF16 = mybir.dt.bfloat16
AF = mybir.ActivationFunctionType
ALU = mybir.AluOpType
AX = mybir.AxisListType

P = 128
C = 1024
NCH = 8
T = 4096
NT = 32
NG = 16
H = 16
HD = 64
FF = 4096
NF = 32
MR = 12582912.0
NEGBIG = -1.0e30
EPS_LN = 1.0e-5
EPS_RMS = 1.0e-6
N_CORES = 8


class Emitter:

    ENGINES = ("sync", "tensor", "vector", "scalar", "gpsimd")

    def __init__(self, nc, stack):
        self.nc = nc
        self.q = {e: [] for e in self.ENGINES}
        self.sems = {}
        self.cnt = {}
        self.stack = stack
        self.dmaq = {}
        self.dmaidx = {}

    def sem(self, name):
        self.sems[name] = self.stack.enter_context(self.nc.semaphore(name))
        self.cnt[name] = 0

    def inc(self, name, n=1):
        self.cnt[name] += n
        return (self.sems[name], n)

    def tok(self, name):
        return self.cnt[name]

    def wait(self, eng, name, value=None):
        v = self.cnt[name] if value is None else value
        assert v <= self.cnt[name], (name, v, self.cnt[name])
        if v <= 0:
            return
        h = self.sems[name]
        self.q[eng].append(lambda e, h=h, v=v: e.wait_ge(h, v))

    DRAIN_ENGINES = ("vector", "scalar", "gpsimd")

    def do(self, eng, build, inc=None, n=1, drain=None):
        if drain is None:
            drain = eng in self.DRAIN_ENGINES
        if drain:
            self.q[eng].append(lambda e: e.drain())
        tgt = self.inc(inc, n) if inc else None

        def fn(e, build=build, tgt=tgt):
            ins = build(e)
            if tgt is not None:
                ins.then_inc(*tgt)

        self.q[eng].append(fn)
        return self.cnt[inc] if inc else None

    def dma_queue(self, qname, nlanes=4):
        lanes = []
        for i in range(nlanes):
            nm = f"{qname}{i}"
            self.sem(nm)
            lanes.append(nm)
        self.dmaq[qname] = lanes
        self.dmaidx[qname] = 0

    def dma(self, eng, qname, build):
        lanes = self.dmaq[qname]
        lane = lanes[self.dmaidx[qname] % len(lanes)]
        self.dmaidx[qname] += 1
        self.wait(eng, lane)
        v = self.do(eng, build, inc=lane, n=16)
        return [(lane, v)]

    def queue_tok(self, qname):
        return [(lane, self.cnt[lane]) for lane in self.dmaq[qname]]

    def wait_tok(self, eng, tok):
        for nm, v in tok:
            self.wait(eng, nm, v)

    def flush_block(self):
        qs = {e: fns for e, fns in self.q.items() if fns}
        with self.nc.Block() as blk:
            for e, fns in qs.items():
                def runner(engine, fns=fns):
                    for f in fns:
                        f(engine)
                getattr(blk, e)(runner)
        self.q = {e: [] for e in self.ENGINES}


def build_nc(phases=4):
    nc = bass.Bass("TRN2", target_bir_lowering=False)

    x_h = nc.dram_tensor("x", [T, C], F32, kind="ExternalInput")
    wqkv_h = nc.dram_tensor("wqkv", [C, 3 * C], F32, kind="ExternalInput")
    lnw_h = nc.dram_tensor("lnw", [1, C], F32, kind="ExternalInput")
    wproj_h = nc.dram_tensor("wproj", [C, C], F32, kind="ExternalInput")
    win_h = nc.dram_tensor("win", [C, FF], F32, kind="ExternalInput")
    bin_h = nc.dram_tensor("bin", [1, FF], F32, kind="ExternalInput")
    wout_h = nc.dram_tensor("wout", [FF, C], F32, kind="ExternalInput")
    bout_h = nc.dram_tensor("bout", [1, C], F32, kind="ExternalInput")
    out_h = nc.dram_tensor("out", [T, C], F32, kind="ExternalOutput")

    xmid_h = nc.dram_tensor("xmid_scr", [T, C], F32)
    hscr_h = nc.dram_tensor("h_scr", [16, P, NF * 256], BF16)
    cc1_in = nc.dram_tensor("cc1_in", [1, 8], F32)
    cc1_out = nc.dram_tensor("cc1_out", [1, 8], F32)
    cc2_in = nc.dram_tensor("cc2_in", [1, 8], F32)
    cc2_out = nc.dram_tensor("cc2_out", [1, 8], F32)

    with contextlib.ExitStack() as gstack:
        E = Emitter(nc, gstack)
        for s in ("smm", "sdv", "sac", "sgp", "scc"):
            E.sem(s)
        E.dma_queue("sx", 6)
        E.dma_queue("sa", 3)
        E.dma_queue("sgd", 4)

        sbg = lambda name, shape, dt: gstack.enter_context(nc.sbuf_tensor(name, shape, dt))

        bout_t = sbg("bout_t", [P, C], F32)
        binp = sbg("binp", [P, NF], F32)
        idb = sbg("idb", [P, P], BF16)
        idf = sbg("idf", [P, P], F32)
        m01 = sbg("m01", [P, P], F32)
        m01q1 = sbg("m01q1", [P, 256], F32)
        c127 = sbg("c127", [P, 1], F32)
        c32k = sbg("c32k", [1, 1], F32)
        ceps1 = sbg("ceps1", [P, 1], F32)
        ceps2 = sbg("ceps2", [P, 1], F32)
        cm05 = sbg("cm05", [P, 1], F32)
        nc.const_aps.register(F32, EPS_LN, ceps1[:, :])
        nc.const_aps.register(F32, EPS_RMS, ceps2[:, :])
        nc.const_aps.register(F32, -0.5, cm05[:, :])
        s1_pp = sbg("s1_pp", [P, 1], F32)
        s1i_pp = sbg("s1i_pp", [P, 1], F32)
        s2_pp = sbg("s2_pp", [P, 1], F32)
        s2i_pp = sbg("s2i_pp", [P, 1], F32)
        am_acc = sbg("am_acc", [P, 1], F32)
        am_row = sbg("am_row", [1, P], F32)
        ccsb = sbg("ccsb", [1, 8], F32)
        sc_a = sbg("sc_a", [1, 1], F32)
        sc_b = sbg("sc_b", [1, 1], F32)
        sc_c = sbg("sc_c", [1, 1], F32)
        rln = sbg("rln", [P, NT], F32)
        mrln = sbg("mrln", [P, NT], F32)
        mvst = sbg("mvst", [P, 2 * NT], F32)
        gmax = sbg("gmax", [P, NT], F32)

        with contextlib.ExitStack() as pstack:
            psb = lambda n, s, d: pstack.enter_context(nc.sbuf_tensor(n, s, d))
            onesf = psb("onesf", [P, P], F32)
            zerof = psb("zerof", [P, P], F32)
            bstage = psb("bstage", [NF, P], F32)
            bps = pstack.enter_context(nc.psum_tensor("bps", [P, NF], F32))

            tok_ld = []
            tok_ld += E.dma("sync", "sx", lambda s: s.dma_start(
                bout_t[:, :], bass.AP(bout_h, 0, [[0, P], [1, C]])))
            tok_ld += E.dma("sync", "sx", lambda s: s.dma_start(
                bstage[:, :], bass.AP(bin_h, 0, [[P, NF], [1, P]])))

            E.do("gpsimd", lambda g: g.memset(onesf[:, :], 1.0), inc="sgp")
            ms_tok = E.do("gpsimd", lambda g: g.memset(zerof[:, :], 0.0), inc="sgp")
            E.wait("gpsimd", "sgp", ms_tok)
            E.do("gpsimd", lambda g: g.affine_select(
                idf[:, :], onesf[:, :], [[1, P]], ALU.is_equal, 0.0,
                channel_multiplier=-1), inc="sgp")
            E.do("gpsimd", lambda g: g.affine_select(
                m01[:, :], zerof[:, :], [[-1, P]], ALU.is_ge, NEGBIG,
                channel_multiplier=1), inc="sgp")
            E.do("gpsimd", lambda g: g.memset(m01q1[:, 0:P], 0.0), inc="sgp")
            E.do("gpsimd", lambda g: g.affine_select(
                m01q1[:, P:256], zerof[:, :], [[-1, P]], ALU.is_ge, NEGBIG,
                channel_multiplier=1), inc="sgp")
            E.do("gpsimd", lambda g: g.memset(c127[:, :], 127.0), inc="sgp")
            E.do("gpsimd", lambda g: g.memset(ceps1[:, :], EPS_LN), inc="sgp")
            E.do("gpsimd", lambda g: g.memset(ceps2[:, :], EPS_RMS), inc="sgp")
            E.do("gpsimd", lambda g: g.memset(cm05[:, :], -0.5), inc="sgp")
            E.do("gpsimd", lambda g: g.memset(c32k[:, :], 32767.0), inc="sgp")
            tok_gp = E.tok("sgp")

            E.wait("vector", "sgp", tok_gp)
            E.wait_tok("vector", tok_ld)
            E.do("vector", lambda v: v.tensor_copy(idb[:, :], idf[:, :]), inc="sdv")

            E.wait("tensor", "sgp", tok_gp)
            E.wait_tok("tensor", tok_ld)
            E.do("tensor", lambda t: t.transpose(
                bps[:, :], bstage[:, :], idf[0:NF, 0:NF]), inc="smm")
            E.wait("vector", "smm")
            E.do("vector", lambda v: v.tensor_copy(binp[:, :], bps[:, :]), inc="sdv")
            E.flush_block()

        with contextlib.ExitStack() as s1k:
            psb = lambda n, s, d: s1k.enter_context(nc.sbuf_tensor(n, s, d))
            pps = lambda n, s, d: s1k.enter_context(nc.psum_tensor(n, s, d))

            wqkv_sb = psb("wqkv_sb", [P, NCH, 3 * C], BF16)
            wproj_sb = psb("wproj_sb", [P, NCH, C], BF16)
            lnwq_t = psb("lnwq_t", [P, C], F32)
            lnwk_t = psb("lnwk_t", [P, C], F32)
            wstage = [psb(f"wstage{i}", [P, 3 * C], F32) for i in range(2)]

            xbuf = [[psb(f"xb{pb}_{t}", [P, C], F32) for t in range(2)] for pb in range(2)]
            x1c = [psb(f"x1c{t}", [P, C], BF16) for t in range(2)]
            qnb = [psb(f"qnb{t}", [P, C], BF16) for t in range(2)]
            knb = [psb(f"knb{t}", [P, C], BF16) for t in range(2)]
            x1T = psb("x1T", [P, 2, NCH, P], BF16)
            v_sb = psb("v_sb", [P, 2, C], BF16)
            qT = psb("qT", [P, NCH, 256], BF16)
            kT = psb("kT", [P, NCH, 256], BF16)
            pP = [psb(f"pP{i}", [P, 2, 384], F32) for i in range(2)]
            pmP = [psb(f"pmP{i}", [P, 2, 384], F32) for i in range(2)]
            pn = [psb(f"pn{i}", [P, 2, 384], BF16) for i in range(2)]
            pTb = [psb(f"pTb{i}", [P, 768], BF16) for i in range(2)]
            y_sb = psb("y_sb", [P, NCH, 256], BF16)
            xm_sb = [[psb(f"xm{pb}_{t}", [P, C], F32) for t in range(2)] for pb in range(2)]
            xp_t = [psb(f"xp{t}", [P, C], F32) for t in range(2)]
            st6 = psb("st6", [P, 12], F32)
            mv2 = psb("mv2", [P, 2], F32)
            t128a = psb("t128a", [P, 1], F32)
            t128b = psb("t128b", [P, 1], F32)
            tmx = psb("tmx", [P, 1], F32)
            tmn = psb("tmn", [P, 1], F32)
            rqk = [psb(f"rqk{i}", [P, 1], F32) for i in range(2)]
            nmq = [psb(f"nmq{i}", [P, 4], F32) for i in range(2)]
            rsq = [psb(f"rsq{i}", [P, 4], F32) for i in range(2)]

            ps_big = pps("ps_big", [P, 3072], F32)
            ps_tp = [pps(f"ps_tp{i}", [P, 768], BF16) for i in range(2)]

            def BK(i, off=0, n=512):
                return ps_big[:, i * 512 + off:i * 512 + off + n]

            x_need = [[], []]
            x_need[0] = (E.dma("sync", "sx", lambda s: s.dma_start(
                             xbuf[0][0][:, :], x_h[0:P, :]))
                         + E.dma("sync", "sx", lambda s: s.dma_start(
                             xbuf[0][1][:, :], x_h[P:2 * P, :])))

            lt = E.dma("sync", "sx", lambda s: s.dma_start(
                lnwq_t[:, :], bass.AP(lnw_h, 0, [[0, P], [1, C]])))
            lt += E.dma("sync", "sx", lambda s: s.dma_start(
                lnwk_t[:, :], bass.AP(lnw_h, 0, [[0, P], [1, C]])))
            E.wait_tok("vector", lt)
            E.do("vector", lambda v: v.tensor_scalar_mul(
                lnwq_t[:, :], lnwq_t[:, :], 0.125), inc="sdv")
            cast_free = [("sdv", 0)] * 4
            wqkv_cast = [None] * 6
            wproj_cast = None
            for pc in range(6):
                for cc in range(NCH):
                    sb4 = (pc * NCH + cc) % 4
                    b, half = sb4 % 2, sb4 // 2
                    E.wait("sync", cast_free[sb4][0], cast_free[sb4][1])
                    wt = E.dma("sync", "sx", lambda s, cc=cc, pc=pc, b=b, half=half:
                               s.dma_start(
                                   wstage[b][:, half * 512:(half + 1) * 512],
                                   wqkv_h[cc * P:(cc + 1) * P,
                                          pc * 512:(pc + 1) * 512]))
                    eng, sem = (("vector", "sdv") if (pc * NCH + cc) % 2 == 0
                                else ("scalar", "sac"))
                    E.wait_tok(eng, wt)
                    if eng == "vector":
                        tok = E.do(eng, lambda v, cc=cc, pc=pc, b=b, half=half:
                                   v.tensor_copy(
                                       wqkv_sb[:, cc, pc * 512:(pc + 1) * 512],
                                       wstage[b][:, half * 512:(half + 1) * 512]),
                                   inc=sem, drain=False)
                    else:
                        tok = E.do(eng, lambda s, cc=cc, pc=pc, b=b, half=half:
                                   s.copy(
                                       wqkv_sb[:, cc, pc * 512:(pc + 1) * 512],
                                       wstage[b][:, half * 512:(half + 1) * 512]),
                                   inc=sem, drain=False)
                    cast_free[sb4] = (sem, tok)
                    wqkv_cast[pc] = (sem, tok)
            for cc in range(NCH):
                for half2 in range(2):
                    sb4 = (cc * 2 + half2) % 4
                    b, half = sb4 % 2, sb4 // 2
                    E.wait("sync", cast_free[sb4][0], cast_free[sb4][1])
                    wt = E.dma("sync", "sx", lambda s, cc=cc, half2=half2, b=b,
                               half=half: s.dma_start(
                                   wstage[b][:, half * 512:(half + 1) * 512],
                                   wproj_h[cc * P:(cc + 1) * P,
                                           half2 * 512:(half2 + 1) * 512]))
                    eng, sem = (("vector", "sdv") if (cc * 2 + half2) % 2 == 0
                                else ("scalar", "sac"))
                    E.wait_tok(eng, wt)
                    if eng == "vector":
                        tok = E.do(eng, lambda v, cc=cc, half2=half2, b=b, half=half:
                                   v.tensor_copy(
                                       wproj_sb[:, cc, half2 * 512:(half2 + 1) * 512],
                                       wstage[b][:, half * 512:(half + 1) * 512]),
                                   inc=sem, drain=False)
                    else:
                        tok = E.do(eng, lambda s, cc=cc, half2=half2, b=b, half=half:
                                   s.copy(
                                       wproj_sb[:, cc, half2 * 512:(half2 + 1) * 512],
                                       wstage[b][:, half * 512:(half + 1) * 512]),
                                   inc=sem, drain=False)
                    cast_free[sb4] = (sem, tok)
                    wproj_cast = (sem, tok)


            ev_bank = [("sdv", 0)] * 6
            evtp = [("sdv", 0), ("sac", 0)]
            nm_rd = [0, 0]
            pp_rd = [0, 0]
            y_rd = [0, 0]
            buf_free = {}
            xm_guard = [[[], []], [[], []]]
            y_proj_rd = 0
            v_y_rd = 0
            t128_rd = 0
            rqk_rd = [0, 0]

            def tbatch(srcs, psum_bank, dst_ap_fn, eng, sem):
                pst = ps_tp[psum_bank]
                n = len(srcs)
                for i in range(n - 1):
                    E.do("tensor", lambda tt, i=i: tt.transpose(
                        pst[:, i * P:(i + 1) * P], srcs[i](), idb[:, :]))
                tp = E.do("tensor", lambda tt, i=n - 1: tt.transpose(
                    pst[:, i * P:(i + 1) * P], srcs[i](), idb[:, :]), inc="smm")
                E.wait(eng, "smm", tp)
                if eng == "scalar":
                    ev = E.do(eng, lambda e: e.copy(
                        dst_ap_fn(), pst[:, 0:n * P]), inc=sem)
                else:
                    ev = E.do(eng, lambda e: e.tensor_copy(
                        dst_ap_fn(), pst[:, 0:n * P]), inc=sem)
                return (sem, ev)

            ln1_tok = [0, 0]

            def emit_ln1(gg, t):
                ppb = gg % 2
                xt = xbuf[ppb][t]
                E.wait_tok("vector", x_need[ppb])
                E.do("vector", lambda v, xt=xt: v.bn_stats(st6[:, 0:6], xt[:, 0:512]))
                E.do("vector", lambda v, xt=xt: v.bn_stats(st6[:, 6:12],
                                                           xt[:, 512:1024]))
                E.do("vector", lambda v: v.bn_aggr(mv2[:, :], st6[:, :]), inc="sdv")
                E.wait("scalar", "sdv")
                E.wait("scalar", "sdv", t128_rd)
                E.do("scalar", lambda s: s.activation(
                    t128a[:, :], mv2[:, 1:2], AF.Ln, bias=EPS_LN))
                E.do("scalar", lambda s: s.activation(
                    t128a[:, :], t128a[:, :], AF.Exp, scale=-0.5), inc="sac")
                E.wait("vector", "sac")
                E.do("vector", lambda v: v.tensor_mul(
                    t128b[:, :], mv2[:, 0:1], t128a[:, :]))
                for sem, tk in buf_free.get(("x1c", t), []):
                    E.wait("vector", sem, tk)
                tok = E.do("vector", lambda v, xt=xt, t=t: v.tensor_scalar(
                    x1c[t][:, :], xt[:, :], t128a[:, :], t128b[:, :],
                    ALU.mult, ALU.subtract), inc="sdv")
                ln1_tok[t] = tok
                return tok

            t128_rd = emit_ln1(0, 0)
            t128_rd = emit_ln1(0, 1)

            for g in range(NG):
                pb = g % 2
                if g + 1 < NG:
                    npb = (g + 1) % 2
                    E.wait("sync", "sdv")
                    tks = []
                    for t in range(2):
                        tks += E.dma("sync", "sx", lambda s, g=g, t=t, npb=npb: s.dma_start(
                            xbuf[npb][t][:, :],
                            x_h[(g + 1) * 256 + t * P:(g + 1) * 256 + (t + 1) * P, :]))
                    x_need[npb] = tks

                qn_tok = [0, 0]
                kn_tok = [0, 0]
                for t in range(2):
                    xt = xbuf[pb][t]
                    x1_tok = ln1_tok[t]

                    evs = []
                    for half in range(2):
                        bank = half
                        E.wait("tensor", "sdv", x1_tok)
                        E.wait("tensor", evtp[bank][0], evtp[bank][1])
                        eng = "vector" if half == 0 else "scalar"
                        sem = "sdv" if half == 0 else "sac"
                        evtp[bank] = tbatch(
                            [(lambda t=t, cc=half * 4 + i: x1c[t][:, cc * P:(cc + 1) * P])
                             for i in range(4)],
                            bank,
                            lambda t=t, half=half: x1T[:, t, half * 4:(half + 1) * 4, :],
                            eng, sem)
                        evs.append(evtp[bank])
                    buf_free[("x1c", t)] = list(evs)

                    x1T_ready = list(evs)
                    qkv_tok = [0] * 6
                    for pc in range(6):
                        E.wait("tensor", ev_bank[pc][0], ev_bank[pc][1])
                        if g == 0 and t == 0:
                            E.wait("tensor", wqkv_cast[pc][0], wqkv_cast[pc][1])
                        for sem, tk in x1T_ready:
                            E.wait("tensor", sem, tk)
                        x1T_ready = []
                        for cc in range(NCH):
                            kw = dict(start=(cc == 0), stop=(cc == NCH - 1))
                            lam = (lambda tt, t=t, cc=cc, pc=pc, kw=kw:
                                   tt.matmul(BK(pc), x1T[:, t, cc, :],
                                             wqkv_sb[:, cc, pc * 512:(pc + 1) * 512],
                                             **kw))
                            if cc == NCH - 1:
                                qkv_tok[pc] = E.do("tensor", lam, inc="smm")
                            else:
                                E.do("tensor", lam)

                    E.wait("vector", "smm", qkv_tok[1])
                    E.do("vector", lambda v: v.bn_stats(st6[:, 0:6], BK(0)))
                    E.do("vector", lambda v: v.bn_stats(st6[:, 6:12], BK(1)))
                    E.do("vector", lambda v: v.bn_aggr(mv2[:, :], st6[:, :]))
                    E.do("vector", lambda v: v.tensor_mul(t128b[:, :], mv2[:, 0:1],
                                                          mv2[:, 0:1]))
                    E.do("vector", lambda v: v.tensor_add(tmx[:, :], mv2[:, 1:2],
                                                          t128b[:, :]), inc="sdv")
                    E.wait("scalar", "sdv")
                    E.wait("scalar", "sdv", rqk_rd[0])
                    E.do("scalar", lambda s: s.activation(
                        rqk[0][:, :], tmx[:, :], AF.Ln, bias=EPS_RMS))
                    E.do("scalar", lambda s: s.activation(
                        rqk[0][:, :], rqk[0][:, :], AF.Exp, scale=-0.5), inc="sac")
                    E.wait("vector", "sac")
                    for sem, tk in buf_free.get(("qnb", t), []):
                        E.wait("vector", sem, tk)
                    qn_tok[t] = E.do("vector", lambda v, t=t: v.scalar_tensor_tensor(
                        qnb[t][:, :], ps_big[:, 0:1024], rqk[0][:, :], lnwq_t[:, :],
                        ALU.mult, ALU.mult), inc="sdv")
                    rqk_rd[0] = qn_tok[t]
                    ev_bank[0] = ev_bank[1] = ("sdv", qn_tok[t])

                    E.wait("vector", "smm", qkv_tok[3])
                    E.do("vector", lambda v: v.bn_stats(st6[:, 0:6], BK(2)))
                    E.do("vector", lambda v: v.bn_stats(st6[:, 6:12], BK(3)))
                    E.do("vector", lambda v: v.bn_aggr(mv2[:, :], st6[:, :]))
                    E.do("vector", lambda v: v.tensor_mul(t128b[:, :], mv2[:, 0:1],
                                                          mv2[:, 0:1]))
                    E.do("vector", lambda v: v.tensor_add(tmx[:, :], mv2[:, 1:2],
                                                          t128b[:, :]), inc="sdv")
                    E.wait("scalar", "sdv")
                    E.wait("scalar", "sdv", rqk_rd[1])
                    E.do("scalar", lambda s: s.activation(
                        rqk[1][:, :], tmx[:, :], AF.Ln, bias=EPS_RMS))
                    E.do("scalar", lambda s: s.activation(
                        rqk[1][:, :], rqk[1][:, :], AF.Exp, scale=-0.5), inc="sac")
                    E.wait("vector", "sac")
                    for sem, tk in buf_free.get(("knb", t), []):
                        E.wait("vector", sem, tk)
                    kn_tok[t] = E.do("vector", lambda v, t=t: v.scalar_tensor_tensor(
                        knb[t][:, :], ps_big[:, 1024:2048], rqk[1][:, :], lnwk_t[:, :],
                        ALU.mult, ALU.mult), inc="sdv")
                    rqk_rd[1] = kn_tok[t]
                    ev_bank[2] = ev_bank[3] = ("sdv", kn_tok[t])

                    E.wait("scalar", "smm", qkv_tok[5])
                    E.wait("scalar", "smm", v_y_rd)
                    E.do("scalar", lambda s, t=t: s.copy(v_sb[:, t, 0:512], BK(4)))
                    vev = E.do("scalar", lambda s, t=t: s.copy(v_sb[:, t, 512:1024],
                                                               BK(5)), inc="sac")
                    ev_bank[4] = ev_bank[5] = ("sac", vev)

                qkT_toks = []
                for t in range(2):
                    E.wait("sync", "sdv", qn_tok[t])
                    tks = []
                    for cc in range(NCH):
                        tks += E.dma("sync", "sx", lambda s, t=t, cc=cc: s.dma_start(
                            qT[:, cc, t * P:(t + 1) * P],
                            qnb[t][:, cc * P:(cc + 1) * P], transpose=True))
                    buf_free[("qnb", t)] = tks
                    qkT_toks += tks
                for t in range(2):
                    evs = []
                    for half in range(2):
                        bank = half
                        E.wait("tensor", "sdv", kn_tok[t])
                        E.wait("tensor", evtp[bank][0], evtp[bank][1])
                        eng = "vector" if half == 0 else "scalar"
                        sem = "sdv" if half == 0 else "sac"
                        evtp[bank] = tbatch(
                            [(lambda t=t, cc=half * 4 + i:
                              knb[t][:, cc * P:(cc + 1) * P]) for i in range(4)],
                            bank,
                            lambda half=half, t=t: bass.AP(
                                kT, (half * 4) * 256 + t * P,
                                [[NCH * 256, P], [256, 4], [1, P]]),
                            eng, sem)
                        evs.append(evtp[bank])
                    buf_free[("knb", t)] = list(evs)
                    qkT_toks.append(evtp[0])
                    qkT_toks.append(evtp[1])

                qkT_done = qkT_toks

                def emit_scores(hp):
                    for sem, tk in qkT_done:
                        E.wait("tensor", sem, tk)
                    qkT_done.clear()
                    E.wait("tensor", ev_bank[0][0], ev_bank[0][1])
                    E.wait("tensor", ev_bank[1][0], ev_bank[1][1])
                    E.do("tensor", lambda tt, hp=hp: tt.matmul(
                        BK(0, 0, P), qT[0:HD, hp, 0:P], kT[0:HD, hp, 0:P],
                        start=True, stop=True, tile_position=(0, 0)))
                    E.do("tensor", lambda tt, hp=hp: tt.matmul(
                        BK(1, 0, P), qT[HD:2 * HD, hp, 0:P], kT[HD:2 * HD, hp, 0:P],
                        start=True, stop=True, tile_position=(HD, 0)))
                    E.do("tensor", lambda tt, hp=hp: tt.matmul(
                        BK(0, P, 256), qT[0:HD, hp, P:256], kT[0:HD, hp, 0:256],
                        start=True, stop=True, tile_position=(0, 0)))
                    return E.do("tensor", lambda tt, hp=hp: tt.matmul(
                        BK(1, P, 256), qT[HD:2 * HD, hp, P:256],
                        kT[HD:2 * HD, hp, 0:256],
                        start=True, stop=True, tile_position=(HD, 0)), inc="smm")

                s_tok = emit_scores(0)
                pend = [None]

                def emit_tail(ctx):
                    hp, par, pE, pnp, rs, exp_tok = ctx
                    E.wait("vector", "sac", exp_tok)
                    E.do("vector", lambda v, rs=rs: v.reciprocal(rs[:, :], rs[:, :]),
                         inc="sdv", drain=False)
                    E.wait("scalar", "sdv")
                    E.do("vector", lambda v, pE=pE, pnp=pnp, rs=rs: v.tensor_scalar_mul(
                        pnp[:, 0, 0:P], pE[:, 0, 0:P], rs[:, 0:1]))
                    nv = E.do("vector", lambda v, pE=pE, pnp=pnp, rs=rs:
                              v.tensor_scalar_mul(
                                  pnp[:, 0, P:384], pE[:, 0, P:384], rs[:, 2:3]),
                              inc="sdv", drain=False)
                    E.do("scalar", lambda s, pE=pE, pnp=pnp, rs=rs: s.activation(
                        pnp[:, 1, 0:P], pE[:, 1, 0:P], AF.Copy, scale=rs[:, 1:2]),
                        drain=False)
                    na = E.do("scalar", lambda s, pE=pE, pnp=pnp, rs=rs: s.activation(
                        pnp[:, 1, P:384], pE[:, 1, P:384], AF.Copy, scale=rs[:, 3:4]),
                        inc="sac", drain=False)
                    pp_rd[par] = nv

                    E.wait("tensor", "sdv", nv)
                    E.wait("tensor", "sac", na)
                    E.wait("tensor", evtp[par][0], evtp[par][1])
                    pst = ps_tp[par]
                    for i, srcf in enumerate((
                            lambda pnp=pnp: pnp[:, 0, 0:P],
                            lambda pnp=pnp: pnp[:, 1, 0:P],
                            lambda pnp=pnp: pnp[:, 0, P:256],
                            lambda pnp=pnp: pnp[:, 0, 256:384],
                            lambda pnp=pnp: pnp[:, 1, P:256])):
                        E.do("tensor", lambda tt, i=i, srcf=srcf, pst=pst: tt.transpose(
                            pst[:, i * P:(i + 1) * P], srcf(), idb[:, :]))
                    tp = E.do("tensor", lambda tt, pst=pst, pnp=pnp: tt.transpose(
                        pst[:, 5 * P:6 * P], pnp[:, 1, 256:384], idb[:, :]), inc="smm")
                    eng = "vector" if par == 0 else "scalar"
                    sem = "sdv" if par == 0 else "sac"
                    E.wait(eng, "smm", tp)
                    E.wait(eng, "smm", y_rd[par])
                    if eng == "scalar":
                        ev = E.do(eng, lambda e, par=par, pst=pst: e.copy(
                            pTb[par][:, :], pst[:, :]), inc=sem)
                    else:
                        ev = E.do(eng, lambda e, par=par, pst=pst: e.tensor_copy(
                            pTb[par][:, :], pst[:, :]), inc=sem)
                    evtp[par] = (sem, ev)

                    yb = 2 + par
                    E.wait("tensor", evtp[par][0], evtp[par][1])
                    E.wait("tensor", ev_bank[yb][0], ev_bank[yb][1])
                    for j in range(2):
                        h = 2 * hp + j
                        E.do("tensor", lambda tt, h=h, j=j, par=par, yb=yb: tt.matmul(
                            ps_big[j * HD:(j + 1) * HD, yb * 512:yb * 512 + P],
                            v_sb[:, 0, h * HD:(h + 1) * HD],
                            pTb[par][:, j * P:(j + 1) * P],
                            start=True, stop=True, tile_position=(0, j * HD)))
                    y_tok = None
                    for j in range(2):
                        h = 2 * hp + j
                        for kt in range(2):
                            kw = dict(start=(kt == 0), stop=(kt == 1),
                                      tile_position=(0, j * HD))
                            lam = (lambda tt, h=h, j=j, kt=kt, kw=kw, par=par, yb=yb:
                                   tt.matmul(
                                       ps_big[j * HD:(j + 1) * HD,
                                              yb * 512 + P:yb * 512 + 256],
                                       v_sb[:, kt, h * HD:(h + 1) * HD],
                                       pTb[par][:, 256 + (2 * j + kt) * P:
                                                256 + (2 * j + kt + 1) * P], **kw))
                            if j == 1 and kt == 1:
                                y_tok = E.do("tensor", lam, inc="smm")
                            else:
                                E.do("tensor", lam)
                    nonlocal_state["v_y_rd"] = y_tok
                    y_rd[par] = y_tok
                    if par == 1:
                        E.wait("scalar", "smm", y_tok)
                        E.wait("scalar", "smm", y_proj_rd)
                        yev = E.do("scalar", lambda s, hp=hp: s.copy(
                            y_sb[:, hp - 1:hp + 1, :],
                            bass.AP(ps_big, 1024, [[3072, P], [512, 2], [1, 256]])),
                            inc="sac")
                        ev_bank[2] = ev_bank[3] = ("sac", yev)

                nonlocal_state = {"v_y_rd": v_y_rd}
                for hp in range(NCH):
                    par = hp % 2
                    sP = pmP[par]
                    pE = pP[par]
                    pnp = pn[par]
                    nm = nmq[par]
                    rs = rsq[par]
                    E.wait("vector", "smm", s_tok)
                    E.wait("vector", "sac", nm_rd[par])
                    E.wait("vector", "sdv", pp_rd[par])
                    E.do("vector", lambda v, sP=sP: v.tensor_tensor(
                        bass.AP(sP, 0, [[768, P], [384, 2], [1, P]]),
                        bass.AP(ps_big, 0, [[3072, P], [512, 2], [1, P]]),
                        bass.AP(m01, 0, [[P, P], [0, 2], [1, P]]), ALU.add))
                    mk_tok = E.do("vector", lambda v, sP=sP: v.tensor_tensor(
                        bass.AP(sP, P, [[768, P], [384, 2], [1, 256]]),
                        bass.AP(ps_big, P, [[3072, P], [512, 2], [1, 256]]),
                        bass.AP(m01q1, 0, [[256, P], [0, 2], [1, 256]]), ALU.add),
                        inc="sdv", drain=False)
                    ev_bank[0] = ev_bank[1] = ("sdv", mk_tok)

                    if hp + 1 < NCH:
                        s_tok = emit_scores(hp + 1)

                    E.do("vector", lambda v, sP=sP, nm=nm: v.tensor_reduce(
                        nm[:, 0:2], bass.AP(sP, 0, [[768, P], [384, 2], [1, P]]),
                        AX.X, ALU.max, negate=True))
                    nm_tok = E.do("vector", lambda v, sP=sP, nm=nm: v.tensor_reduce(
                        nm[:, 2:4], bass.AP(sP, P, [[768, P], [384, 2], [1, 256]]),
                        AX.X, ALU.max, negate=True), inc="sdv", drain=False)

                    E.wait("scalar", "sdv", nm_tok)
                    E.do("scalar", lambda s, sP=sP, pE=pE, nm=nm, rs=rs: s.activation(
                        pE[:, 0, 0:P], sP[:, 0, 0:P], AF.Exp, bias=nm[:, 0:1],
                        accum_out=rs[:, 0:1]), drain=False)
                    E.do("scalar", lambda s, sP=sP, pE=pE, nm=nm, rs=rs: s.activation(
                        pE[:, 1, 0:P], sP[:, 1, 0:P], AF.Exp, bias=nm[:, 1:2],
                        accum_out=rs[:, 1:2]), drain=False)
                    E.do("scalar", lambda s, sP=sP, pE=pE, nm=nm, rs=rs: s.activation(
                        pE[:, 0, P:384], sP[:, 0, P:384], AF.Exp, bias=nm[:, 2:3],
                        accum_out=rs[:, 2:3]), drain=False)
                    exp_tok = E.do("scalar", lambda s, sP=sP, pE=pE, nm=nm, rs=rs:
                                   s.activation(
                                       pE[:, 1, P:384], sP[:, 1, P:384], AF.Exp,
                                       bias=nm[:, 3:4], accum_out=rs[:, 3:4]),
                                   inc="sac", drain=False)
                    nm_rd[par] = exp_tok

                    if pend[0] is not None:
                        emit_tail(pend[0])
                    pend[0] = (hp, par, pE, pnp, rs, exp_tok)
                emit_tail(pend[0])
                pend[0] = None
                v_y_rd = nonlocal_state["v_y_rd"]

                if g + 1 < NG:
                    t128_rd = emit_ln1(g + 1, 0)
                    t128_rd = emit_ln1(g + 1, 1)

                y_ready = ("sac", E.tok("sac"))
                proj_tok = [0, 0]
                proj_banks = [(4, 5), (2, 3)]
                for t in range(2):
                    ba, bb = proj_banks[t]
                    if g == 0 and t == 0:
                        E.wait("tensor", wproj_cast[0], wproj_cast[1])
                    E.wait("tensor", y_ready[0], y_ready[1])
                    E.wait("tensor", ev_bank[ba][0], ev_bank[ba][1])
                    E.wait("tensor", ev_bank[bb][0], ev_bank[bb][1])
                    for hp in range(NCH):
                        last = hp == NCH - 1
                        kw0 = dict(start=(hp == 0), stop=last)
                        E.do("tensor", lambda tt, hp=hp, t=t, ba=ba, kw=kw0: tt.matmul(
                            BK(ba), y_sb[:, hp, t * P:(t + 1) * P],
                            wproj_sb[:, hp, 0:512], **kw))
                        lam = (lambda tt, hp=hp, t=t, bb=bb, kw=kw0: tt.matmul(
                            BK(bb), y_sb[:, hp, t * P:(t + 1) * P],
                            wproj_sb[:, hp, 512:1024], **kw))
                        if last:
                            proj_tok[t] = E.do("tensor", lam, inc="smm")
                        else:
                            E.do("tensor", lam)
                y_proj_rd = proj_tok[1]

                for t in range(2):
                    ba, bb = proj_banks[t]
                    tl = g * 2 + t
                    E.wait("vector", "smm", proj_tok[t])
                    E.do("vector", lambda v, ba=ba: v.bn_stats(st6[:, 0:6], BK(ba)))
                    E.do("vector", lambda v, bb=bb: v.bn_stats(st6[:, 6:12], BK(bb)))
                    E.do("vector", lambda v: v.bn_aggr(mv2[:, :], st6[:, :]), inc="sdv")
                    E.wait("scalar", "sdv")
                    E.wait("scalar", "sdv", t128_rd)
                    E.do("scalar", lambda s: s.activation(
                        t128a[:, :], mv2[:, 1:2], AF.Ln, bias=EPS_LN))
                    E.do("scalar", lambda s: s.activation(
                        t128a[:, :], t128a[:, :], AF.Exp, scale=-0.5), inc="sac")
                    E.wait("vector", "sac")
                    E.do("vector", lambda v: v.tensor_mul(
                        t128b[:, :], mv2[:, 0:1], t128a[:, :]))
                    E.do("vector", lambda v, pb=pb, t=t: v.tensor_scalar(
                        xp_t[t][:, :], xbuf[pb][t][:, :], 1.0, t128b[:, :],
                        ALU.mult, ALU.subtract))
                    for sem, tk in xm_guard[pb][t]:
                        E.wait("vector", sem, tk)
                    E.do("vector", lambda v, pb=pb, t=t, ba=ba: v.scalar_tensor_tensor(
                        xm_sb[pb][t][:, 0:512], BK(ba), t128a[:, :],
                        xp_t[t][:, 0:512], ALU.mult, ALU.add), inc="sdv")
                    ev_bank[ba] = ("sdv", E.tok("sdv"))
                    stt2 = E.do("vector", lambda v, pb=pb, t=t, bb=bb:
                                v.scalar_tensor_tensor(
                                    xm_sb[pb][t][:, 512:1024], BK(bb), t128a[:, :],
                                    xp_t[t][:, 512:1024], ALU.mult, ALU.add), inc="sdv")
                    ev_bank[bb] = ("sdv", stt2)
                    t128_rd = stt2
                    E.wait("gpsimd", "sdv", stt2)
                    dtok = E.dma("gpsimd", "sgd",
                                 lambda gp, g=g, t=t, pb=pb: gp.dma_start(
                                     xmid_h[g * 256 + t * P:g * 256 + (t + 1) * P, :],
                                     xm_sb[pb][t][:, :]))
                    E.do("vector", lambda v, pb=pb, t=t: v.bn_stats(
                        st6[:, 0:6], xm_sb[pb][t][:, 0:512]))
                    E.do("vector", lambda v, pb=pb, t=t: v.bn_stats(
                        st6[:, 6:12], xm_sb[pb][t][:, 512:1024]))
                    E.do("vector", lambda v, tl=tl: v.bn_aggr(
                        mvst[:, 2 * tl:2 * tl + 2], st6[:, :]))
                    E.do("vector", lambda v, pb=pb, t=t: v.tensor_reduce(
                        tmx[:, :], xm_sb[pb][t][:, :], AX.X, ALU.max))
                    E.do("vector", lambda v, pb=pb, t=t: v.tensor_reduce(
                        tmn[:, :], xm_sb[pb][t][:, :], AX.X, ALU.min))
                    E.do("vector", lambda v, tl=tl: v.tensor_sub(
                        tmx[:, :], tmx[:, :], mvst[:, 2 * tl:2 * tl + 1]))
                    E.do("vector", lambda v, tl=tl: v.tensor_sub(
                        tmn[:, :], mvst[:, 2 * tl:2 * tl + 1], tmn[:, :]))
                    sttok = E.do("vector", lambda v, tl=tl: v.tensor_tensor(
                        gmax[:, tl:tl + 1], tmx[:, :], tmn[:, :], ALU.max), inc="sdv")
                    xm_guard[pb][t] = dtok + [("sdv", sttok)]

            E.wait("scalar", "sdv")
            E.do("scalar", lambda s: s.activation(
                rln[:, :], bass.AP(mvst, 1, [[2 * NT, P], [2, NT]]), AF.Ln,
                bias=EPS_LN))
            rln_tok = E.do("scalar", lambda s: s.activation(
                rln[:, :], rln[:, :], AF.Exp, scale=-0.5), inc="sac")
            E.wait("vector", "sac", rln_tok)
            E.do("vector", lambda v: v.tensor_tensor(
                mrln[:, :], bass.AP(mvst, 0, [[2 * NT, P], [2, NT]]), rln[:, :],
                ALU.mult))
            E.do("vector", lambda v: v.tensor_mul(gmax[:, :], gmax[:, :], rln[:, :]))
            E.do("vector", lambda v: v.tensor_reduce(
                am_acc[:, :], gmax[:, :], AX.X, ALU.max), inc="sdv")

            E.wait_tok("gpsimd", E.queue_tok("sgd"))
            E.wait_tok("sync", E.queue_tok("sx"))
            E.flush_block()

        if phases < 2:
            return nc
        wqout = gstack.enter_context(nc.sbuf_tensor("wqout", [P, NF, C], BF16))
        with contextlib.ExitStack() as s2k:
            psb = lambda n, s, d: s2k.enter_context(nc.sbuf_tensor(n, s, d))
            xmb = [psb(f"xmb{i}", [P, C], F32) for i in range(2)]
            ta = psb("ta", [P, 1], F32)

            def emit_collective(cin, cout, s_pp, si_pp):
                t1 = E.dma("gpsimd", "sgd", lambda g: g.dma_start(
                    am_row[0:1, :], am_acc[:, 0:1]))
                E.wait_tok("vector", t1)
                E.do("vector", lambda v: v.tensor_reduce(
                    ccsb[:, :], bass.AP(am_row, 0, [[P, 1], [16, 8], [1, 16]]),
                    AX.X, ALU.max), inc="sdv")
                E.wait("gpsimd", "sdv")
                t2 = E.dma("gpsimd", "sgd", lambda g: g.dma_start(cin[:, :], ccsb[:, :]))
                E.wait_tok("gpsimd", t2)
                cc_tok = E.do("gpsimd", lambda g: g.collective_compute(
                    "AllReduce", ALU.max, replica_groups=[list(range(N_CORES))],
                    ins=[cin.ap()], outs=[cout.ap()]), inc="scc")
                E.wait("gpsimd", "scc", cc_tok)
                t3 = E.dma("gpsimd", "sgd", lambda g: g.dma_start(ccsb[:, :], cout[:, :]))
                E.wait_tok("vector", t3)
                E.do("vector", lambda v: v.tensor_reduce(
                    sc_a[:, :], ccsb[:, :], AX.X, ALU.max))
                E.do("vector", lambda v: v.tensor_scalar_max(sc_a[:, :], sc_a[:, :], 1.0e-8))
                E.do("vector", lambda v: v.reciprocal(sc_b[:, :], sc_a[:, :]))
                E.do("vector", lambda v: v.tensor_scalar_mul(
                    sc_b[:, :], sc_b[:, :], 32767.0))
                E.do("vector", lambda v: v.tensor_scalar_mul(
                    sc_c[:, :], sc_a[:, :], 1.0 / 32767.0), inc="sdv")
                bs1 = E.do("vector", lambda v: v.tensor_scalar(
                    am_row[0:1, :], am_row[0:1, :], 0.0, sc_b[0:1, 0:1],
                    ALU.mult, ALU.add), inc="sdv")
                E.wait("gpsimd", "sdv", bs1)
                t4 = E.dma("gpsimd", "sgd", lambda g: g.dma_start(
                    s_pp[:, 0:1], am_row[0:1, 0:P]))
                E.wait_tok("vector", t4)
                bs2 = E.do("vector", lambda v: v.tensor_scalar(
                    am_row[0:1, :], am_row[0:1, :], 0.0, sc_c[0:1, 0:1],
                    ALU.mult, ALU.add), inc="sdv")
                E.wait("gpsimd", "sdv", bs2)
                t5 = E.dma("gpsimd", "sgd", lambda g: g.dma_start(
                    si_pp[:, 0:1], am_row[0:1, 0:P]))
                E.wait_tok("gpsimd", t5)
                E.do("gpsimd", lambda g: g.memset(am_row[0:1, 0:1], 0.0), inc="sgp")
                E.flush_block()

            emit_collective(cc1_in, cc1_out, s1_pp, s1i_pp)
            if phases < 3:
                return nc

            x2qT = psb("x2qT", [P, NCH, T], BF16)
            u2 = [psb(f"u2_{i}", [P, C], BF16) for i in range(2)]
            u_sb = psb("u_sb", [P, C], BF16)
            tq = [psb(f"tq{i}", [P, C], F32) for i in range(2)]
            rs_ = psb("rs_", [P, NT], F32)
            mrs = psb("mrs", [P, NT], F32)
            ps_h = [s2k.enter_context(nc.psum_tensor(f"ps_h{i}", [P, 512], F32))
                    for i in range(2)]
            x2stack = contextlib.ExitStack()
            ps_t2 = [x2stack.enter_context(nc.psum_tensor(f"ps_t2{i}", [P, 512], BF16))
                     for i in range(2)]

            xm_tok = [None, None]
            xm_tok[0] = E.dma("sync", "sx", lambda s: s.dma_start(
                xmb[0][:, :], xmid_h[0:P, :]))
            xm_tok[1] = E.dma("sync", "sx", lambda s: s.dma_start(
                xmb[1][:, :], xmid_h[P:2 * P, :]))
            ev_t2 = [("sdv", 0), ("sac", 0)]
            u_free = [[], []]
            E.wait("vector", "sgp")
            E.do("vector", lambda v: v.tensor_scalar_mul(
                rs_[:, :], rln[:, :], s1_pp[:, 0:1]))
            rsm_tok = E.do("vector", lambda v: v.tensor_scalar(
                mrs[:, :], mrln[:, :], s1_pp[:, 0:1], -1.0, ALU.mult, ALU.mult),
                inc="sdv")
            tq_free = [0, 0]
            for tl in range(NT):
                b = tl % 2
                E.wait_tok("scalar", xm_tok[b])
                E.wait("scalar", "sdv", rsm_tok)
                E.wait("scalar", "sdv", tq_free[b])
                tq_tok = E.do("scalar", lambda s, b=b, tl=tl: s.activation(
                    tq[b][:, :], xmb[b][:, :], AF.Identity, scale=rs_[:, tl:tl + 1],
                    bias=mrs[:, tl:tl + 1]), inc="sac")
                for sem, tk in u_free[b]:
                    E.wait("vector", sem, tk)
                E.wait("vector", "sac", tq_tok)
                ld_tok = E.do("vector", lambda v, b=b: v.tensor_scalar(
                    u2[b][:, :], tq[b][:, :], MR, MR, ALU.add, ALU.subtract),
                    inc="sdv")
                tq_free[b] = ld_tok
                if tl + 2 < NT:
                    E.wait("sync", "sdv", ld_tok)
                    xm_tok[b] = E.dma("sync", "sx", lambda s, tl=tl, b=b: s.dma_start(
                        xmb[b][:, :], xmid_h[(tl + 2) * P:(tl + 3) * P, :]))
                evs = []
                for half in range(2):
                    bank = half
                    E.wait("tensor", "sdv", ld_tok)
                    E.wait("tensor", ev_t2[bank][0], ev_t2[bank][1])
                    for i in range(3):
                        cc = half * 4 + i
                        E.do("tensor", lambda tt, bank=bank, i=i, cc=cc, b=b:
                             tt.transpose(
                                 ps_t2[bank][:, i * P:(i + 1) * P],
                                 u2[b][:, cc * P:(cc + 1) * P], idb[:, :]))
                    tp = E.do("tensor", lambda tt, bank=bank, half=half, b=b:
                              tt.transpose(
                                  ps_t2[bank][:, 3 * P:4 * P],
                                  u2[b][:, (half * 4 + 3) * P:(half * 4 + 4) * P],
                                  idb[:, :]), inc="smm")
                    eng = "vector" if half == 0 else "scalar"
                    sem = "sdv" if half == 0 else "sac"
                    E.wait(eng, "smm", tp)

                    def _evac(e, half=half, tl=tl, bank=bank):
                        dst = bass.AP(x2qT, (half * 4) * T + tl * P,
                                      [[NCH * T, P], [T, 4], [1, P]])
                        if hasattr(e, "tensor_copy"):
                            return e.tensor_copy(dst, ps_t2[bank][:, :])
                        return e.copy(dst, ps_t2[bank][:, :])
                    ev = E.do(eng, _evac, inc=sem)
                    ev_t2[bank] = (sem, ev)
                    evs.append((sem, ev))
                u_free[b] = evs
            E.wait_tok("sync", E.queue_tok("sx"))
            E.flush_block()
            x2stack.close()

            ps_w = [s2k.enter_context(nc.psum_tensor(f"ps_w{i}", [P, 512], F32))
                    for i in range(4)]
            winst = [psb(f"winst{i}", [P, NCH, P], F32) for i in range(2)]
            wst = [psb(f"wst{i}", [P, C], F32) for i in range(2)]
            wtq2 = psb("wtq2", [P, C], F32)
            aw2 = psb("aw2", [P, 1], F32)
            swo = psb("swoc", [P, 1], F32)
            swoi = psb("swoic", [P, 1], F32)
            wtq = psb("wtq", [P, C], F32)
            wq_fc = [psb(f"wqfc{i}", [P, NCH, P], BF16) for i in range(2)]
            h_sb = [psb(f"hsb{i}", [P, 512], BF16) for i in range(3)]
            aw = psb("aw", [P, 1], F32)
            swv = psb("swv", [P, 1], F32)
            schf = psb("schf", [P, 1], F32)
            ps_wb = s2k.enter_context(nc.psum_tensor("ps_wb", [P, 512], BF16))

            E.do("vector", lambda v: v.memset(am_acc[:, :], 0.0))

            def win_slice_ap(fc):
                return bass.AP(win_h, fc * P, [[FF, P], [P * FF, NCH], [1, P]])

            win_tok = [None, None]
            wst_tok = [None, None]
            wo_used = [0, 0]
            wo_cast = 0
            win_rd = [0, 0]
            wt_tp = [0, 0]
            wq_rd = [0, 0]
            ev_w = [("sdv", 0)] * 2
            ev_wb = ("sdv", 0)
            cast_prev = 0
            gelu_prev = 0
            ev_h = [0, 0]
            h_dma = [[], [], []]

            def emit_wt_trans(fc):
                b = fc % 2
                toks = []
                for half in range(2):
                    bank = b * 2 + half
                    E.wait_tok("tensor", win_tok[b])
                    E.wait("tensor", ev_w[b][0], ev_w[b][1])
                    for i in range(3):
                        cc = half * 4 + i
                        E.do("tensor", lambda tt, bank=bank, i=i, cc=cc, b=b:
                             tt.transpose(ps_w[bank][:, i * P:(i + 1) * P],
                                          winst[b][:, cc, :], idf[:, :]))
                    tp = E.do("tensor", lambda tt, bank=bank, b=b, half=half:
                              tt.transpose(ps_w[bank][:, 3 * P:4 * P],
                                           winst[b][:, half * 4 + 3, :], idf[:, :]),
                              inc="smm")
                    toks.append(tp)
                    if half == 1:
                        win_rd[b] = tp
                wt_tp[b] = toks[1]

            win_tok[0] = E.dma("sync", "sx", lambda s: s.dma_start(
                winst[0][:, :, :], win_slice_ap(0)))
            win_tok[1] = E.dma("sync", "sx", lambda s: s.dma_start(
                winst[1][:, :, :], win_slice_ap(1)))
            wst_tok[0] = E.dma("sync", "sx", lambda s: s.dma_start(
                wst[0][:, :], wout_h[0:P, :]))
            emit_wt_trans(0)

            def emit_wout_quant(fc):
                nonlocal wo_cast
                b = fc % 2
                nb = (fc + 1) % 2
                if fc + 1 < NF:
                    E.wait("sync", "sdv", wo_used[nb])
                    wst_tok[nb] = E.dma("sync", "sx", lambda s, fc=fc, nb=nb:
                                        s.dma_start(wst[nb][:, :],
                                                    wout_h[(fc + 1) * P:
                                                           (fc + 2) * P, :]))
                E.wait_tok("vector", wst_tok[b])
                E.wait("vector", "sac", wo_cast)
                E.do("vector", lambda v, b=b: v.tensor_reduce(
                    aw2[:, :], wst[b][:, :], AX.X, ALU.max, apply_absolute_value=True))
                E.do("vector", lambda v: v.tensor_scalar_max(aw2[:, :], aw2[:, :],
                                                             1.0e-8))
                E.do("vector", lambda v: v.reciprocal(swo[:, :], aw2[:, :]))
                E.do("vector", lambda v: v.tensor_scalar_mul(swo[:, :], swo[:, :],
                                                             127.0))
                E.do("vector", lambda v: v.tensor_scalar_mul(
                    swoi[:, :], aw2[:, :], 1.0 / 127.0), drain=False)
                E.do("vector", lambda v, b=b: v.tensor_scalar(
                    wtq2[:, :], wst[b][:, :], swo[:, 0:1], None, ALU.mult))
                wo_used[b] = E.do("vector", lambda v: v.tensor_scalar(
                    wtq2[:, :], wtq2[:, :], MR, MR, ALU.add, ALU.subtract), inc="sdv")
                E.wait("scalar", "sdv", wo_used[b])
                wo_cast = E.do("scalar", lambda s, fc=fc: s.activation(
                    wqout[:, fc, :], wtq2[:, :], AF.Copy, scale=swoi[:, 0:1]),
                    inc="sac")

            for fc in range(NF):
                b = fc % 2
                if fc + 2 < NF:
                    nb = fc % 2
                    E.wait("sync", "smm", win_rd[nb])
                    win_tok[nb] = E.dma("sync", "sx", lambda s, fc=fc, nb=nb:
                                        s.dma_start(winst[nb][:, :, :],
                                                    win_slice_ap(fc + 2)))
                if fc + 1 < NF:
                    emit_wt_trans(fc + 1)
                E.wait("vector", "smm", wt_tp[b])
                E.wait("vector", "sac", gelu_prev)
                E.do("vector", lambda v, b=b: v.tensor_reduce(
                    aw[:, :], ps_w[2 * b][:, :], AX.X, ALU.max,
                    apply_absolute_value=True), drain=False)
                E.do("vector", lambda v, b=b: v.tensor_reduce(
                    swv[:, :], ps_w[2 * b + 1][:, :], AX.X, ALU.max,
                    apply_absolute_value=True), drain=False)
                E.do("vector", lambda v: v.tensor_max(aw[:, :], aw[:, :], swv[:, :]))
                E.do("vector", lambda v: v.tensor_scalar_max(aw[:, :], aw[:, :], 1.0e-8))
                E.do("vector", lambda v: v.reciprocal(swv[:, :], aw[:, :]))
                E.do("vector", lambda v: v.tensor_scalar_mul(swv[:, :], swv[:, :], 127.0))
                E.do("vector", lambda v: v.tensor_scalar_mul(
                    schf[:, :], aw[:, :], 1.0 / 127.0), drain=False)
                E.wait("vector", "sgp")
                sch_tok = E.do("vector", lambda v: v.tensor_mul(
                    schf[:, :], schf[:, :], s1i_pp[:, :]), inc="sdv")
                E.wait("vector", "sac", cast_prev)
                E.do("vector", lambda v, b=b: v.tensor_scalar(
                    wtq[:, 0:512], ps_w[2 * b][:, :], swv[:, 0:1], None, ALU.mult))
                E.do("vector", lambda v, b=b: v.tensor_scalar(
                    wtq[:, 512:1024], ps_w[2 * b + 1][:, :], swv[:, 0:1], None,
                    ALU.mult), drain=False)
                uq_tok = E.do("vector", lambda v: v.tensor_scalar(
                    wtq[:, :], wtq[:, :], MR, MR, ALU.add, ALU.subtract), inc="sdv")
                ev_w[b] = ("sdv", uq_tok)
                E.wait("scalar", "sdv", uq_tok)
                E.wait("scalar", ev_wb[0], ev_wb[1])
                cast_prev = E.do("scalar", lambda s: s.copy(u_sb[:, :], wtq[:, :]),
                                 inc="sac")
                for half in range(2):
                    E.wait("tensor", "sac", cast_prev)
                    E.wait("tensor", ev_wb[0], ev_wb[1])
                    E.wait("tensor", "smm", wq_rd[b])
                    for i in range(3):
                        cc = half * 4 + i
                        E.do("tensor", lambda tt, i=i, cc=cc: tt.transpose(
                            ps_wb[:, i * P:(i + 1) * P], u_sb[:, cc * P:(cc + 1) * P],
                            idb[:, :]))
                    tp = E.do("tensor", lambda tt, half=half: tt.transpose(
                        ps_wb[:, 3 * P:4 * P],
                        u_sb[:, (half * 4 + 3) * P:(half * 4 + 4) * P], idb[:, :]),
                        inc="smm")
                    E.wait("vector", "smm", tp)
                    ev = E.do("vector", lambda v, half=half, b=b: v.tensor_copy(
                        bass.AP(wq_fc[b], (half * 4) * P, [[NCH * P, P], [P, 4], [1, P]]),
                        ps_wb[:, :]), inc="sdv")
                    ev_wb = ("sdv", ev)
                wq_ready = E.tok("sdv")

                for tb_ in range(8):
                    hb = (fc * 8 + tb_) % 3
                    bank = tb_ % 2
                    E.wait("tensor", "sdv", wq_ready)
                    E.wait("tensor", "sac", ev_h[bank])
                    mm_tok = None
                    for cc in range(NCH):
                        kw = dict(start=(cc == 0), stop=(cc == NCH - 1))
                        lam = (lambda tt, b=b, cc=cc, tb_=tb_, bank=bank, kw=kw:
                               tt.matmul(ps_h[bank][:, :], wq_fc[b][:, cc, :],
                                         x2qT[:, cc, tb_ * 512:(tb_ + 1) * 512], **kw))
                        if cc == NCH - 1:
                            mm_tok = E.do("tensor", lam, inc="smm")
                        else:
                            E.do("tensor", lam)
                    if tb_ == 7:
                        wq_rd[b] = mm_tok
                    E.wait("scalar", "smm", mm_tok)
                    E.wait("scalar", "sdv", sch_tok)
                    E.wait_tok("scalar", h_dma[hb])
                    ge_tok = E.do("scalar", lambda s, hb=hb, fc=fc, bank=bank: s.activation(
                        h_sb[hb][:, :], ps_h[bank][:, :], AF.Gelu,
                        bias=binp[:, fc:fc + 1], scale=schf[:, 0:1]), inc="sac")
                    ev_h[bank] = ge_tok
                    gelu_prev = ge_tok
                    E.wait("vector", "sac", ge_tok)
                    E.do("vector", lambda v, hb=hb: v.tensor_reduce(
                        ta[:, :], h_sb[hb][:, :], AX.X, ALU.max,
                        apply_absolute_value=True))
                    am_tok = E.do("vector", lambda v: v.tensor_max(
                        am_acc[:, :], am_acc[:, :], ta[:, :]), inc="sdv")
                    E.wait("gpsimd", "sdv", am_tok)
                    h_dma[hb] = E.dma("gpsimd", "sgd",
                                      lambda g, fc=fc, tb_=tb_, hb=hb: g.dma_start(
                                          bass.AP(hscr_h,
                                                  2 * tb_ * P * NF * 256 + fc * 256,
                                                  [[NF * 256, P], [P * NF * 256, 2],
                                                   [1, 256]]),
                                          bass.AP(h_sb[hb], 0,
                                                  [[512, P], [256, 2], [1, 256]])))
                emit_wout_quant(fc)

            E.wait_tok("gpsimd", E.queue_tok("sgd"))
            E.wait_tok("sync", E.queue_tok("sx"))
            E.flush_block()

            emit_collective(cc2_in, cc2_out, s2_pp, s2i_pp)

        if phases < 4:
            return nc
        with contextlib.ExitStack() as s3k:
            psb = lambda n, s, d: s3k.enter_context(nc.sbuf_tensor(n, s, d))
            hin = [psb(f"hin{i}", [P, NF, 256], BF16) for i in range(2)]
            hq = [psb(f"hq{i}", [P, NF, 256], BF16) for i in range(2)]
            thq = psb("thq", [P, NF * 256], F32)
            xmb2 = [psb(f"xmb2{i}", [P, C], F32) for i in range(2)]
            ffn = psb("ffn", [P, C], F32)
            outb = [psb(f"outb{i}", [P, C], F32) for i in range(2)]
            st6 = psb("st6c", [P, 12], F32)
            mv2 = psb("mv2c", [P, 2], F32)
            ta = psb("tac", [P, 1], F32)
            tb = psb("tbc", [P, 1], F32)
            ps_f = [s3k.enter_context(nc.psum_tensor(f"ps_f{i}", [P, C], F32))
                    for i in range(2)]

            wq_done = 0
            NB = 16
            hin_tok = [None, None]
            hin_tok[0] = E.dma("sync", "sx", lambda s: s.dma_start(
                bass.AP(hin[0], 0, [[NF * 256, P], [1, NF * 256]]),
                hscr_h[0, :, :]))
            xm2_tok = [None, None]
            xm2_tok[0] = E.dma("sync", "sx", lambda s: s.dma_start(
                xmb2[0][:, :], xmid_h[0:P, :]))
            hin_used = [0, 0]
            hq_rd = [0, 0]
            tq_prev = 0
            ev_f = [0, 0]
            out_dma = [[], []]
            xm_used = [0, 0]

            def emit_quant(blk):
                nonlocal tq_prev
                b = blk % 2
                E.wait_tok("scalar", hin_tok[b])
                E.wait("scalar", "sgp")
                E.wait("scalar", "sdv", tq_prev)
                hin_used[b] = E.do("scalar", lambda s, b=b: s.activation(
                    thq[:, :], bass.AP(hin[b], 0, [[NF * 256, P], [1, NF * 256]]),
                    AF.Copy, scale=s2_pp[:, 0:1]), inc="sac")
                E.wait("vector", "sac", hin_used[b])
                E.wait("vector", "smm", hq_rd[b])
                tq_prev = E.do("vector", lambda v, b=b: v.tensor_scalar(
                    bass.AP(hq[b], 0, [[NF * 256, P], [1, NF * 256]]),
                    thq[:, :], MR, MR, ALU.add, ALU.subtract), inc="sdv")
                return tq_prev

            hq_toks = [0, 0]
            hq_toks[0] = emit_quant(0)
            for blk in range(NB):
                b = blk % 2
                nb = (blk + 1) % 2
                if blk + 1 < NB:
                    E.wait("sync", "sac", hin_used[nb])
                    hin_tok[nb] = E.dma("sync", "sx",
                                        lambda s, blk=blk, nb=nb: s.dma_start(
                                            bass.AP(hin[nb], 0,
                                                    [[NF * 256, P], [1, NF * 256]]),
                                            hscr_h[blk + 1, :, :]))
                    E.wait("sync", "sdv", xm_used[nb])
                    xm2_tok[nb] = E.dma("sync", "sx", lambda s, blk=blk, nb=nb: s.dma_start(
                        xmb2[nb][:, :],
                        xmid_h[(blk + 1) * 2 * P:((blk + 1) * 2 + 1) * P, :]))
                    hq_toks[nb] = emit_quant(blk + 1)
                hq_tok = hq_toks[b]

                for sub in range(2):
                    tl = blk * 2 + sub
                    bank = sub
                    E.wait("tensor", "sdv", hq_tok)
                    E.wait("tensor", "sdv", ev_f[bank])
                    mm_tok = None
                    for fc in range(NF):
                        for pc in range(2):
                            kw = dict(start=(fc == 0), stop=(fc == NF - 1))
                            lam = (lambda tt, fc=fc, sub=sub, pc=pc, bank=bank, kw=kw,
                                   b=b: tt.matmul(
                                       ps_f[bank][:, pc * 512:(pc + 1) * 512],
                                       hq[b][:, fc, sub * P:(sub + 1) * P],
                                       wqout[:, fc, pc * 512:(pc + 1) * 512], **kw))
                            if fc == NF - 1 and pc == 1:
                                mm_tok = E.do("tensor", lam, inc="smm")
                            else:
                                E.do("tensor", lam)
                    if sub == 1:
                        hq_rd[b] = mm_tok
                    E.wait("vector", "smm", mm_tok)
                    ev_f[bank] = E.do("vector", lambda v, bank=bank: v.scalar_tensor_tensor(
                        ffn[:, :], ps_f[bank][:, :], s2i_pp[:, 0:1], bout_t[:, :],
                        ALU.mult, ALU.add), inc="sdv")
                    E.do("vector", lambda v: v.bn_stats(st6[:, 0:6], ffn[:, 0:512]))
                    E.do("vector", lambda v: v.bn_stats(st6[:, 6:12], ffn[:, 512:1024]))
                    E.do("vector", lambda v: v.bn_aggr(mv2[:, :], st6[:, :]), inc="sdv")
                    E.wait("scalar", "sdv")
                    E.do("scalar", lambda s: s.activation(
                        ta[:, :], mv2[:, 1:2], AF.Ln, bias=EPS_LN))
                    E.do("scalar", lambda s: s.activation(
                        ta[:, :], ta[:, :], AF.Exp, scale=-0.5), inc="sac")
                    E.wait("vector", "sac")
                    E.do("vector", lambda v: v.tensor_mul(tb[:, :], mv2[:, 0:1], ta[:, :]))
                    E.wait_tok("vector", xm2_tok[b])
                    E.wait_tok("vector", out_dma[b])
                    E.do("vector", lambda v, b=b: v.tensor_scalar(
                        outb[b][:, :], xmb2[b][:, :], 1.0, tb[:, :],
                        ALU.mult, ALU.subtract))
                    out_tok = E.do("vector", lambda v, b=b: v.scalar_tensor_tensor(
                        outb[b][:, :], ffn[:, :], ta[:, 0:1], outb[b][:, :],
                        ALU.mult, ALU.add), inc="sdv")
                    if sub == 1:
                        xm_used[b] = out_tok
                    E.wait("gpsimd", "sdv", out_tok)
                    out_dma[b] = E.dma("gpsimd", "sgd", lambda g, tl=tl, b=b: g.dma_start(
                        out_h[tl * P:(tl + 1) * P, :], outb[b][:, :]))
                    if sub == 0:
                        E.wait("sync", "sdv", out_tok)
                        xm2_tok[b] = E.dma("sync", "sx", lambda s, blk=blk, b=b:
                                           s.dma_start(
                                               xmb2[b][:, :],
                                               xmid_h[(blk * 2 + 1) * P:
                                                      (blk * 2 + 2) * P, :]))

            E.wait_tok("sync", E.queue_tok("sx"))
            E.wait_tok("gpsimd", E.queue_tok("sgd"))
            E.flush_block()

    return nc


_NC_CACHE = {}


def kernel(L0_attn_x, L0_attn_w_qkv, L0_attn_ln_w, L0_attn_w_proj,
           L0_ffn_W_in, L0_ffn_b_in, L0_ffn_W_out, L0_ffn_b_out):
    x = np.ascontiguousarray(np.asarray(L0_attn_x))
    wqkv = np.ascontiguousarray(np.asarray(L0_attn_w_qkv))
    lnw = np.ascontiguousarray(np.asarray(L0_attn_ln_w)).reshape(1, C)
    wproj = np.ascontiguousarray(np.asarray(L0_attn_w_proj))
    win = np.ascontiguousarray(np.asarray(L0_ffn_W_in))
    bin_ = np.ascontiguousarray(np.asarray(L0_ffn_b_in)).reshape(1, FF)
    wout = np.ascontiguousarray(np.asarray(L0_ffn_W_out))
    bout = np.ascontiguousarray(np.asarray(L0_ffn_b_out)).reshape(1, C)

    if "nc" not in _NC_CACHE:
        _NC_CACHE["nc"] = build_nc()
    nc = _NC_CACHE["nc"]

    B = x.shape[0]
    per = B // N_CORES
    in_maps = []
    for c in range(N_CORES):
        shard = np.ascontiguousarray(x[c * per:(c + 1) * per].reshape(T, C))
        in_maps.append({
            "x": shard, "wqkv": wqkv, "lnw": lnw, "wproj": wproj,
            "win": win, "bin": bin_, "wout": wout, "bout": bout,
        })
    res = run_bass_kernel_spmd(nc, in_maps, core_ids=list(range(N_CORES)))
    out = np.concatenate(
        [np.asarray(r["out"]).reshape(per, 2048, C) for r in res.results], axis=0)
    return out
